# version 02
# baseline (speedup 1.0000x reference)
import math
import numpy as np
import ml_dtypes
import concourse.bass as bass
import concourse.mybir as mybir
from concourse.bass_utils import run_bass_kernel_spmd
from contextlib import ExitStack

F32 = mybir.dt.float32
BF16 = mybir.dt.bfloat16
AF = mybir.ActivationFunctionType
ALU = mybir.AluOpType
AX = mybir.AxisListType

NB, T, C = 4, 4096, 1024
H, HS, DHR = 16, 64, 32
E = 8
SPLIT = 2560
NTMAX = SPLIT
RS = 1.0 / math.sqrt(HS + DHR)

DBG_PHASE = None
N_DSEM = 40
N_GSEM = 8


class Sem:
    def __init__(self, handle):
        self.h = handle
        self.v = 0


class Em:
    ENGS = ("t", "v", "s", "g", "y")

    def __init__(self, sems, dpool, gpool):
        self.q = {k: [] for k in self.ENGS}
        self.sem = sems
        self.dsems = [Sem(h) for h in dpool]
        self.gsems = [Sem(h) for h in gpool]
        self.dnext = 0
        self.gnext = 0
        self.stopped = False

    def dsem(self, qk="y"):
        if qk == "g":
            sm = self.gsems[self.gnext % len(self.gsems)]
            self.gnext += 1
            return sm
        sm = self.dsems[self.dnext % len(self.dsems)]
        self.dnext += 1
        return sm

    def w(self, e, sem, val=None):
        if self.stopped:
            return
        v = sem.v if val is None else val
        if v <= 0:
            return
        self.q[e].append(lambda eng, s=sem.h, v=v: eng.wait_ge(s, v))

    def op(self, e, fn, inc=None, n=1):
        if self.stopped:
            return None
        if inc is None:
            self.q[e].append(lambda eng, fn=fn: fn(eng))
            return None
        inc.v += n
        self.q[e].append(lambda eng, fn=fn, s=inc.h, n=n: fn(eng).then_inc(s, n))
        return inc.v

    def drain(self, e, inc=None, n=1):
        if self.stopped:
            return None
        if inc is None:
            self.q[e].append(lambda eng: eng.drain())
            return None
        inc.v += n
        self.q[e].append(lambda eng, s=inc.h, n=n: eng.drain().then_inc(s, n))
        return inc.v

    def dma(self, e, dst, src, sem):
        if self.stopped:
            return None
        sem.v += 16
        self.q[e].append(
            lambda eng, dst=dst, src=src, s=sem.h: eng.dma_start(dst, src).then_inc(s, 16))
        return sem.v

    def barrier(self):
        if self.stopped:
            return
        S = self.sem
        self.drain("t", inc=S["PE"])
        self.drain("v", inc=S["DV"])
        self.drain("s", inc=S["AC"])
        self.drain("g", inc=S["GP"])
        self.drain("y", inc=S["SY"])
        toks = [(S[k], S[k].v) for k in ("PE", "DV", "AC", "GP", "SY")]
        for e in self.ENGS:
            for sem, v in toks:
                self.w(e, sem, v)

    def stop_with_dump(self, d, taps):
        self.barrier()
        sm = self.dsem("g")
        col = 0
        for ap, rows, ncols in taps:
            self.dma("g", d["odbg"][0:rows, col:col + ncols], ap, sm)
            col += ncols
        self.w("g", sm, sm.v)
        self.stopped = True


class Rot:

    def __init__(self, n, em=None, qk="y"):
        self.n = n
        self.rel = [[] for _ in range(n)]
        self.i = 0
        self.sems = [em.dsem(qk) for _ in range(n)] if em is not None else None

    def acquire(self, em, engine):
        sl = self.i % self.n
        for sem, v in self.rel[sl]:
            em.w(engine, sem, v)
        self.rel[sl] = []
        self.i += 1
        return sl

    def release(self, *tokens):
        sl = (self.i - 1) % self.n
        self.rel[sl].extend(tokens)

    def release_slot(self, sl, *tokens):
        self.rel[sl].extend(tokens)

    def finish(self, em, engine):
        if self.sems:
            for sm in self.sems:
                em.w(engine, sm, sm.v)


PA0, PA1 = 0, 512
PB0, PB1 = 1024, 2048
PM = 3072

ROLE = [dict(NKT=20, NTT=20, Q0=0), dict(NKT=32, NTT=12, Q0=20)]


def preload(em, d, s):
    S = em.sem
    em.op("v", lambda v: v.memset(s["ones1f"][:, :], 1.0))
    em.op("v", lambda v: v.memset(s["ones512f"][:, :], 1.0))
    em.op("v", lambda v: v.memset(s["onescol"][:, :], 1.0))
    em.op("v", lambda v: v.memset(s["epsc"][:, :], 1e-5))
    em.drain("v", inc=S["DV"])
    sm = em.dsem("g")
    em.dma("g", s["cidf"][:, :], d["cidf"][:, :], sm)
    em.dma("g", s["cidb"][:, :], d["cidb"][:, :], sm)
    em.dma("g", s["dmask"][:, :], d["dmask"][:, :], sm)
    em.dma("g", s["bg"][:, :], d["bg"][:, :], sm)
    em.dma("g", s["b1c"][:, :], d["b1c"][:, :], sm)
    em.dma("g", s["b2b"][:, :], d["b2"][:, :], sm)
    em.dma("g", s["cunit"][:, :], d["cunit"][:, :], sm)
    for xc in range(8):
        em.dma("g", s["wgc"][:, xc * 8:(xc + 1) * 8],
               d["wg"][xc * 128:(xc + 1) * 128, :], sm)
    em.w("g", sm, sm.v)
    em.barrier()


def ln_phase(em, d, s, p, ntiles, src_fn, dst_dram):
    S = em.sem
    st = s["stat"]
    pbf = p["bf"]

    em.op("v", lambda v: v.memset(st[:, :], 0.0))
    em.drain("v", inc=S["DV"])
    dv_zero = S["DV"].v

    xt_rot = Rot(4, em)
    sq_rot = Rot(2)
    for i in range(ntiles):
        sl = xt_rot.acquire(em, "y")
        xs = s["xt"][:, sl * C:(sl + 1) * C]
        em.dma("y", xs, src_fn(i), xt_rot.sems[sl])
        lv = xt_rot.sems[sl].v
        em.w("s", xt_rot.sems[sl], lv)
        em.w("s", S["DV"], dv_zero)
        qsl = sq_rot.acquire(em, "s")
        em.op("s", lambda sc, xs=xs, i=i, qsl=qsl: sc.activation(
            s["sqs"][:, qsl * C:(qsl + 1) * C], xs, AF.Square,
            accum_out=st[:, 32 + i:33 + i]), inc=S["AC"])
        ac = S["AC"].v
        sq_rot.release((S["AC"], ac))
        em.w("v", xt_rot.sems[sl], lv)
        em.op("v", lambda v, xs=xs, i=i: v.reduce_sum(st[:, i:i + 1], xs, axis=AX.X),
              inc=S["DV"])
        xt_rot.release((S["AC"], ac), (S["DV"], S["DV"].v))
    em.drain("s", inc=S["AC"])
    em.drain("v", inc=S["DV"])
    ac_stats = S["AC"].v

    n = ntiles
    em.w("v", S["AC"], ac_stats)
    em.op("v", lambda v: v.tensor_scalar(st[:, 64:64 + n], st[:, 0:n], 1.0 / C, None, ALU.mult))
    em.drain("v")
    em.op("v", lambda v: v.tensor_scalar(st[:, 96:96 + n], st[:, 32:32 + n], 1.0 / C, None, ALU.mult))
    em.drain("v")
    em.op("v", lambda v: v.tensor_mul(st[:, 160:160 + n], st[:, 64:64 + n], st[:, 64:64 + n]))
    em.drain("v")
    em.op("v", lambda v: v.tensor_sub(st[:, 96:96 + n], st[:, 96:96 + n], st[:, 160:160 + n]))
    em.drain("v", inc=S["DV"])
    em.w("s", S["DV"], S["DV"].v)
    em.op("s", lambda sc: sc.activation(st[:, 160:160 + n], st[:, 96:96 + n], AF.Ln,
                                        bias=s["epsc"][:, :]))
    em.drain("s")
    em.op("s", lambda sc: sc.activation(st[:, 128:128 + n], st[:, 160:160 + n], AF.Exp,
                                        scale=-0.5))
    em.drain("s", inc=S["AC"])
    ac_rstd = S["AC"].v
    em.w("v", S["AC"], ac_rstd)

    xnt_rot = Rot(2)
    pbf_rot = Rot(1)
    fm_rot = Rot(2, em)
    fsl = 0
    for i in range(ntiles):
        sl = xt_rot.acquire(em, "y")
        xs = s["xt"][:, sl * C:(sl + 1) * C]
        em.dma("y", xs, src_fn(i), xt_rot.sems[sl])
        lv = xt_rot.sems[sl].v
        nsl = xnt_rot.acquire(em, "v")
        xnt = s["xnt"][:, nsl * C:(nsl + 1) * C]
        em.w("v", xt_rot.sems[sl], lv)
        em.op("v", lambda v, xs=xs, xnt=xnt, i=i: v.tensor_scalar(
            xnt, xs, st[:, 64 + i:65 + i], st[:, 128 + i:129 + i], ALU.subtract, ALU.mult))
        em.drain("v", inc=S["DV"])
        dv_xnt = S["DV"].v
        xt_rot.release((S["DV"], dv_xnt))

        pbf_rot.acquire(em, "t")
        em.w("t", S["DV"], dv_xnt)
        for xc in range(8):
            em.op("t", lambda t, xnt=xnt, xc=xc: t.transpose(
                pbf[:, xc * 128:(xc + 1) * 128], xnt[:, xc * 128:(xc + 1) * 128],
                s["cidb"][:, :]))
        em.drain("t", inc=S["PE"])
        pe_tr = S["PE"].v
        xnt_rot.release((S["PE"], pe_tr))
        em.w("v", S["PE"], pe_tr)
        if i % 4 == 0:
            fsl = fm_rot.acquire(em, "v")
        for xc in range(8):
            em.op("v", lambda v, xc=xc, i=i, fsl=fsl: v.tensor_copy(
                s["fmst"][:, fsl * 4096 + xc * 512 + (i % 4) * 128:
                          fsl * 4096 + xc * 512 + (i % 4) * 128 + 128],
                pbf[:, xc * 128:(xc + 1) * 128]))
        em.drain("v", inc=S["DV"])
        pbf_rot.release((S["DV"], S["DV"].v))
        if i % 4 == 3 or i == ntiles - 1:
            g0 = (i // 4) * 4
            ncb = (i - g0 + 1) * 128
            em.w("y", S["DV"], S["DV"].v)
            for xc in range(8):
                em.dma("y", dst_dram[xc, :, g0 * 128: g0 * 128 + ncb],
                       s["fmst"][:, fsl * 4096 + xc * 512: fsl * 4096 + xc * 512 + ncb],
                       fm_rot.sems[fsl])
            fm_rot.release((fm_rot.sems[fsl], fm_rot.sems[fsl].v))
    xt_rot.finish(em, "y")
    fm_rot.finish(em, "y")


def gate_phase(em, d, s, p, ntiles, xn_dram):
    S = em.sem
    pbig = p["big"]
    gld_rot = Rot(2, em)
    sc_rot = Rot(2)
    tr_rot = Rot(2)
    for i in range(ntiles):
        lsl = gld_rot.acquire(em, "y")
        gl = s["gld"][:, lsl * 1024:(lsl + 1) * 1024]
        for xc in range(8):
            em.dma("y", gl[:, xc * 128:(xc + 1) * 128],
                   xn_dram[xc, :, i * 128:(i + 1) * 128], gld_rot.sems[lsl])
        lv = gld_rot.sems[lsl].v
        ssl = sc_rot.acquire(em, "t")
        ps = pbig[:, (PA0, PA1)[ssl]: (PA0, PA1)[ssl] + 8]
        em.w("t", gld_rot.sems[lsl], lv)
        for xc in range(8):
            em.op("t", lambda t, ps=ps, gl=gl, xc=xc: t.matmul(
                ps, gl[:, xc * 128:(xc + 1) * 128],
                s["wgc"][:, xc * 8:(xc + 1) * 8], start=(xc == 0), stop=False))
        em.op("t", lambda t, ps=ps: t.matmul(
            ps, s["ones1f"][:, :], s["bg"][:, :], start=False, stop=True), inc=S["PE"])
        pe_sc = S["PE"].v
        gld_rot.release((S["PE"], pe_sc))
        em.w("v", S["PE"], pe_sc)
        em.op("v", lambda v, ps=ps: v.tensor_copy(s["gsc"][:, :], ps))
        em.drain("v")
        em.op("v", lambda v: v.max(out=s["gmx"][:, :], in_=s["gsc"][:, :]))
        em.drain("v")
        em.op("v", lambda v: v.tensor_scalar(
            s["gge"][:, :], s["gsc"][:, :], s["gmx"][:, 1:2], None, ALU.is_ge))
        em.drain("v")
        em.op("v", lambda v: v.tensor_mul(s["gwv"][:, :], s["gsc"][:, :], s["gge"][:, :]))
        em.drain("v", inc=S["DV"])
        dv_w = S["DV"].v
        sc_rot.release((S["DV"], dv_w))

        tsl = tr_rot.acquire(em, "t")
        em.w("t", S["DV"], dv_w)
        p8 = pbig[0:8, (PB0, PB1)[tsl]: (PB0, PB1)[tsl] + 128]
        em.op("t", lambda t, p8=p8: t.matmul(
            p8, s["gwv"][:, :], s["cidf"][:, :], start=True, stop=True), inc=S["PE"])
        pe_tr = S["PE"].v
        em.w("v", S["PE"], pe_tr)
        em.op("v", lambda v, p8=p8, i=i: v.tensor_copy(
            s["wfm2"][0:8, i * 128:(i + 1) * 128], p8))
        em.drain("v", inc=S["DV"])
        tr_rot.release((S["DV"], S["DV"].v))
    gld_rot.finish(em, "y")


def ffn_phase(em, d, s, p, nkt, xn_dram, res_fn, dst_fn):
    S = em.sem
    pbig = p["big"]
    nch = nkt * 128 // 512

    w1_rot = Rot(2, em, "g")
    xn_rot = Rot(2, em)
    wbc_rot = Rot(2)
    h_rot = Rot(2)
    relu_rot = Rot(4)
    hst_rot = Rot(4, em)
    w2sem = [em.dsem("g"), em.dsem("g")]
    pe0 = S["PE"].v
    w1info = []
    for e in range(E):
        wsl = w1_rot.acquire(em, "g")
        for xc in range(8):
            em.dma("g", s["w1b"][:, wsl * 8 * C + xc * C: wsl * 8 * C + (xc + 1) * C],
                   d["w1"][e, xc * 128:(xc + 1) * 128, :], w1_rot.sems[wsl])
        w1info.append((wsl, w1_rot.sems[wsl].v))
        w1_rot.release((S["PE"], pe0 + (e + 1) * nch * 9))
        if e < 4:
            for hc in range(8):
                em.dma("g", s["w2b"][:, (e * 8 + hc) * C:(e * 8 + hc + 1) * C],
                       d["w2"][e, hc * 128:(hc + 1) * 128, :], w2sem[0])

    def ld_xn(tc):
        xsl_ = xn_rot.acquire(em, "y")
        for xc in range(8):
            em.dma("y", s["xnld"][:, xsl_ * 4096 + xc * 512: xsl_ * 4096 + (xc + 1) * 512],
                   xn_dram[xc, :, tc * 512: tc * 512 + 512], xn_rot.sems[xsl_])
        return xsl_, xn_rot.sems[xsl_].v

    pend = ld_xn(0)
    for idx in range(E * nch):
        e, tc = divmod(idx, nch)
        xsl, la_x = pend
        if idx + 1 < E * nch:
            pend = ld_xn((idx + 1) % nch)
        wsl, lw_e = w1info[e]
        if tc == 0:
            em.w("t", w1_rot.sems[wsl], lw_e)
        if True:
            bsl = wbc_rot.acquire(em, "t")
            pw = pbig[:, PB0 + bsl * 512: PB0 + bsl * 512 + 512]
            em.op("t", lambda t, pw=pw, e=e, tc=tc: t.matmul(
                pw, s["cunit"][0:8, e * 128:(e + 1) * 128],
                s["wfm2"][0:8, tc * 512: tc * 512 + 512],
                start=True, stop=True), inc=S["PE"])
            pe_wbc = S["PE"].v
            em.w("t", xn_rot.sems[xsl], la_x)
            for ht in range(8):
                hsl = h_rot.acquire(em, "t")
                ph = pbig[:, PA0 + hsl * 512: PA0 + hsl * 512 + 512]
                for xc in range(8):
                    em.op("t", lambda t, ph=ph, wsl=wsl, xsl=xsl, xc=xc, ht=ht: t.matmul(
                        ph,
                        s["w1b"][:, wsl * 8 * C + xc * C + ht * 128:
                                 wsl * 8 * C + xc * C + ht * 128 + 128],
                        s["xnld"][:, xsl * 4096 + xc * 512: xsl * 4096 + (xc + 1) * 512],
                        start=(xc == 0), stop=(xc == 7)),
                        inc=(S["PE"] if xc == 7 else None))
                pe_h = S["PE"].v
                pe_last = pe_h
                rsl = relu_rot.acquire(em, "s")
                rb = s["hrelu"][:, rsl * 512:(rsl + 1) * 512]
                em.w("s", S["PE"], pe_h)
                em.op("s", lambda sc, rb=rb, ph=ph, e=e, ht=ht: sc.activation(
                    rb, ph, AF.Relu, bias=s["b1c"][:, e * 8 + ht: e * 8 + ht + 1]),
                    inc=S["AC"])
                ac_r = S["AC"].v
                h_rot.release((S["AC"], ac_r))
                osl = hst_rot.acquire(em, "v")
                ob = s["hst"][:, osl * 512:(osl + 1) * 512]
                em.w("v", S["AC"], ac_r + 1)
                em.w("v", S["PE"], pe_wbc)
                em.op("v", lambda v, ob=ob, rb=rb, pw=pw: v.tensor_tensor(
                    ob, rb, pw, ALU.mult), inc=S["DV"])
                dv_h = S["DV"].v
                relu_rot.release((S["DV"], dv_h))
                if ht == 7:
                    wbc_rot.release((S["DV"], dv_h))
                    xn_rot.release_slot(xsl, (S["PE"], pe_h))
                em.w("y", S["DV"], dv_h + 1)
                em.dma("y", d["d_h"][e, :, ht, tc * 512: tc * 512 + 512], ob,
                       hst_rot.sems[osl])
                hst_rot.release((hst_rot.sems[osl], hst_rot.sems[osl].v))
    assert S["PE"].v == pe0 + E * nch * 9
    em.drain("s", inc=S["AC"])
    em.drain("v", inc=S["DV"])
    dv_w1_done = S["DV"].v

    for pa in range(2):
        es = list(range(pa * 4, pa * 4 + 4))
        if pa == 1:
            em.w("g", S["PE"], S["PE"].v)
            for e in es:
                for hc in range(8):
                    em.dma("g", s["w2b"][:, ((e - 4) * 8 + hc) * C:
                                         ((e - 4) * 8 + hc + 1) * C],
                           d["w2"][e, hc * 128:(hc + 1) * 128, :], w2sem[1])
        em.w("t", w2sem[pa], w2sem[pa].v)
        hld_rot = Rot(2, em)
        xt_rot = Rot(2, em)
        yp_rot = Rot(2, em)
        y_rot = Rot(2)
        o_rot = Rot(2, em)
        hst_rot.finish(em, "y")

        def ld_j(j):
            lsl_ = hld_rot.acquire(em, "y")
            for ei, e in enumerate(es):
                em.dma("y",
                       s["hld"][:, lsl_ * 4096 + ei * 1024: lsl_ * 4096 + (ei + 1) * 1024]
                       .rearrange("p (a c) -> p a c", a=8),
                       d["d_h"][e, :, :, j * 128:(j + 1) * 128],
                       hld_rot.sems[lsl_])
            if pa == 1:
                xsl_ = xt_rot.acquire(em, "y")
                xres_ = s["xt"][:, xsl_ * C:(xsl_ + 1) * C]
                em.dma("y", xres_, res_fn(j), xt_rot.sems[xsl_])
                psl_ = yp_rot.acquire(em, "y")
                ypl_ = s["ypld"][:, psl_ * C:(psl_ + 1) * C]
                em.dma("y", ypl_, d["d_yp"][j, :, :], yp_rot.sems[psl_])
                return (lsl_, hld_rot.sems[lsl_].v, xsl_, xt_rot.sems[xsl_].v,
                        xres_, psl_, yp_rot.sems[psl_].v, ypl_)
            return (lsl_, hld_rot.sems[lsl_].v, 0, 0, None, 0, 0, None)

        pend = ld_j(0)
        for j in range(nkt):
            lsl, la_h, xsl, la_x, xres, psl, la_yp, ypl = pend
            if j + 1 < nkt:
                pend = ld_j(j + 1)
            ysl = y_rot.acquire(em, "t")
            py = pbig[:, PB0 + ysl * 1024: PB0 + ysl * 1024 + 1024]
            em.w("t", hld_rot.sems[lsl], la_h)
            em.w("t", S["DV"], dv_w1_done)
            for half in range(2):
                ph = py[:, half * 512: half * 512 + 512]
                k = 0
                for ei, e in enumerate(es):
                    for hc in range(8):
                        last = (ei == 3 and hc == 7)
                        em.op("t", lambda t, ph=ph, lsl=lsl, ei=ei, hc=hc,
                              half=half, k=k, last=last, pa=pa: t.matmul(
                            ph,
                            s["hld"][:, lsl * 4096 + ei * 1024 + hc * 128:
                                     lsl * 4096 + ei * 1024 + hc * 128 + 128],
                            s["w2b"][:, (ei * 8 + hc) * C + half * 512:
                                     (ei * 8 + hc) * C + half * 512 + 512],
                            start=(k == 0), stop=(last and pa == 0)),
                            inc=(S["PE"] if (last and pa == 0 and half == 1) else None))
                        k += 1
                if pa == 1:
                    em.op("t", lambda t, ph=ph, j=j, half=half: t.matmul(
                        ph, s["wfm2"][0:8, j * 128:(j + 1) * 128],
                        s["b2b"][0:8, half * 512: half * 512 + 512],
                        start=False, stop=True),
                        inc=(S["PE"] if half == 1 else None))
            pe_y = S["PE"].v
            hld_rot.release_slot(lsl, (S["PE"], pe_y))
            if pa == 0:
                osl = o_rot.acquire(em, "v")
                yt = s["yt"][:, osl * C:(osl + 1) * C]
                em.w("v", S["PE"], pe_y)
                em.op("v", lambda v, yt=yt, py=py: v.tensor_copy(yt, py), inc=S["DV"])
                dv_y = S["DV"].v
                y_rot.release((S["DV"], dv_y))
                em.w("y", S["DV"], dv_y + 1)
                em.dma("y", d["d_yp"][j, :, :], yt, o_rot.sems[osl])
                o_rot.release((o_rot.sems[osl], o_rot.sems[osl].v))
            else:
                osl = o_rot.acquire(em, "v")
                xo = s["x1t"][:, osl * C:(osl + 1) * C]
                em.w("v", S["PE"], pe_y)
                em.w("v", xt_rot.sems[xsl], la_x)
                em.w("v", yp_rot.sems[psl], la_yp)
                em.op("v", lambda v, py=py, ypl=ypl: v.tensor_add(ypl, ypl, py))
                em.drain("v")
                em.op("v", lambda v, xo=xo, ypl=ypl, xres=xres: v.scalar_tensor_tensor(
                    xo, ypl, 0.5, xres, ALU.mult, ALU.add), inc=S["DV"])
                dv_x = S["DV"].v
                y_rot.release((S["DV"], dv_x))
                xt_rot.release_slot(xsl, (S["DV"], dv_x))
                yp_rot.release_slot(psl, (S["DV"], dv_x))
                em.w("y", S["DV"], dv_x + 1)
                em.dma("y", dst_fn(j), xo, o_rot.sems[osl])
                o_rot.release((o_rot.sems[osl], o_rot.sems[osl].v))
        em.drain("v", inc=S["DV"])
        o_rot.finish(em, "y")
        hld_rot.finish(em, "y")
        if pa == 1:
            xt_rot.finish(em, "y")
            yp_rot.finish(em, "y")
    xn_rot.finish(em, "y")
    w1_rot.finish(em, "g")
    em.w("g", w2sem[0], w2sem[0].v)
    em.w("g", w2sem[1], w2sem[1].v)


def kv_phase(em, d, s, p, nkt):
    S = em.sem
    pbig = p["big"]
    nch = nkt * 128 // 512

    wsem = em.dsem("g")
    em.dma("g", s["rope"][:, :], d["rope"][:, :], wsem)
    for xc in range(8):
        em.dma("g", s["wdkvb"][:, xc * 256:(xc + 1) * 256],
               d["wdkv"][xc * 128:(xc + 1) * 128, :], wsem)
        em.dma("g", s["wkrb"][:, xc * DHR:(xc + 1) * DHR],
               d["wkr"][xc * 128:(xc + 1) * 128, :], wsem)
    for cc in range(2):
        em.dma("g", s["wukb"][:, cc * C:(cc + 1) * C],
               d["wuk"][cc * 128:(cc + 1) * 128, :], wsem)
        em.dma("g", s["wuvb"][:, cc * C:(cc + 1) * C],
               d["wuv"][cc * 128:(cc + 1) * 128, :], wsem)
    em.w("t", wsem, wsem.v)
    em.w("v", wsem, wsem.v)
    em.w("s", wsem, wsem.v)

    xn_rot = Rot(2, em)
    pa_rot = Rot(2)
    pm_rot = Rot(1)
    for tc in range(nch):
        xsl = xn_rot.acquire(em, "y")
        for xc in range(8):
            em.dma("y", s["xnld"][:, xsl * 4096 + xc * 512: xsl * 4096 + (xc + 1) * 512],
                   d["d_xn2"][xc, :, tc * 512: tc * 512 + 512], xn_rot.sems[xsl])
        la_x = xn_rot.sems[xsl].v
        em.w("t", xn_rot.sems[xsl], la_x)
        for cc in range(2):
            psl = pa_rot.acquire(em, "t")
            ps = pbig[:, (PA0, PA1)[psl]: (PA0, PA1)[psl] + 512]
            for xc in range(8):
                em.op("t", lambda t, ps=ps, xsl=xsl, xc=xc, cc=cc: t.matmul(
                    ps, s["wdkvb"][:, xc * 256 + cc * 128: xc * 256 + cc * 128 + 128],
                    s["xnld"][:, xsl * 4096 + xc * 512: xsl * 4096 + (xc + 1) * 512],
                    start=(xc == 0), stop=(xc == 7)), inc=(S["PE"] if xc == 7 else None))
            pe_c = S["PE"].v
            em.w("v", S["PE"], pe_c)
            em.op("v", lambda v, ps=ps, cc=cc, tc=tc: v.tensor_copy(
                s["ckv"][:, cc * T + tc * 512: cc * T + tc * 512 + 512], ps), inc=S["DV"])
            pa_rot.release((S["DV"], S["DV"].v))
        pm_rot.acquire(em, "t")
        pk = pbig[0:DHR, PM: PM + 512]
        for xc in range(8):
            em.op("t", lambda t, pk=pk, xsl=xsl, xc=xc: t.matmul(
                pk, s["wkrb"][:, xc * DHR:(xc + 1) * DHR],
                s["xnld"][:, xsl * 4096 + xc * 512: xsl * 4096 + (xc + 1) * 512],
                start=(xc == 0), stop=(xc == 7)), inc=(S["PE"] if xc == 7 else None))
        pe_k = S["PE"].v
        em.w("v", S["PE"], pe_k)
        em.op("v", lambda v, pk=pk, tc=tc: v.tensor_tensor(
            s["kr"][:, tc * 512: tc * 512 + 512], pk,
            s["rope"][0:DHR, tc * 512: tc * 512 + 512], ALU.mult), inc=S["DV"])
        dv_kr = S["DV"].v
        pm_rot.release((S["DV"], dv_kr))
        xn_rot.release((S["DV"], dv_kr))
    em.drain("v", inc=S["DV"])
    xn_rot.finish(em, "y")

    em.op("v", lambda v: v.memset(s["mk"][:, :], 0.0))
    em.drain("v", inc=S["DV"])

    for tc in range(nch):
        for cc in range(2):
            em.w("s", S["DV"], S["DV"].v)
            em.w("s", S["PE"], S["PE"].v)
            em.op("s", lambda sc, cc=cc, tc=tc: sc.activation(
                s["q2s"][:, 0:512], s["ckv"][:, cc * T + tc * 512: cc * T + tc * 512 + 512],
                AF.Square), inc=S["AC"])
            em.drain("s", inc=S["AC"])
            em.w("t", S["AC"], S["AC"].v)
            em.op("t", lambda t, cc=cc: t.matmul(
                pbig[0:1, PM: PM + 512], s["onescol"][0:128, 0:1], s["q2s"][:, 0:512],
                start=(cc == 0), stop=(cc == 1)), inc=S["PE"])
            if cc == 1:
                em.w("v", S["PE"], S["PE"].v)
                em.op("v", lambda v: v.reduce_max(s["nrm"][0:1, 0:1],
                                                  pbig[0:1, PM: PM + 512], axis=AX.X))
                em.drain("v")
                em.op("v", lambda v: v.tensor_max(
                    s["mk"][0:1, H:H + 1], s["mk"][0:1, H:H + 1], s["nrm"][0:1, 0:1]))
                em.drain("v", inc=S["DV"])
    for tc in range(nch):
        em.w("s", S["DV"], S["DV"].v)
        em.op("s", lambda sc, tc=tc: sc.activation(
            s["q2s"][0:DHR, 0:512], s["kr"][:, tc * 512: tc * 512 + 512], AF.Square),
            inc=S["AC"])
        em.drain("s", inc=S["AC"])
        em.w("t", S["AC"], S["AC"].v)
        em.op("t", lambda t: t.matmul(
            pbig[0:1, PM: PM + 512], s["onescol"][0:DHR, 0:1], s["q2s"][0:DHR, 0:512],
            start=True, stop=True), inc=S["PE"])
        em.w("v", S["PE"], S["PE"].v)
        em.op("v", lambda v: v.reduce_max(s["nrm"][0:1, 0:1],
                                          pbig[0:1, PM: PM + 512], axis=AX.X))
        em.drain("v")
        em.op("v", lambda v: v.tensor_max(
            s["mk"][0:1, H + 1:H + 2], s["mk"][0:1, H + 1:H + 2], s["nrm"][0:1, 0:1]))
        em.drain("v", inc=S["DV"])

    for colh in range(2):
        for cc in range(2):
            em.w("s", S["DV"], S["DV"].v)
            em.w("s", S["PE"], S["PE"].v)
            em.op("s", lambda sc, cc=cc, colh=colh: sc.activation(
                s["q2s"][:, 0:512],
                s["wukb"][:, cc * C + colh * 512: cc * C + colh * 512 + 512],
                AF.Square), inc=S["AC"])
            em.drain("s", inc=S["AC"])
            em.w("t", S["AC"], S["AC"].v)
            em.op("t", lambda t, cc=cc: t.matmul(
                pbig[0:1, PM: PM + 512], s["onescol"][0:128, 0:1], s["q2s"][:, 0:512],
                start=(cc == 0), stop=(cc == 1)), inc=S["PE"])
        em.w("v", S["PE"], S["PE"].v)
        em.op("v", lambda v, colh=colh: v.tensor_copy(
            s["fro"][0:1, colh * 512: colh * 512 + 512], pbig[0:1, PM: PM + 512]),
            inc=S["DV"])
    em.drain("v", inc=S["DV"])
    for h in range(H):
        em.op("v", lambda v, h=h: v.reduce_sum(
            s["mk"][0:1, h:h + 1], s["fro"][0:1, h * 64:(h + 1) * 64], axis=AX.X))
        em.drain("v")
    em.op("v", lambda v: v.tensor_scalar(
        s["mk"][0:1, 0:H], s["mk"][0:1, 0:H], s["mk"][0:1, H:H + 1],
        s["mk"][0:1, H + 1:H + 2], ALU.mult, ALU.add))
    em.drain("v", inc=S["DV"])
    em.w("s", S["DV"], S["DV"].v)
    em.op("s", lambda sc: sc.activation(s["mk"][0:1, 0:H], s["mk"][0:1, 0:H], AF.Sqrt))
    em.drain("s", inc=S["AC"])
    em.w("v", S["AC"], S["AC"].v)
    em.w("t", S["AC"], S["AC"].v)


def q1_phase(em, d, s, p, ntt, q0):
    S = em.sem
    pbig = p["big"]
    nch = ntt * 128 // 512
    wsem = em.dsem("g")
    xn_rot = Rot(2, em)
    st_rot = Rot(2, em)
    for wh in range(2):
        if wh == 1:
            em.w("g", S["PE"], S["PE"].v)
        for xc in range(8):
            em.dma("g", s["wdqh"][:, xc * 2048:(xc + 1) * 2048],
                   d["wdq"][xc * 128:(xc + 1) * 128, wh * 2048:(wh + 1) * 2048], wsem)
        em.w("t", wsem, wsem.v)
        pa_rot = Rot(2)

        def ld_xn(tc):
            xsl_ = xn_rot.acquire(em, "y")
            for xc in range(8):
                em.dma("y", s["xnld"][:, xsl_ * 4096 + xc * 512: xsl_ * 4096 + (xc + 1) * 512],
                       d["d_xn2"][xc, :, q0 * 128 + tc * 512: q0 * 128 + tc * 512 + 512],
                       xn_rot.sems[xsl_])
            return xsl_, xn_rot.sems[xsl_].v

        pend = ld_xn(0)
        for tc in range(nch):
            xsl, la_x = pend
            if tc + 1 < nch:
                pend = ld_xn(tc + 1)
            em.w("t", xn_rot.sems[xsl], la_x)
            dv_last = 0
            for cq in range(16):
                cqc = wh * 16 + cq
                psl = pa_rot.acquire(em, "t")
                ps = pbig[:, (PA0, PA1)[psl]: (PA0, PA1)[psl] + 512]
                for xc in range(8):
                    em.op("t", lambda t, ps=ps, xsl=xsl, xc=xc, cq=cq: t.matmul(
                        ps, s["wdqh"][:, xc * 2048 + cq * 128: xc * 2048 + cq * 128 + 128],
                        s["xnld"][:, xsl * 4096 + xc * 512: xsl * 4096 + (xc + 1) * 512],
                        start=(xc == 0), stop=(xc == 7)), inc=(S["PE"] if xc == 7 else None))
                pe_c = S["PE"].v
                ssl = st_rot.acquire(em, "v")
                cb = s["cqst"][:, ssl * 512:(ssl + 1) * 512]
                em.w("v", S["PE"], pe_c)
                em.op("v", lambda v, cb=cb, ps=ps: v.tensor_copy(cb, ps), inc=S["DV"])
                dv_c = S["DV"].v
                dv_last = dv_c
                pa_rot.release((S["DV"], dv_c))
                em.w("y", S["DV"], dv_c + 1)
                em.dma("y", d["d_cq"][cqc, :, tc * 512: tc * 512 + 512], cb,
                       st_rot.sems[ssl])
                st_rot.release((st_rot.sems[ssl], st_rot.sems[ssl].v))
            xn_rot.release_slot(xsl, (S["DV"], dv_last))
        em.drain("v", inc=S["DV"])
    st_rot.finish(em, "y")
    xn_rot.finish(em, "y")


def q2_quarter(em, d, s, p, ntt, q0, hq):
    S = em.sem
    pbig = p["big"]
    nch = ntt * 128 // 512
    nt = ntt * 128

    wsem = em.dsem("g")
    em.w("g", S["PE"], S["PE"].v)
    for cqc in range(32):
        em.dma("g", s["wuqh"][:, cqc * 256: cqc * 256 + 256],
               d["wuq"][cqc * 128:(cqc + 1) * 128, hq * 256:(hq + 1) * 256], wsem)
        em.dma("g", s["wqrh"][:, cqc * 128: cqc * 128 + 128],
               d["wqr"][cqc * 128:(cqc + 1) * 128, hq * 128:(hq + 1) * 128], wsem)
    em.w("t", wsem, wsem.v)

    for hl in range(4):
        em.op("v", lambda v, hl=hl: v.memset(
            s["qh"][64:128, hl * NTMAX: hl * NTMAX + nt], 0.0))
    em.drain("v", inc=S["DV"])
    em.w("t", S["DV"], S["DV"].v)

    cq_rot = Rot(4, em)
    ev_rot = Rot(2, em)
    dv_prev_evac = 0
    for tc in range(nch):
        slots = [pbig[:, off: off + 512] for off in (PA0, PA1, PB0)]
        em.w("t", S["DV"], dv_prev_evac)
        pe_acc = 0
        for cqc in range(32):
            csl = cq_rot.acquire(em, "y")
            cl = s["cqld"][:, csl * 512:(csl + 1) * 512]
            em.dma("y", cl, d["d_cq"][cqc, :, tc * 512: tc * 512 + 512],
                   cq_rot.sems[csl])
            la_c = cq_rot.sems[csl].v
            em.w("t", cq_rot.sems[csl], la_c)
            for dt in range(2):
                em.op("t", lambda t, ps=slots[dt], cqc=cqc, dt=dt, cl=cl: t.matmul(
                    ps, s["wuqh"][:, cqc * 256 + dt * 128: cqc * 256 + dt * 128 + 128],
                    cl, start=(cqc == 0), stop=(cqc == 31), skip_group_check=True))
            em.op("t", lambda t, ps=slots[2], cqc=cqc, cl=cl: t.matmul(
                ps, s["wqrh"][:, cqc * 128: cqc * 128 + 128],
                cl, start=(cqc == 0), stop=(cqc == 31), skip_group_check=True),
                inc=S["PE"])
            pe_acc = S["PE"].v
            cq_rot.release((S["PE"], pe_acc))
        em.w("v", S["PE"], pe_acc)
        for dt in range(2):
            esl = ev_rot.acquire(em, "v")
            eb = s["qev"][:, esl * 512:(esl + 1) * 512]
            em.op("v", lambda v, eb=eb, ps=slots[dt]: v.tensor_scalar(
                eb, ps, RS, None, ALU.mult))
            em.drain("v", inc=S["DV"])
            h0 = dt * 2
            em.op("v", lambda v, eb=eb, h0=h0, tc=tc: v.tensor_copy(
                s["qh"][0:64, h0 * NTMAX + tc * 512: h0 * NTMAX + tc * 512 + 512],
                eb[0:64, :]))
            em.drain("v", inc=S["DV"])
            dv_cp = S["DV"].v
            em.w("y", S["DV"], dv_cp)
            em.dma("y", s["qh"][0:64, (h0 + 1) * NTMAX + tc * 512:
                                (h0 + 1) * NTMAX + tc * 512 + 512],
                   eb[64:128, :], ev_rot.sems[esl])
            ev_rot.release((ev_rot.sems[esl], ev_rot.sems[esl].v))
        esl = ev_rot.acquire(em, "v")
        eb = s["qev"][:, esl * 512:(esl + 1) * 512]
        em.op("v", lambda v, eb=eb, ps=slots[2]: v.tensor_scalar(
            eb, ps, RS, None, ALU.mult))
        em.drain("v")
        em.op("v", lambda v, eb=eb, tc=tc: v.tensor_tensor(
            eb, eb, s["rope"][:, q0 * 128 + tc * 512: q0 * 128 + tc * 512 + 512],
            ALU.mult))
        em.drain("v", inc=S["DV"])
        dv_e = S["DV"].v
        dv_prev_evac = dv_e
        em.w("y", S["DV"], dv_e)
        for qq in range(4):
            em.dma("y", s["qh"][64:96, qq * NTMAX + tc * 512: qq * NTMAX + tc * 512 + 512],
                   eb[qq * 32:(qq + 1) * 32, :], ev_rot.sems[esl])
        ev_rot.release((ev_rot.sems[esl], ev_rot.sems[esl].v))
    ev_rot.finish(em, "v")
    cq_rot.finish(em, "y")

    dv_rope = S["DV"].v

    bsem = em.dsem()
    em.w("s", S["DV"], dv_rope)
    for hl in range(4):
        h = hq * 4 + hl
        for tc in range(nch):
            em.w("s", S["DV"], S["DV"].v)
            em.w("s", bsem, bsem.v)
            em.op("s", lambda sc, hl=hl, tc=tc: sc.activation(
                s["q2s"][0:96, 0:512],
                s["qh"][0:96, hl * NTMAX + tc * 512: hl * NTMAX + tc * 512 + 512],
                AF.Square), inc=S["AC"])
            em.drain("s", inc=S["AC"])
            em.w("t", S["AC"], S["AC"].v)
            em.op("t", lambda t: t.matmul(
                pbig[0:1, PM: PM + 512], s["onescol"][0:96, 0:1], s["q2s"][0:96, 0:512],
                start=True, stop=True), inc=S["PE"])
            em.w("s", S["PE"], S["PE"].v)
            em.op("s", lambda sc: sc.activation(
                s["qn"][0:1, 0:512], pbig[0:1, PM: PM + 512], AF.Sqrt))
            em.drain("s", inc=S["AC"])
            em.w("v", S["AC"], S["AC"].v)
            em.op("v", lambda v, h=h: v.tensor_scalar(
                s["brow"][0:1, 0:512], s["qn"][0:1, 0:512],
                s["mk"][0:1, h:h + 1], -1.0, ALU.mult, ALU.mult))
            em.drain("v")
            em.op("v", lambda v: v.tensor_copy(s["browb"][0:1, 0:512],
                                               s["brow"][0:1, 0:512]))
            em.drain("v", inc=S["DV"])
            dv_b = S["DV"].v
            em.w("y", S["DV"], dv_b)
            em.dma("y", s["qh"][96:97, hl * NTMAX + tc * 512: hl * NTMAX + tc * 512 + 512],
                   s["browb"][0:1, 0:512], bsem)
    em.w("v", bsem, bsem.v)
    em.w("t", bsem, bsem.v)


def vhat_quarter(em, d, s, p, nkt, hq):
    S = em.sem
    pbig = p["big"]
    em.op("v", lambda v: v.memset(s["vhat"][:, :], 1.0))
    em.drain("v", inc=S["DV"])
    em.w("t", S["DV"], S["DV"].v)
    pb_rot = Rot(2)
    for kt in range(nkt):
        ysl = pb_rot.acquire(em, "t")
        py = pbig[:, (PA0, PA1)[ysl]: (PA0, PA1)[ysl] + 256]
        for cc in range(2):
            em.op("t", lambda t, py=py, cc=cc, kt=kt, hq=hq: t.matmul(
                py, s["ckv"][:, cc * T + kt * 128: cc * T + (kt + 1) * 128],
                s["wuvb"][:, cc * C + hq * 256: cc * C + hq * 256 + 256],
                start=(cc == 0), stop=(cc == 1)), inc=(S["PE"] if cc == 1 else None))
        pe_v = S["PE"].v
        em.w("v", S["PE"], pe_v)
        for hl in range(4):
            em.op("v", lambda v, py=py, hl=hl, kt=kt: v.tensor_copy(
                s["vhat"][:, kt * 260 + hl * 65: kt * 260 + hl * 65 + 64],
                py[:, hl * 64:(hl + 1) * 64]))
        em.drain("v", inc=S["DV"])
        pb_rot.release((S["DV"], S["DV"].v))


def attn_quarter(em, d, s, p, nkt, ntt, q0, hq):
    S = em.sem
    pbig = p["big"]
    nk = nkt * 128

    secs = []
    t0 = 0
    while t0 < ntt:
        secs.append((t0, min(t0 + 8, ntt)))
        t0 += 8

    kh_rot = Rot(2, em)
    pa_rot = Rot(2)
    pt_rot = Rot(2)
    yev_rot = Rot(2, em)
    kc_rot = Rot(2)
    lsem = em.dsem()
    for hl in range(4):
        h = hq * 4 + hl
        ksl = kh_rot.acquire(em, "v")
        kh = s["khat"][:, ksl * T: ksl * T + T]
        em.op("v", lambda v, kh=kh: v.memset(kh[64:128, 0:nk], 0.0))
        em.drain("v", inc=S["DV"])
        dv_z = S["DV"].v
        em.w("y", S["DV"], dv_z)
        em.dma("y", kh[64:96, 0:nk], s["kr"][:, 0:nk], kh_rot.sems[ksl])
        em.dma("y", kh[96:97, 0:nk], d["onesr"][0:1, 0:nk], kh_rot.sems[ksl])
        la_kr = kh_rot.sems[ksl].v
        for tc in range(nk // 512):
            csl = kc_rot.acquire(em, "t")
            pk = pbig[0:64, PB1 + csl * 512: PB1 + csl * 512 + 512]
            for cc in range(2):
                em.op("t", lambda t, pk=pk, cc=cc, h=h, tc=tc: t.matmul(
                    pk, s["wukb"][:, cc * C + h * 64: cc * C + h * 64 + 64],
                    s["ckv"][:, cc * T + tc * 512: cc * T + tc * 512 + 512],
                    start=(cc == 0), stop=(cc == 1), skip_group_check=True),
                    inc=(S["PE"] if cc == 1 else None))
            pe_kc = S["PE"].v
            em.w("v", S["PE"], pe_kc)
            em.op("v", lambda v, kh=kh, pk=pk, tc=tc: v.tensor_copy(
                kh[0:64, tc * 512: tc * 512 + 512], pk), inc=S["DV"])
            kc_rot.release((S["DV"], S["DV"].v))
        em.drain("v", inc=S["DV"])
        dv_kh = S["DV"].v
        em.w("t", kh_rot.sems[ksl], la_kr)
        em.w("t", S["DV"], dv_kh)

        for (st_, en) in secs:
            seclen = (en - st_) * 128
            pav = pbig[0:65, PB0: PB0 + seclen]
            kmax = q0 + en
            first_kt = True
            for kt in range(kmax):
                qlo = max(st_, kt - q0)
                cols0 = qlo * 128
                ncols = (en - qlo) * 128
                diag = (kt - q0 == qlo and kt >= q0 + st_)
                for ch in range((ncols + 511) // 512):
                    cw = min(512, ncols - ch * 512)
                    co = cols0 + ch * 512
                    psl = pa_rot.acquire(em, "t")
                    ps = pbig[:, (PA0, PA1)[psl]: (PA0, PA1)[psl] + cw]
                    em.op("t", lambda t, ps=ps, kh=kh, kt=kt, hl=hl, co=co, cw=cw: t.matmul(
                        ps, kh[:, kt * 128:(kt + 1) * 128],
                        s["qh"][:, hl * NTMAX + co: hl * NTMAX + co + cw],
                        start=True, stop=True), inc=S["PE"])
                    pe_s = S["PE"].v
                    tsl = pt_rot.acquire(em, "s")
                    ptb = s["pt"][:, tsl * 512: tsl * 512 + cw]
                    em.w("s", S["PE"], pe_s)
                    em.op("s", lambda sc, ptb=ptb, ps=ps: sc.activation(
                        ptb, ps, AF.Exp), inc=S["AC"])
                    em.drain("s", inc=S["AC"])
                    ac_e = S["AC"].v
                    pa_rot.release((S["AC"], ac_e))
                    dv_m = None
                    if diag and ch == 0:
                        em.w("v", S["AC"], ac_e)
                        em.op("v", lambda v, ptb=ptb: v.tensor_mul(
                            ptb[:, 0:128], ptb[:, 0:128], s["dmask"][:, :]))
                        em.drain("v", inc=S["DV"])
                        dv_m = S["DV"].v
                    em.w("t", S["AC"], ac_e)
                    if dv_m is not None:
                        em.w("t", S["DV"], dv_m)
                    off0 = co - st_ * 128
                    pieces = []
                    o_, rel_ = off0, 0
                    while rel_ < cw:
                        w_ = min(cw - rel_, 512 - (o_ % 512))
                        pieces.append((o_, rel_, w_))
                        o_ += w_
                        rel_ += w_
                    for pi, (po, pr, pw) in enumerate(pieces):
                        em.op("t", lambda t, pav=pav, ptb=ptb, kt=kt, hl=hl,
                              po=po, pr=pr, pw=pw, fk=first_kt: t.matmul(
                            pav[:, po: po + pw],
                            s["vhat"][:, kt * 260 + hl * 65: kt * 260 + (hl + 1) * 65],
                            ptb[:, pr: pr + pw],
                            start=fk, stop=False, skip_group_check=True),
                            inc=(S["PE"] if pi == len(pieces) - 1 else None))
                    pt_rot.release((S["PE"], S["PE"].v))
                first_kt = False
            em.drain("t", inc=S["PE"])
            pe_av = S["PE"].v
            em.w("v", S["PE"], pe_av)
            em.op("v", lambda v, pav=pav, seclen=seclen: v.tensor_copy(
                s["lr"][64:65, 0:seclen], pav[64:65, 0:seclen]))
            em.drain("v", inc=S["DV"])
            dv_l = S["DV"].v
            em.w("y", S["DV"], dv_l)
            em.w("y", lsem, lsem.v)
            em.dma("y", s["lr"][0:1, 0:seclen], s["lr"][64:65, 0:seclen], lsem)
            la_l = lsem.v
            em.w("v", lsem, la_l)
            em.op("v", lambda v, seclen=seclen: v.reciprocal(
                s["lr"][0:1, 0:seclen], s["lr"][0:1, 0:seclen]))
            em.drain("v", inc=S["DV"])
            for ch in range((seclen + 511) // 512):
                cw = min(512, seclen - ch * 512)
                esl = yev_rot.acquire(em, "v")
                yb = s["yev"][0:64, esl * 512: esl * 512 + cw]
                em.op("v", lambda v, yb=yb, pav=pav, ch=ch, cw=cw: v.tensor_copy(
                    yb, pav[0:64, ch * 512: ch * 512 + cw]))
                em.drain("v", inc=S["DV"])
                em.w("t", S["DV"], S["DV"].v)
                prl = pbig[0:64, PM: PM + cw]
                em.op("t", lambda t, prl=prl, ch=ch, cw=cw: t.matmul(
                    prl, s["ones1f"][0:1, 0:64], s["lr"][0:1, ch * 512: ch * 512 + cw],
                    start=True, stop=True), inc=S["PE"])
                em.w("v", S["PE"], S["PE"].v)
                em.op("v", lambda v, yb=yb, prl=prl: v.tensor_tensor(
                    yb, yb, prl, ALU.mult))
                em.drain("v", inc=S["DV"])
                dv_y = S["DV"].v
                em.w("y", S["DV"], dv_y)
                em.dma("y", d["d_yn"][h, :, q0 * 128 + st_ * 128 + ch * 512:
                                      q0 * 128 + st_ * 128 + ch * 512 + cw], yb,
                       yev_rot.sems[esl])
                yev_rot.release((yev_rot.sems[esl], yev_rot.sems[esl].v))
            em.w("t", S["DV"], S["DV"].v)
        kh_rot.release((S["PE"], S["PE"].v))
    yev_rot.finish(em, "y")
    kh_rot.finish(em, "y")
    em.w("y", lsem, lsem.v)


def wo_phase(em, d, s, p, ntt, q0):
    S = em.sem
    pbig = p["big"]
    wsem = em.dsem("g")
    for dc in range(8):
        em.dma("g", s["wob"][:, dc * C:(dc + 1) * C],
               d["wo"][dc * 128:(dc + 1) * 128, :], wsem)
    em.w("t", wsem, wsem.v)

    yw_rot = Rot(2, em)
    xt_rot = Rot(4, em)
    y_rot = Rot(2)
    o_rot = Rot(2, em)

    def ld_x(j):
        xsl_ = xt_rot.acquire(em, "y")
        xres_ = s["xt"][:, xsl_ * C:(xsl_ + 1) * C]
        em.dma("y", xres_, d["d_x1"][q0 + j, :, :], xt_rot.sems[xsl_])
        return xsl_, xt_rot.sems[xsl_].v, xres_

    def ld_yw(g):
        j0_ = g * 4
        j1_ = min(j0_ + 4, ntt)
        gw_ = (j1_ - j0_) * 128
        ysl_ = yw_rot.acquire(em, "y")
        for dc in range(8):
            em.dma("y", s["yw"][:, ysl_ * 4096 + dc * 512: ysl_ * 4096 + dc * 512 + gw_],
                   d["d_yn"][2 * dc: 2 * dc + 2, :,
                             q0 * 128 + j0_ * 128: q0 * 128 + j0_ * 128 + gw_]
                   .rearrange("a p c -> (a p) c"),
                   yw_rot.sems[ysl_])
        return ysl_, yw_rot.sems[ysl_].v

    ngr = (ntt + 3) // 4
    pend = ld_x(0)
    pendy = ld_yw(0)
    for g in range(ngr):
        j0 = g * 4
        j1 = min(j0 + 4, ntt)
        ysl, la_y = pendy
        if g + 1 < ngr:
            pendy = ld_yw(g + 1)
        pe_g = 0
        for j in range(j0, j1):
            xsl, la_x, xres = pend
            if j + 1 < ntt:
                pend = ld_x(j + 1)
            psl = y_rot.acquire(em, "t")
            py = pbig[:, PB0 + psl * 1024: PB0 + psl * 1024 + 1024]
            em.w("t", yw_rot.sems[ysl], la_y)
            for half in range(2):
                ph = py[:, half * 512: half * 512 + 512]
                for dc in range(8):
                    em.op("t", lambda t, ph=ph, ysl=ysl, dc=dc, j=j, j0=j0, half=half: t.matmul(
                        ph,
                        s["yw"][:, ysl * 4096 + dc * 512 + (j - j0) * 128:
                                ysl * 4096 + dc * 512 + (j - j0) * 128 + 128],
                        s["wob"][:, dc * C + half * 512: dc * C + half * 512 + 512],
                        start=(dc == 0), stop=(dc == 7)),
                        inc=(S["PE"] if (dc == 7 and half == 1) else None))
            pe_y = S["PE"].v
            pe_g = pe_y
            osl = o_rot.acquire(em, "v")
            xo = s["x1t"][:, osl * C:(osl + 1) * C]
            em.w("v", S["PE"], pe_y)
            em.w("v", xt_rot.sems[xsl], la_x)
            em.op("v", lambda v, xo=xo, py=py, xres=xres: v.tensor_add(
                xo, xres, py), inc=S["DV"])
            dv_x = S["DV"].v
            y_rot.release((S["DV"], dv_x))
            xt_rot.release_slot(xsl, (S["DV"], dv_x))
            em.w("y", S["DV"], dv_x + 1)
            em.dma("y", d["d_x2"][j, :, :], xo, o_rot.sems[osl])
            o_rot.release((o_rot.sems[osl], o_rot.sems[osl].v))
        yw_rot.release_slot(ysl, (S["PE"], pe_g))
    em.drain("v", inc=S["DV"])
    o_rot.finish(em, "y")
    yw_rot.finish(em, "y")
    xt_rot.finish(em, "y")


def build_nc():
    nc = bass.Bass("TRN2", target_bir_lowering=False, debug=False)
    d = {}

    def di(name, shape, dt, kind=None):
        if kind:
            d[name] = nc.dram_tensor(name, shape, dt, kind=kind)
        else:
            d[name] = nc.dram_tensor(name, shape, dt)

    di("x4", [T, C], F32, "ExternalInput")
    di("wdq", [C, 4096], F32, "ExternalInput")
    di("wuq", [4096, C], F32, "ExternalInput")
    di("wdkv", [C, 256], F32, "ExternalInput")
    di("wuk", [256, C], F32, "ExternalInput")
    di("wuv", [256, C], F32, "ExternalInput")
    di("wqr", [4096, 512], F32, "ExternalInput")
    di("wkr", [C, DHR], F32, "ExternalInput")
    di("wo", [C, C], F32, "ExternalInput")
    di("wg", [C, E], F32, "ExternalInput")
    di("bg", [1, E], F32, "ExternalInput")
    di("w1", [E, C, C], F32, "ExternalInput")
    di("b1c", [128, 64], F32, "ExternalInput")
    di("w2", [E, C, C], F32, "ExternalInput")
    di("b2", [E, C], BF16, "ExternalInput")
    di("cidf", [128, 128], F32, "ExternalInput")
    di("cidb", [128, 128], BF16, "ExternalInput")
    di("rope", [128, T], F32, "ExternalInput")
    di("dmask", [128, 128], BF16, "ExternalInput")
    di("cunit", [8, 8 * 128], BF16, "ExternalInput")
    di("onesr", [1, T], BF16, "ExternalInput")
    di("out", [NTMAX, C], F32, "ExternalOutput")
    if DBG_PHASE is not None:
        di("odbg", [128, 8192], F32, "ExternalOutput")

    di("d_h", [E, 128, 8, T], BF16)
    di("d_x1", [32, 128, C], F32)
    di("d_x2", [20, 128, C], F32)
    di("d_xnA", [8, 128, T], BF16)
    di("d_xn2", [8, 128, T], BF16)
    di("d_cq", [32, 128, NTMAX], BF16)
    di("d_yn", [H, HS, T], BF16)
    di("d_yp", [32, 128, C], F32)

    ctx = ExitStack()
    semsA, semsB = {}, {}
    for name in ("PE", "DV", "AC", "GP", "SY"):
        hdl = ctx.enter_context(nc.semaphore(name))
        semsA[name] = Sem(hdl)
        semsB[name] = Sem(hdl)
    dpool = [ctx.enter_context(nc.semaphore(f"D{i}")) for i in range(N_DSEM)]
    gpool = [ctx.enter_context(nc.semaphore(f"G{i}")) for i in range(N_GSEM)]
    emA, emB = Em(semsA, dpool, gpool), Em(semsB, dpool, gpool)
    EMS = ((emA, ROLE[0]), (emB, ROLE[1]))

    s = {}
    p = {}
    p["big"] = ctx.enter_context(nc.psum_tensor("pbig", [128, 3584], F32))
    p["bf"] = ctx.enter_context(nc.psum_tensor("pbf", [128, 1024], BF16))

    _ser = [0]

    def sb(scope, name, shape, dt):
        _ser[0] += 1
        s[name] = scope.enter_context(
            nc.sbuf_tensor(f"sb{_ser[0]}_{name}", shape, dt))

    sb(ctx, "ones1f", [1, 128], F32)
    sb(ctx, "ones512f", [1, 512], F32)
    sb(ctx, "onescol", [128, 1], BF16)
    sb(ctx, "epsc", [128, 1], F32)
    sb(ctx, "cidf", [128, 128], F32)
    sb(ctx, "cidb", [128, 128], BF16)
    sb(ctx, "dmask", [128, 128], BF16)
    sb(ctx, "bg", [1, E], F32)
    sb(ctx, "b1c", [128, 64], F32)
    sb(ctx, "b2b", [E, C], BF16)
    sb(ctx, "cunit", [8, 8 * 128], BF16)
    sb(ctx, "wgc", [128, 8 * E], BF16)
    sb(ctx, "wfm2", [8, T], BF16)
    sb(ctx, "xt", [128, 4 * C], F32)
    sb(ctx, "x1t", [128, 2 * C], F32)
    sb(ctx, "sqs", [128, 2 * C], BF16)
    sb(ctx, "q2s", [128, 512], BF16)
    sb(ctx, "nrm", [1, 512], F32)
    sb(ctx, "mk", [1, 2 * H], F32)
    sb(ctx, "qn", [1, 512], F32)
    sb(ctx, "brow", [1, 512], F32)
    sb(ctx, "browb", [1, 512], BF16)
    sb(ctx, "gsc", [128, E], F32)
    sb(ctx, "gmx", [128, E], F32)
    sb(ctx, "gge", [128, E], F32)
    sb(ctx, "gwv", [128, E], F32)
    sb(ctx, "stat", [128, 6 * 32], F32)

    for em, r in EMS:
        preload(em, d, s)
    if DBG_PHASE == "pre":
        for em, r in EMS:
            em.stop_with_dump(d, [(s["cidf"][:, :], 128, 128),
                                  (s["wgc"][:, 0:64], 128, 64),
                                  (s["b1c"][:, :], 128, 64)])
        return _finish(nc, ctx, EMS)

    def run_ln(ntiles_key, src_key, dst):
        with ExitStack() as sp:
            sb(sp, "xnt", [128, 2 * C], BF16)
            sb(sp, "fmst", [128, 2 * 8 * 512], BF16)
            sl_ = dict(s)
            for em, r in EMS:
                if src_key == "x4":
                    src = lambda i: d["x4"][i * 128:(i + 1) * 128, :]
                elif src_key == "x1":
                    src = lambda i: d["d_x1"][i, :, :]
                else:
                    src = lambda i: d["d_x2"][i, :, :]
                ln_phase(em, d, sl_, p, r[ntiles_key], src, dst)
                em.barrier()

    def run_gate(ntiles_key):
        with ExitStack() as sp:
            sb(sp, "gld", [128, 2 * 1024], BF16)
            sl_ = dict(s)
            for em, r in EMS:
                gate_phase(em, d, sl_, p, r[ntiles_key], d["d_xnA"])
                em.barrier()

    def run_ffn(nkt_key, res_key, dst_key):
        with ExitStack() as sp:
            sb(sp, "w1b", [128, 2 * 8 * C], BF16)
            sb(sp, "w2b", [128, 32 * C], BF16)
            sb(sp, "hrelu", [128, 4 * 512], BF16)
            sb(sp, "hst", [128, 4 * 512], BF16)
            sb(sp, "hld", [128, 2 * 4096], BF16)
            sb(sp, "yt", [128, 2 * C], F32)
            sb(sp, "ypld", [128, 2 * C], F32)
            sb(sp, "xnld", [128, 2 * 8 * 512], BF16)
            sl_ = dict(s)
            for em, r in EMS:
                if res_key == "x4":
                    res = lambda j: d["x4"][j * 128:(j + 1) * 128, :]
                else:
                    res = lambda j: d["d_x2"][j, :, :]
                if dst_key == "x1":
                    dst = lambda j: d["d_x1"][j, :, :]
                else:
                    dst = lambda j: d["out"][j * 128:(j + 1) * 128, :]
                ffn_phase(em, d, sl_, p, r[nkt_key], d["d_xnA"], res, dst)
                em.barrier()

    def dbg_stop(taps_fn=None):
        for em, r in EMS:
            em.stop_with_dump(d, taps_fn() if taps_fn else [])

    run_ln("NKT", "x4", d["d_xnA"])
    if DBG_PHASE == "ln1":
        dbg_stop(lambda: [(d["d_xnA"][0, :, 0:2048], 128, 2048),
                          (d["d_xnA"][1, :, 0:2048], 128, 2048)])
        return _finish(nc, ctx, EMS)
    run_gate("NKT")
    if DBG_PHASE == "gate1":
        dbg_stop(lambda: [(s["wfm2"][0:8, 0:2560], 8, 2560)])
        return _finish(nc, ctx, EMS)
    run_ffn("NKT", "x4", "x1")
    if DBG_PHASE == "ffn1":
        dbg_stop(lambda: [(d["d_x1"][0, :, :], 128, 1024),
                          (d["d_x1"][11, :, :], 128, 1024)])
        return _finish(nc, ctx, EMS)
    run_ln("NKT", "x1", d["d_xn2"])
    if DBG_PHASE == "ln2":
        dbg_stop(lambda: [(d["d_xn2"][2, :, 0:2048], 128, 2048),
                          (d["d_xn2"][2, :, 2048:4096], 128, 2048),
                          (d["d_xn2"][5, :, 0:2048], 128, 2048),
                          (d["d_xn2"][5, :, 2048:4096], 128, 2048)])
        return _finish(nc, ctx, EMS)

    with ExitStack() as sp:
        sb(sp, "rope", [128, T], F32)
        sb(sp, "ckv", [128, 2 * T], BF16)
        sb(sp, "kr", [DHR, T], BF16)
        sb(sp, "wdkvb", [128, 8 * 256], BF16)
        sb(sp, "wukb", [128, 2 * C], BF16)
        sb(sp, "wuvb", [128, 2 * C], BF16)
        sb(sp, "wkrb", [128, 8 * DHR], BF16)
        with ExitStack() as spk:
            sb(spk, "xnld", [128, 2 * 8 * 512], BF16)
            sb(spk, "fro", [1, C], F32)
            sk_ = dict(s)
            for em, r in EMS:
                kv_phase(em, d, sk_, p, r["NKT"])
                em.barrier()
        if DBG_PHASE == "kv":
            dbg_stop(lambda: [(s["ckv"][:, 0:2048], 128, 2048),
                              (s["kr"][0:DHR, 0:1024], DHR, 1024),
                              (s["mk"][0:1, 0:2 * H], 1, 2 * H),
                              (d["d_xn2"][0, :, 0:2048], 128, 2048),
                              (d["d_xn2"][3, :, 0:2048], 128, 2048)])
            sp.close()
            return _finish(nc, ctx, EMS)

        with ExitStack() as spq:
            sb(spq, "wdqh", [128, 8 * 2048], BF16)
            sb(spq, "xnld", [128, 2 * 8 * 512], BF16)
            sb(spq, "cqst", [128, 2 * 512], BF16)
            sq_ = dict(s)
            for em, r in EMS:
                q1_phase(em, d, sq_, p, r["NTT"], r["Q0"])
                em.barrier()
        if DBG_PHASE == "q1":
            dbg_stop(lambda: [(d["d_cq"][0, :, 0:2048], 128, 2048),
                              (d["d_cq"][31, :, 0:2048], 128, 2048)])
            sp.close()
            return _finish(nc, ctx, EMS)

        with ExitStack() as spq:
            sb(spq, "wuqh", [128, 32 * 256], BF16)
            sb(spq, "wqrh", [128, 32 * 128], BF16)
            sb(spq, "cqld", [128, 4 * 512], BF16)
            sb(spq, "qev", [128, 2 * 512], BF16)
            sb(spq, "qh", [128, 4 * NTMAX], BF16)
            sb(spq, "vhat", [128, 32 * 4 * 65], BF16)
            sb(spq, "khat", [128, 2 * T], BF16)
            sb(spq, "pt", [128, 2 * 512], BF16)
            sb(spq, "lr", [128, 1024], F32)
            sb(spq, "yev", [64, 2 * 512], BF16)
            sa_ = dict(s)
            for em, r in EMS:
                nq = 1 if DBG_PHASE in ("q2", "vhat", "attn") else 4
                for hq in range(nq):
                    q2_quarter(em, d, sa_, p, r["NTT"], r["Q0"], hq)
                    em.barrier()
                    if DBG_PHASE == "q2":
                        em.stop_with_dump(d, [(s["qh"][:, 0:2048], 128, 2048),
                                              (s["qh"][:, NTMAX:NTMAX + 2048], 128, 2048)])
                        break
                    vhat_quarter(em, d, sa_, p, r["NKT"], hq)
                    em.barrier()
                    if DBG_PHASE == "vhat":
                        em.stop_with_dump(d, [(s["vhat"][:, 0:2080], 128, 2080)])
                        break
                    attn_quarter(em, d, sa_, p, r["NKT"], r["NTT"], r["Q0"], hq)
                    em.barrier()
                    if DBG_PHASE == "attn":
                        nt_, o_ = r["NTT"] * 128, r["Q0"] * 128
                        em.stop_with_dump(
                            d, [(d["d_yn"][0, :, o_:o_ + nt_], HS, nt_),
                                (d["d_yn"][3, :, o_:o_ + nt_], HS, nt_)])
                        break
        if DBG_PHASE in ("q2", "vhat", "attn"):
            sp.close()
            return _finish(nc, ctx, EMS)

    with ExitStack() as sp:
        sb(sp, "wob", [128, 8 * C], BF16)
        sb(sp, "yw", [128, 2 * 4096], BF16)
        sw_ = dict(s)
        for em, r in EMS:
            wo_phase(em, d, sw_, p, r["NTT"], r["Q0"])
            em.barrier()
    if DBG_PHASE == "wo":
        dbg_stop(lambda: [(d["d_x2"][0, :, :], 128, 1024),
                          (d["d_x2"][11, :, :], 128, 1024)])
        return _finish(nc, ctx, EMS)
    run_ln("NTT", "x2", d["d_xnA"])
    run_gate("NTT")
    run_ffn("NTT", "x2", "out")
    for em, r in EMS:
        for sm in em.dsems:
            em.w("y", sm, sm.v)

    return _finish(nc, ctx, EMS)


def _finish(nc, ctx, EMS):
    emA, emB = EMS[0][0], EMS[1][0]
    blk = ctx.enter_context(nc.Block())

    def body(eng, key):
        pid = eng.partition_id()
        with eng.register(f"role_{key}") as r:
            eng.reg_mov(r, pid)
            eng.reg_mod(r, r, 2)
            with eng.If_eq(r, 0):
                for fn in emA.q[key]:
                    fn(eng)
            with eng.Else():
                for fn in emB.q[key]:
                    fn(eng)

    @blk.tensor
    def _(t):
        body(t, "t")

    @blk.vector
    def _(v):
        body(v, "v")

    @blk.scalar
    def _(sc):
        body(sc, "s")

    @blk.gpsimd
    def _(g):
        body(g, "g")

    @blk.sync
    def _(y):
        body(y, "y")

    ctx.close()
    return nc


def make_consts():
    t = np.arange(T, dtype=np.float32)
    inv = 1.0 / (10000.0 ** (np.arange(0, DHR, 2, dtype=np.float32) / DHR))
    freqs = t[:, None] * inv[None, :]
    emb = np.concatenate([freqs, freqs], axis=-1)
    rope = np.ascontiguousarray(np.tile(
        (np.cos(emb) + np.sin(emb)).astype(np.float32).T, (4, 1)))
    ki, qi = np.meshgrid(np.arange(128), np.arange(128), indexing="ij")
    dmask = (qi >= ki).astype(ml_dtypes.bfloat16)
    identf = np.eye(128, dtype=np.float32)
    identb = np.eye(128).astype(ml_dtypes.bfloat16)
    cunit = np.zeros((8, 8 * 128), dtype=ml_dtypes.bfloat16)
    for e in range(8):
        cunit[e, e * 128:(e + 1) * 128] = 1.0
    onesr = np.ones((1, T), dtype=ml_dtypes.bfloat16)
    return rope, dmask, identf, identb, cunit, onesr


_NC_CACHE = {}


def kernel(L0_attn_x, L0_attn_W_DQ, L0_attn_W_UQ, L0_attn_W_DKV, L0_attn_W_UK,
           L0_attn_W_UV, L0_attn_W_QR, L0_attn_W_KR, L0_attn_W_O,
           L0_ffn_W_gate, L0_ffn_b_gate, L0_ffn_W1, L0_ffn_b1, L0_ffn_W2,
           L0_ffn_b2):
    key = ("nc", DBG_PHASE)
    if key not in _NC_CACHE:
        _NC_CACHE[key] = build_nc()
    nc = _NC_CACHE[key]
    rope, dmask, identf, identb, cunit, onesr = make_consts()

    x = np.ascontiguousarray(np.asarray(L0_attn_x, dtype=np.float32))
    b1 = np.asarray(L0_ffn_b1, np.float32).reshape(E * 8, 128).T.copy()
    common = dict(
        wdq=np.asarray(L0_attn_W_DQ, np.float32),
        wuq=np.asarray(L0_attn_W_UQ, np.float32),
        wdkv=np.asarray(L0_attn_W_DKV, np.float32),
        wuk=np.asarray(L0_attn_W_UK, np.float32),
        wuv=np.asarray(L0_attn_W_UV, np.float32),
        wqr=np.asarray(L0_attn_W_QR, np.float32),
        wkr=np.asarray(L0_attn_W_KR, np.float32),
        wo=np.asarray(L0_attn_W_O, np.float32),
        wg=np.asarray(L0_ffn_W_gate, np.float32),
        bg=np.asarray(L0_ffn_b_gate, np.float32).reshape(1, E),
        w1=np.asarray(L0_ffn_W1, np.float32),
        b1c=b1,
        w2=np.asarray(L0_ffn_W2, np.float32),
        b2=np.asarray(L0_ffn_b2, np.float32).astype(ml_dtypes.bfloat16),
        cidf=identf, cidb=identb, rope=rope, dmask=dmask, cunit=cunit,
        onesr=onesr,
    )
    in_maps = []
    for c in range(8):
        m = dict(common)
        m["x4"] = x[c // 2]
        in_maps.append(m)

    res = run_bass_kernel_spmd(nc, in_maps, core_ids=list(range(8)))
    out = np.empty((NB, T, C), dtype=np.float32)
    for b in range(NB):
        out[b, 0:SPLIT, :] = res.results[2 * b]["out"]
        out[b, SPLIT:T, :] = res.results[2 * b + 1]["out"][0: T - SPLIT]
    if DBG_PHASE is not None:
        kernel.dbg = [np.asarray(r["odbg"]) for r in res.results if "odbg" in r]
    return out


# version 03
# speedup vs baseline: 1.0148x; 1.0148x over previous
import math
import numpy as np
import ml_dtypes
import concourse.bass as bass
import concourse.mybir as mybir
from concourse.bass_utils import run_bass_kernel_spmd
from contextlib import ExitStack

F32 = mybir.dt.float32
BF16 = mybir.dt.bfloat16
AF = mybir.ActivationFunctionType
ALU = mybir.AluOpType
AX = mybir.AxisListType

NB, T, C = 4, 4096, 1024
H, HS, DHR = 16, 64, 32
E = 8
SPLIT = 2560
NTMAX = SPLIT
RS = 1.0 / math.sqrt(HS + DHR)

DBG_PHASE = None
N_DSEM = 40
N_GSEM = 8


class Sem:
    def __init__(self, handle):
        self.h = handle
        self.v = 0


class Em:
    ENGS = ("t", "v", "s", "g", "y")

    def __init__(self, sems, dpool, gpool):
        self.q = {k: [] for k in self.ENGS}
        self.sem = sems
        self.dsems = [Sem(h) for h in dpool]
        self.gsems = [Sem(h) for h in gpool]
        self.dnext = 0
        self.gnext = 0
        self.stopped = False

    def dsem(self, qk="y"):
        if qk == "g":
            sm = self.gsems[self.gnext % len(self.gsems)]
            self.gnext += 1
            return sm
        sm = self.dsems[self.dnext % len(self.dsems)]
        self.dnext += 1
        return sm

    def w(self, e, sem, val=None):
        if self.stopped:
            return
        v = sem.v if val is None else val
        if v <= 0:
            return
        self.q[e].append(lambda eng, s=sem.h, v=v: eng.wait_ge(s, v))

    def op(self, e, fn, inc=None, n=1):
        if self.stopped:
            return None
        if inc is None:
            self.q[e].append(lambda eng, fn=fn: fn(eng))
            return None
        inc.v += n
        self.q[e].append(lambda eng, fn=fn, s=inc.h, n=n: fn(eng).then_inc(s, n))
        return inc.v

    def drain(self, e, inc=None, n=1):
        if self.stopped:
            return None
        if inc is None:
            self.q[e].append(lambda eng: eng.drain())
            return None
        inc.v += n
        self.q[e].append(lambda eng, s=inc.h, n=n: eng.drain().then_inc(s, n))
        return inc.v

    def dma(self, e, dst, src, sem):
        if self.stopped:
            return None
        sem.v += 16
        self.q[e].append(
            lambda eng, dst=dst, src=src, s=sem.h: eng.dma_start(dst, src).then_inc(s, 16))
        return sem.v

    def barrier(self):
        if self.stopped:
            return
        S = self.sem
        self.drain("t", inc=S["PE"])
        self.drain("v", inc=S["DV"])
        self.drain("s", inc=S["AC"])
        self.drain("g", inc=S["GP"])
        self.drain("y", inc=S["SY"])
        toks = [(S[k], S[k].v) for k in ("PE", "DV", "AC", "GP", "SY")]
        for e in self.ENGS:
            for sem, v in toks:
                self.w(e, sem, v)

    def stop_with_dump(self, d, taps):
        self.barrier()
        sm = self.dsem("g")
        col = 0
        for ap, rows, ncols in taps:
            self.dma("g", d["odbg"][0:rows, col:col + ncols], ap, sm)
            col += ncols
        self.w("g", sm, sm.v)
        self.stopped = True


class Rot:

    def __init__(self, n, em=None, qk="y"):
        self.n = n
        self.rel = [[] for _ in range(n)]
        self.i = 0
        self.sems = [em.dsem(qk) for _ in range(n)] if em is not None else None

    def acquire(self, em, engine):
        sl = self.i % self.n
        for sem, v in self.rel[sl]:
            em.w(engine, sem, v)
        self.rel[sl] = []
        self.i += 1
        return sl

    def release(self, *tokens):
        sl = (self.i - 1) % self.n
        self.rel[sl].extend(tokens)

    def release_slot(self, sl, *tokens):
        self.rel[sl].extend(tokens)

    def finish(self, em, engine):
        if self.sems:
            for sm in self.sems:
                em.w(engine, sm, sm.v)


PA0, PA1 = 0, 512
PB0, PB1 = 1024, 2048
PM = 3072

ROLE = [dict(NKT=20, NTT=20, Q0=0), dict(NKT=32, NTT=12, Q0=20)]


def preload(em, d, s):
    S = em.sem
    em.op("v", lambda v: v.memset(s["ones1f"][:, :], 1.0))
    em.op("v", lambda v: v.memset(s["ones512f"][:, :], 1.0))
    em.op("v", lambda v: v.memset(s["onescol"][:, :], 1.0))
    em.op("v", lambda v: v.memset(s["epsc"][:, :], 1e-5))
    em.drain("v", inc=S["DV"])
    sm = em.dsem("g")
    em.dma("g", s["cidf"][:, :], d["cidf"][:, :], sm)
    em.dma("g", s["cidb"][:, :], d["cidb"][:, :], sm)
    em.dma("g", s["dmask"][:, :], d["dmask"][:, :], sm)
    em.dma("g", s["bg"][:, :], d["bg"][:, :], sm)
    em.dma("g", s["b1c"][:, :], d["b1c"][:, :], sm)
    em.dma("g", s["b2b"][:, :], d["b2"][:, :], sm)
    em.dma("g", s["cunit"][:, :], d["cunit"][:, :], sm)
    for xc in range(8):
        em.dma("g", s["wgc"][:, xc * 8:(xc + 1) * 8],
               d["wg"][xc * 128:(xc + 1) * 128, :], sm)
    em.w("g", sm, sm.v)
    em.barrier()


def ln_phase(em, d, s, p, ntiles, src_fn, dst_dram):
    S = em.sem
    st = s["stat"]
    pbf = p["bf"]

    em.op("v", lambda v: v.memset(st[:, :], 0.0))
    em.drain("v", inc=S["DV"])
    dv_zero = S["DV"].v

    xt_rot = Rot(4, em)
    sq_rot = Rot(2)
    for i in range(ntiles):
        sl = xt_rot.acquire(em, "y")
        xs = s["xt"][:, sl * C:(sl + 1) * C]
        em.dma("y", xs, src_fn(i), xt_rot.sems[sl])
        lv = xt_rot.sems[sl].v
        em.w("s", xt_rot.sems[sl], lv)
        em.w("s", S["DV"], dv_zero)
        qsl = sq_rot.acquire(em, "s")
        em.op("s", lambda sc, xs=xs, i=i, qsl=qsl: sc.activation(
            s["sqs"][:, qsl * C:(qsl + 1) * C], xs, AF.Square,
            accum_out=st[:, 32 + i:33 + i]), inc=S["AC"])
        ac = S["AC"].v
        sq_rot.release((S["AC"], ac))
        em.w("v", xt_rot.sems[sl], lv)
        em.op("v", lambda v, xs=xs, i=i: v.reduce_sum(st[:, i:i + 1], xs, axis=AX.X),
              inc=S["DV"])
        xt_rot.release((S["AC"], ac), (S["DV"], S["DV"].v))
    em.drain("s", inc=S["AC"])
    em.drain("v", inc=S["DV"])
    ac_stats = S["AC"].v

    n = ntiles
    em.w("v", S["AC"], ac_stats)
    em.op("v", lambda v: v.tensor_scalar(st[:, 64:64 + n], st[:, 0:n], 1.0 / C, None, ALU.mult))
    em.drain("v")
    em.op("v", lambda v: v.tensor_scalar(st[:, 96:96 + n], st[:, 32:32 + n], 1.0 / C, None, ALU.mult))
    em.drain("v")
    em.op("v", lambda v: v.tensor_mul(st[:, 160:160 + n], st[:, 64:64 + n], st[:, 64:64 + n]))
    em.drain("v")
    em.op("v", lambda v: v.tensor_sub(st[:, 96:96 + n], st[:, 96:96 + n], st[:, 160:160 + n]))
    em.drain("v", inc=S["DV"])
    em.w("s", S["DV"], S["DV"].v)
    em.op("s", lambda sc: sc.activation(st[:, 160:160 + n], st[:, 96:96 + n], AF.Ln,
                                        bias=s["epsc"][:, :]))
    em.drain("s")
    em.op("s", lambda sc: sc.activation(st[:, 128:128 + n], st[:, 160:160 + n], AF.Exp,
                                        scale=-0.5))
    em.drain("s", inc=S["AC"])
    ac_rstd = S["AC"].v
    em.w("v", S["AC"], ac_rstd)

    xnt_rot = Rot(2)
    pbf_rot = Rot(1)
    fm_rot = Rot(2, em)
    fsl = 0
    for i in range(ntiles):
        sl = xt_rot.acquire(em, "y")
        xs = s["xt"][:, sl * C:(sl + 1) * C]
        em.dma("y", xs, src_fn(i), xt_rot.sems[sl])
        lv = xt_rot.sems[sl].v
        nsl = xnt_rot.acquire(em, "v")
        xnt = s["xnt"][:, nsl * C:(nsl + 1) * C]
        em.w("v", xt_rot.sems[sl], lv)
        em.op("v", lambda v, xs=xs, xnt=xnt, i=i: v.tensor_scalar(
            xnt, xs, st[:, 64 + i:65 + i], st[:, 128 + i:129 + i], ALU.subtract, ALU.mult))
        em.drain("v", inc=S["DV"])
        dv_xnt = S["DV"].v
        xt_rot.release((S["DV"], dv_xnt))

        pbf_rot.acquire(em, "t")
        em.w("t", S["DV"], dv_xnt)
        for xc in range(8):
            em.op("t", lambda t, xnt=xnt, xc=xc: t.transpose(
                pbf[:, xc * 128:(xc + 1) * 128], xnt[:, xc * 128:(xc + 1) * 128],
                s["cidb"][:, :]))
        em.drain("t", inc=S["PE"])
        pe_tr = S["PE"].v
        xnt_rot.release((S["PE"], pe_tr))
        em.w("v", S["PE"], pe_tr)
        if i % 4 == 0:
            fsl = fm_rot.acquire(em, "v")
        for xc in range(8):
            em.op("v", lambda v, xc=xc, i=i, fsl=fsl: v.tensor_copy(
                s["fmst"][:, fsl * 4096 + xc * 512 + (i % 4) * 128:
                          fsl * 4096 + xc * 512 + (i % 4) * 128 + 128],
                pbf[:, xc * 128:(xc + 1) * 128]))
        em.drain("v", inc=S["DV"])
        pbf_rot.release((S["DV"], S["DV"].v))
        if i % 4 == 3 or i == ntiles - 1:
            g0 = (i // 4) * 4
            ncb = (i - g0 + 1) * 128
            em.w("y", S["DV"], S["DV"].v)
            for xc in range(8):
                em.dma("y", dst_dram[xc, :, g0 * 128: g0 * 128 + ncb],
                       s["fmst"][:, fsl * 4096 + xc * 512: fsl * 4096 + xc * 512 + ncb],
                       fm_rot.sems[fsl])
            fm_rot.release((fm_rot.sems[fsl], fm_rot.sems[fsl].v))
    xt_rot.finish(em, "y")
    fm_rot.finish(em, "y")


def gate_phase(em, d, s, p, ntiles, xn_dram):
    S = em.sem
    pbig = p["big"]
    gld_rot = Rot(2, em)
    sc_rot = Rot(2)
    tr_rot = Rot(2)
    for i in range(ntiles):
        lsl = gld_rot.acquire(em, "y")
        gl = s["gld"][:, lsl * 1024:(lsl + 1) * 1024]
        for xc in range(8):
            em.dma("y", gl[:, xc * 128:(xc + 1) * 128],
                   xn_dram[xc, :, i * 128:(i + 1) * 128], gld_rot.sems[lsl])
        lv = gld_rot.sems[lsl].v
        ssl = sc_rot.acquire(em, "t")
        ps = pbig[:, (PA0, PA1)[ssl]: (PA0, PA1)[ssl] + 8]
        em.w("t", gld_rot.sems[lsl], lv)
        for xc in range(8):
            em.op("t", lambda t, ps=ps, gl=gl, xc=xc: t.matmul(
                ps, gl[:, xc * 128:(xc + 1) * 128],
                s["wgc"][:, xc * 8:(xc + 1) * 8], start=(xc == 0), stop=False))
        em.op("t", lambda t, ps=ps: t.matmul(
            ps, s["ones1f"][:, :], s["bg"][:, :], start=False, stop=True), inc=S["PE"])
        pe_sc = S["PE"].v
        gld_rot.release((S["PE"], pe_sc))
        em.w("v", S["PE"], pe_sc)
        em.op("v", lambda v, ps=ps: v.tensor_copy(s["gsc"][:, :], ps))
        em.drain("v")
        em.op("v", lambda v: v.max(out=s["gmx"][:, :], in_=s["gsc"][:, :]))
        em.drain("v")
        em.op("v", lambda v: v.tensor_scalar(
            s["gge"][:, :], s["gsc"][:, :], s["gmx"][:, 1:2], None, ALU.is_ge))
        em.drain("v")
        em.op("v", lambda v: v.tensor_mul(s["gwv"][:, :], s["gsc"][:, :], s["gge"][:, :]))
        em.drain("v", inc=S["DV"])
        dv_w = S["DV"].v
        sc_rot.release((S["DV"], dv_w))

        tsl = tr_rot.acquire(em, "t")
        em.w("t", S["DV"], dv_w)
        p8 = pbig[0:8, (PB0, PB1)[tsl]: (PB0, PB1)[tsl] + 128]
        em.op("t", lambda t, p8=p8: t.matmul(
            p8, s["gwv"][:, :], s["cidf"][:, :], start=True, stop=True), inc=S["PE"])
        pe_tr = S["PE"].v
        em.w("v", S["PE"], pe_tr)
        em.op("v", lambda v, p8=p8, i=i: v.tensor_copy(
            s["wfm2"][0:8, i * 128:(i + 1) * 128], p8))
        em.drain("v", inc=S["DV"])
        tr_rot.release((S["DV"], S["DV"].v))
    gld_rot.finish(em, "y")


def ffn_phase(em, d, s, p, nkt, xn_dram, res_fn, dst_fn):
    S = em.sem
    pbig = p["big"]
    nch = nkt * 128 // 512

    w1_rot = Rot(2, em, "g")
    xn_rot = Rot(2, em)
    wbc_rot = Rot(2)
    h_rot = Rot(2)
    relu_rot = Rot(4)
    hst_rot = Rot(4, em)
    w2sem = [em.dsem("g"), em.dsem("g")]
    pe0 = S["PE"].v
    w1info = []
    for e in range(E):
        wsl = w1_rot.acquire(em, "g")
        for xc in range(8):
            em.dma("g", s["w1b"][:, wsl * 8 * C + xc * C: wsl * 8 * C + (xc + 1) * C],
                   d["w1"][e, xc * 128:(xc + 1) * 128, :], w1_rot.sems[wsl])
        w1info.append((wsl, w1_rot.sems[wsl].v))
        w1_rot.release((S["PE"], pe0 + (e + 1) * nch * 9))
        if e < 4:
            for hc in range(8):
                em.dma("g", s["w2b"][:, (e * 8 + hc) * C:(e * 8 + hc + 1) * C],
                       d["w2"][e, hc * 128:(hc + 1) * 128, :], w2sem[0])

    def ld_xn(tc):
        xsl_ = xn_rot.acquire(em, "y")
        for xc in range(8):
            em.dma("y", s["xnld"][:, xsl_ * 4096 + xc * 512: xsl_ * 4096 + (xc + 1) * 512],
                   xn_dram[xc, :, tc * 512: tc * 512 + 512], xn_rot.sems[xsl_])
        return xsl_, xn_rot.sems[xsl_].v

    pend = ld_xn(0)
    for idx in range(E * nch):
        e, tc = divmod(idx, nch)
        xsl, la_x = pend
        if idx + 1 < E * nch:
            pend = ld_xn((idx + 1) % nch)
        wsl, lw_e = w1info[e]
        if tc == 0:
            em.w("t", w1_rot.sems[wsl], lw_e)
        if True:
            bsl = wbc_rot.acquire(em, "t")
            pw = pbig[:, PB0 + bsl * 512: PB0 + bsl * 512 + 512]
            em.op("t", lambda t, pw=pw, e=e, tc=tc: t.matmul(
                pw, s["cunit"][0:8, e * 128:(e + 1) * 128],
                s["wfm2"][0:8, tc * 512: tc * 512 + 512],
                start=True, stop=True), inc=S["PE"])
            pe_wbc = S["PE"].v
            em.w("t", xn_rot.sems[xsl], la_x)
            for ht in range(8):
                hsl = h_rot.acquire(em, "t")
                ph = pbig[:, PA0 + hsl * 512: PA0 + hsl * 512 + 512]
                for xc in range(8):
                    em.op("t", lambda t, ph=ph, wsl=wsl, xsl=xsl, xc=xc, ht=ht: t.matmul(
                        ph,
                        s["w1b"][:, wsl * 8 * C + xc * C + ht * 128:
                                 wsl * 8 * C + xc * C + ht * 128 + 128],
                        s["xnld"][:, xsl * 4096 + xc * 512: xsl * 4096 + (xc + 1) * 512],
                        start=(xc == 0), stop=(xc == 7)),
                        inc=(S["PE"] if xc == 7 else None))
                pe_h = S["PE"].v
                pe_last = pe_h
                rsl = relu_rot.acquire(em, "s")
                rb = s["hrelu"][:, rsl * 512:(rsl + 1) * 512]
                em.w("s", S["PE"], pe_h)
                em.op("s", lambda sc, rb=rb, ph=ph, e=e, ht=ht: sc.activation(
                    rb, ph, AF.Relu, bias=s["b1c"][:, e * 8 + ht: e * 8 + ht + 1]),
                    inc=S["AC"])
                ac_r = S["AC"].v
                h_rot.release((S["AC"], ac_r))
                osl = hst_rot.acquire(em, "v")
                ob = s["hst"][:, osl * 512:(osl + 1) * 512]
                em.w("v", S["AC"], ac_r + 1)
                em.w("v", S["PE"], pe_wbc)
                em.op("v", lambda v, ob=ob, rb=rb, pw=pw: v.tensor_tensor(
                    ob, rb, pw, ALU.mult), inc=S["DV"])
                dv_h = S["DV"].v
                relu_rot.release((S["DV"], dv_h))
                if ht == 7:
                    wbc_rot.release((S["DV"], dv_h))
                    xn_rot.release_slot(xsl, (S["PE"], pe_h))
                em.w("y", S["DV"], dv_h + 1)
                em.dma("y", d["d_h"][e, :, ht, tc * 512: tc * 512 + 512], ob,
                       hst_rot.sems[osl])
                hst_rot.release((hst_rot.sems[osl], hst_rot.sems[osl].v))
    assert S["PE"].v == pe0 + E * nch * 9
    em.drain("s", inc=S["AC"])
    em.drain("v", inc=S["DV"])
    dv_w1_done = S["DV"].v

    for pa in range(2):
        es = list(range(pa * 4, pa * 4 + 4))
        if pa == 1:
            em.w("g", S["PE"], S["PE"].v)
            for e in es:
                for hc in range(8):
                    em.dma("g", s["w2b"][:, ((e - 4) * 8 + hc) * C:
                                         ((e - 4) * 8 + hc + 1) * C],
                           d["w2"][e, hc * 128:(hc + 1) * 128, :], w2sem[1])
        em.w("t", w2sem[pa], w2sem[pa].v)
        hld_rot = Rot(2, em)
        xt_rot = Rot(2, em)
        yp_rot = Rot(2, em)
        y_rot = Rot(2)
        o_rot = Rot(2, em)
        hst_rot.finish(em, "y")

        def ld_j(j):
            lsl_ = hld_rot.acquire(em, "y")
            for ei, e in enumerate(es):
                em.dma("y",
                       s["hld"][:, lsl_ * 4096 + ei * 1024: lsl_ * 4096 + (ei + 1) * 1024]
                       .rearrange("p (a c) -> p a c", a=8),
                       d["d_h"][e, :, :, j * 128:(j + 1) * 128],
                       hld_rot.sems[lsl_])
            if pa == 1:
                xsl_ = xt_rot.acquire(em, "y")
                xres_ = s["xt"][:, xsl_ * C:(xsl_ + 1) * C]
                em.dma("y", xres_, res_fn(j), xt_rot.sems[xsl_])
                psl_ = yp_rot.acquire(em, "y")
                ypl_ = s["ypld"][:, psl_ * C:(psl_ + 1) * C]
                em.dma("y", ypl_, d["d_yp"][j, :, :], yp_rot.sems[psl_])
                return (lsl_, hld_rot.sems[lsl_].v, xsl_, xt_rot.sems[xsl_].v,
                        xres_, psl_, yp_rot.sems[psl_].v, ypl_)
            return (lsl_, hld_rot.sems[lsl_].v, 0, 0, None, 0, 0, None)

        pend = ld_j(0)
        for j in range(nkt):
            lsl, la_h, xsl, la_x, xres, psl, la_yp, ypl = pend
            if j + 1 < nkt:
                pend = ld_j(j + 1)
            ysl = y_rot.acquire(em, "t")
            py = pbig[:, PB0 + ysl * 1024: PB0 + ysl * 1024 + 1024]
            em.w("t", hld_rot.sems[lsl], la_h)
            em.w("t", S["DV"], dv_w1_done)
            for half in range(2):
                ph = py[:, half * 512: half * 512 + 512]
                k = 0
                for ei, e in enumerate(es):
                    for hc in range(8):
                        last = (ei == 3 and hc == 7)
                        em.op("t", lambda t, ph=ph, lsl=lsl, ei=ei, hc=hc,
                              half=half, k=k, last=last, pa=pa: t.matmul(
                            ph,
                            s["hld"][:, lsl * 4096 + ei * 1024 + hc * 128:
                                     lsl * 4096 + ei * 1024 + hc * 128 + 128],
                            s["w2b"][:, (ei * 8 + hc) * C + half * 512:
                                     (ei * 8 + hc) * C + half * 512 + 512],
                            start=(k == 0), stop=(last and pa == 0)),
                            inc=(S["PE"] if (last and pa == 0 and half == 1) else None))
                        k += 1
                if pa == 1:
                    em.op("t", lambda t, ph=ph, j=j, half=half: t.matmul(
                        ph, s["wfm2"][0:8, j * 128:(j + 1) * 128],
                        s["b2b"][0:8, half * 512: half * 512 + 512],
                        start=False, stop=True),
                        inc=(S["PE"] if half == 1 else None))
            pe_y = S["PE"].v
            hld_rot.release_slot(lsl, (S["PE"], pe_y))
            if pa == 0:
                osl = o_rot.acquire(em, "v")
                yt = s["yt"][:, osl * C:(osl + 1) * C]
                em.w("v", S["PE"], pe_y)
                em.op("v", lambda v, yt=yt, py=py: v.tensor_copy(yt, py), inc=S["DV"])
                dv_y = S["DV"].v
                y_rot.release((S["DV"], dv_y))
                em.w("y", S["DV"], dv_y + 1)
                em.dma("y", d["d_yp"][j, :, :], yt, o_rot.sems[osl])
                o_rot.release((o_rot.sems[osl], o_rot.sems[osl].v))
            else:
                osl = o_rot.acquire(em, "v")
                xo = s["x1t"][:, osl * C:(osl + 1) * C]
                em.w("v", S["PE"], pe_y)
                em.w("v", xt_rot.sems[xsl], la_x)
                em.w("v", yp_rot.sems[psl], la_yp)
                em.op("v", lambda v, py=py, ypl=ypl: v.tensor_add(ypl, ypl, py))
                em.drain("v")
                em.op("v", lambda v, xo=xo, ypl=ypl, xres=xres: v.scalar_tensor_tensor(
                    xo, ypl, 0.5, xres, ALU.mult, ALU.add), inc=S["DV"])
                dv_x = S["DV"].v
                y_rot.release((S["DV"], dv_x))
                xt_rot.release_slot(xsl, (S["DV"], dv_x))
                yp_rot.release_slot(psl, (S["DV"], dv_x))
                em.w("y", S["DV"], dv_x + 1)
                em.dma("y", dst_fn(j), xo, o_rot.sems[osl])
                o_rot.release((o_rot.sems[osl], o_rot.sems[osl].v))
        em.drain("v", inc=S["DV"])
        o_rot.finish(em, "y")
        hld_rot.finish(em, "y")
        if pa == 1:
            xt_rot.finish(em, "y")
            yp_rot.finish(em, "y")
    xn_rot.finish(em, "y")
    w1_rot.finish(em, "g")
    em.w("g", w2sem[0], w2sem[0].v)
    em.w("g", w2sem[1], w2sem[1].v)


def kv_phase(em, d, s, p, nkt):
    S = em.sem
    pbig = p["big"]
    nch = nkt * 128 // 512

    wsem = em.dsem("g")
    em.dma("g", s["rope"][:, :], d["rope"][:, :], wsem)
    for xc in range(8):
        em.dma("g", s["wdkvb"][:, xc * 256:(xc + 1) * 256],
               d["wdkv"][xc * 128:(xc + 1) * 128, :], wsem)
        em.dma("g", s["wkrb"][:, xc * DHR:(xc + 1) * DHR],
               d["wkr"][xc * 128:(xc + 1) * 128, :], wsem)
    for cc in range(2):
        em.dma("g", s["wukb"][:, cc * C:(cc + 1) * C],
               d["wuk"][cc * 128:(cc + 1) * 128, :], wsem)
        em.dma("g", s["wuvb"][:, cc * C:(cc + 1) * C],
               d["wuv"][cc * 128:(cc + 1) * 128, :], wsem)
    em.w("t", wsem, wsem.v)
    em.w("v", wsem, wsem.v)
    em.w("s", wsem, wsem.v)

    xn_rot = Rot(2, em)
    pa_rot = Rot(2)
    pm_rot = Rot(1)
    for tc in range(nch):
        xsl = xn_rot.acquire(em, "y")
        for xc in range(8):
            em.dma("y", s["xnld"][:, xsl * 4096 + xc * 512: xsl * 4096 + (xc + 1) * 512],
                   d["d_xn2"][xc, :, tc * 512: tc * 512 + 512], xn_rot.sems[xsl])
        la_x = xn_rot.sems[xsl].v
        em.w("t", xn_rot.sems[xsl], la_x)
        for cc in range(2):
            psl = pa_rot.acquire(em, "t")
            ps = pbig[:, (PA0, PA1)[psl]: (PA0, PA1)[psl] + 512]
            for xc in range(8):
                em.op("t", lambda t, ps=ps, xsl=xsl, xc=xc, cc=cc: t.matmul(
                    ps, s["wdkvb"][:, xc * 256 + cc * 128: xc * 256 + cc * 128 + 128],
                    s["xnld"][:, xsl * 4096 + xc * 512: xsl * 4096 + (xc + 1) * 512],
                    start=(xc == 0), stop=(xc == 7)), inc=(S["PE"] if xc == 7 else None))
            pe_c = S["PE"].v
            em.w("v", S["PE"], pe_c)
            em.op("v", lambda v, ps=ps, cc=cc, tc=tc: v.tensor_copy(
                s["ckv"][:, cc * T + tc * 512: cc * T + tc * 512 + 512], ps), inc=S["DV"])
            pa_rot.release((S["DV"], S["DV"].v))
        pm_rot.acquire(em, "t")
        pk = pbig[0:DHR, PM: PM + 512]
        for xc in range(8):
            em.op("t", lambda t, pk=pk, xsl=xsl, xc=xc: t.matmul(
                pk, s["wkrb"][:, xc * DHR:(xc + 1) * DHR],
                s["xnld"][:, xsl * 4096 + xc * 512: xsl * 4096 + (xc + 1) * 512],
                start=(xc == 0), stop=(xc == 7)), inc=(S["PE"] if xc == 7 else None))
        pe_k = S["PE"].v
        em.w("v", S["PE"], pe_k)
        em.op("v", lambda v, pk=pk, tc=tc: v.tensor_tensor(
            s["kr"][:, tc * 512: tc * 512 + 512], pk,
            s["rope"][0:DHR, tc * 512: tc * 512 + 512], ALU.mult), inc=S["DV"])
        dv_kr = S["DV"].v
        pm_rot.release((S["DV"], dv_kr))
        xn_rot.release((S["DV"], dv_kr))
    em.drain("v", inc=S["DV"])
    xn_rot.finish(em, "y")

    em.op("v", lambda v: v.memset(s["mk"][:, :], 0.0))
    em.drain("v", inc=S["DV"])

    for tc in range(nch):
        for cc in range(2):
            em.w("s", S["DV"], S["DV"].v)
            em.w("s", S["PE"], S["PE"].v)
            em.op("s", lambda sc, cc=cc, tc=tc: sc.activation(
                s["q2s"][:, 0:512], s["ckv"][:, cc * T + tc * 512: cc * T + tc * 512 + 512],
                AF.Square), inc=S["AC"])
            em.drain("s", inc=S["AC"])
            em.w("t", S["AC"], S["AC"].v)
            em.op("t", lambda t, cc=cc: t.matmul(
                pbig[0:1, PM: PM + 512], s["onescol"][0:128, 0:1], s["q2s"][:, 0:512],
                start=(cc == 0), stop=(cc == 1)), inc=S["PE"])
            if cc == 1:
                em.w("v", S["PE"], S["PE"].v)
                em.op("v", lambda v: v.reduce_max(s["nrm"][0:1, 0:1],
                                                  pbig[0:1, PM: PM + 512], axis=AX.X))
                em.drain("v")
                em.op("v", lambda v: v.tensor_max(
                    s["mk"][0:1, H:H + 1], s["mk"][0:1, H:H + 1], s["nrm"][0:1, 0:1]))
                em.drain("v", inc=S["DV"])
    for tc in range(nch):
        em.w("s", S["DV"], S["DV"].v)
        em.op("s", lambda sc, tc=tc: sc.activation(
            s["q2s"][0:DHR, 0:512], s["kr"][:, tc * 512: tc * 512 + 512], AF.Square),
            inc=S["AC"])
        em.drain("s", inc=S["AC"])
        em.w("t", S["AC"], S["AC"].v)
        em.op("t", lambda t: t.matmul(
            pbig[0:1, PM: PM + 512], s["onescol"][0:DHR, 0:1], s["q2s"][0:DHR, 0:512],
            start=True, stop=True), inc=S["PE"])
        em.w("v", S["PE"], S["PE"].v)
        em.op("v", lambda v: v.reduce_max(s["nrm"][0:1, 0:1],
                                          pbig[0:1, PM: PM + 512], axis=AX.X))
        em.drain("v")
        em.op("v", lambda v: v.tensor_max(
            s["mk"][0:1, H + 1:H + 2], s["mk"][0:1, H + 1:H + 2], s["nrm"][0:1, 0:1]))
        em.drain("v", inc=S["DV"])

    for colh in range(2):
        for cc in range(2):
            em.w("s", S["DV"], S["DV"].v)
            em.w("s", S["PE"], S["PE"].v)
            em.op("s", lambda sc, cc=cc, colh=colh: sc.activation(
                s["q2s"][:, 0:512],
                s["wukb"][:, cc * C + colh * 512: cc * C + colh * 512 + 512],
                AF.Square), inc=S["AC"])
            em.drain("s", inc=S["AC"])
            em.w("t", S["AC"], S["AC"].v)
            em.op("t", lambda t, cc=cc: t.matmul(
                pbig[0:1, PM: PM + 512], s["onescol"][0:128, 0:1], s["q2s"][:, 0:512],
                start=(cc == 0), stop=(cc == 1)), inc=S["PE"])
        em.w("v", S["PE"], S["PE"].v)
        em.op("v", lambda v, colh=colh: v.tensor_copy(
            s["fro"][0:1, colh * 512: colh * 512 + 512], pbig[0:1, PM: PM + 512]),
            inc=S["DV"])
    em.drain("v", inc=S["DV"])
    for h in range(H):
        em.op("v", lambda v, h=h: v.reduce_sum(
            s["mk"][0:1, h:h + 1], s["fro"][0:1, h * 64:(h + 1) * 64], axis=AX.X))
        em.drain("v")
    em.op("v", lambda v: v.tensor_scalar(
        s["mk"][0:1, 0:H], s["mk"][0:1, 0:H], s["mk"][0:1, H:H + 1],
        s["mk"][0:1, H + 1:H + 2], ALU.mult, ALU.add))
    em.drain("v", inc=S["DV"])
    em.w("s", S["DV"], S["DV"].v)
    em.op("s", lambda sc: sc.activation(s["mk"][0:1, 0:H], s["mk"][0:1, 0:H], AF.Sqrt))
    em.drain("s", inc=S["AC"])
    em.w("v", S["AC"], S["AC"].v)
    em.w("t", S["AC"], S["AC"].v)


def q1_phase(em, d, s, p, ntt, q0):
    S = em.sem
    pbig = p["big"]
    nch = ntt * 128 // 512
    wsem = em.dsem("g")
    xn_rot = Rot(2, em)
    st_rot = Rot(2, em)
    for wh in range(2):
        if wh == 1:
            em.w("g", S["PE"], S["PE"].v)
        for xc in range(8):
            em.dma("g", s["wdqh"][:, xc * 2048:(xc + 1) * 2048],
                   d["wdq"][xc * 128:(xc + 1) * 128, wh * 2048:(wh + 1) * 2048], wsem)
        em.w("t", wsem, wsem.v)
        pa_rot = Rot(2)

        def ld_xn(tc):
            xsl_ = xn_rot.acquire(em, "y")
            for xc in range(8):
                em.dma("y", s["xnld"][:, xsl_ * 4096 + xc * 512: xsl_ * 4096 + (xc + 1) * 512],
                       d["d_xn2"][xc, :, q0 * 128 + tc * 512: q0 * 128 + tc * 512 + 512],
                       xn_rot.sems[xsl_])
            return xsl_, xn_rot.sems[xsl_].v

        pend = ld_xn(0)
        for tc in range(nch):
            xsl, la_x = pend
            if tc + 1 < nch:
                pend = ld_xn(tc + 1)
            em.w("t", xn_rot.sems[xsl], la_x)
            dv_last = 0
            for cq in range(16):
                cqc = wh * 16 + cq
                psl = pa_rot.acquire(em, "t")
                ps = pbig[:, (PA0, PA1)[psl]: (PA0, PA1)[psl] + 512]
                for xc in range(8):
                    em.op("t", lambda t, ps=ps, xsl=xsl, xc=xc, cq=cq: t.matmul(
                        ps, s["wdqh"][:, xc * 2048 + cq * 128: xc * 2048 + cq * 128 + 128],
                        s["xnld"][:, xsl * 4096 + xc * 512: xsl * 4096 + (xc + 1) * 512],
                        start=(xc == 0), stop=(xc == 7)), inc=(S["PE"] if xc == 7 else None))
                pe_c = S["PE"].v
                ssl = st_rot.acquire(em, "v")
                cb = s["cqst"][:, ssl * 512:(ssl + 1) * 512]
                em.w("v", S["PE"], pe_c)
                em.op("v", lambda v, cb=cb, ps=ps: v.tensor_copy(cb, ps), inc=S["DV"])
                dv_c = S["DV"].v
                dv_last = dv_c
                pa_rot.release((S["DV"], dv_c))
                em.w("y", S["DV"], dv_c + 1)
                em.dma("y", d["d_cq"][cqc, :, tc * 512: tc * 512 + 512], cb,
                       st_rot.sems[ssl])
                st_rot.release((st_rot.sems[ssl], st_rot.sems[ssl].v))
            xn_rot.release_slot(xsl, (S["DV"], dv_last))
        em.drain("v", inc=S["DV"])
    st_rot.finish(em, "y")
    xn_rot.finish(em, "y")


def q2_quarter(em, d, s, p, ntt, q0, hq):
    S = em.sem
    pbig = p["big"]
    nch = ntt * 128 // 512
    nt = ntt * 128

    wsem = em.dsem("g")
    em.w("g", S["PE"], S["PE"].v)
    for cqc in range(32):
        em.dma("g", s["wuqh"][:, cqc * 256: cqc * 256 + 256],
               d["wuq"][cqc * 128:(cqc + 1) * 128, hq * 256:(hq + 1) * 256], wsem)
        em.dma("g", s["wqrh"][:, cqc * 128: cqc * 128 + 128],
               d["wqr"][cqc * 128:(cqc + 1) * 128, hq * 128:(hq + 1) * 128], wsem)
    em.w("t", wsem, wsem.v)

    for hl in range(4):
        em.op("v", lambda v, hl=hl: v.memset(
            s["qh"][64:128, hl * NTMAX: hl * NTMAX + nt], 0.0))
    em.drain("v", inc=S["DV"])
    em.w("t", S["DV"], S["DV"].v)

    cq_rot = Rot(4, em)
    ev_rot = Rot(2, em)
    dv_prev_evac = 0
    for tc in range(nch):
        slots = [pbig[:, off: off + 512] for off in (PA0, PA1, PB0)]
        em.w("t", S["DV"], dv_prev_evac)
        pe_acc = 0
        for cqc in range(32):
            csl = cq_rot.acquire(em, "y")
            cl = s["cqld"][:, csl * 512:(csl + 1) * 512]
            em.dma("y", cl, d["d_cq"][cqc, :, tc * 512: tc * 512 + 512],
                   cq_rot.sems[csl])
            la_c = cq_rot.sems[csl].v
            em.w("t", cq_rot.sems[csl], la_c)
            for dt in range(2):
                em.op("t", lambda t, ps=slots[dt], cqc=cqc, dt=dt, cl=cl: t.matmul(
                    ps, s["wuqh"][:, cqc * 256 + dt * 128: cqc * 256 + dt * 128 + 128],
                    cl, start=(cqc == 0), stop=(cqc == 31), skip_group_check=True))
            em.op("t", lambda t, ps=slots[2], cqc=cqc, cl=cl: t.matmul(
                ps, s["wqrh"][:, cqc * 128: cqc * 128 + 128],
                cl, start=(cqc == 0), stop=(cqc == 31), skip_group_check=True),
                inc=S["PE"])
            pe_acc = S["PE"].v
            cq_rot.release((S["PE"], pe_acc))
        em.w("v", S["PE"], pe_acc)
        for dt in range(2):
            esl = ev_rot.acquire(em, "v")
            eb = s["qev"][:, esl * 512:(esl + 1) * 512]
            em.op("v", lambda v, eb=eb, ps=slots[dt]: v.tensor_scalar(
                eb, ps, RS, None, ALU.mult))
            em.drain("v", inc=S["DV"])
            h0 = dt * 2
            em.op("v", lambda v, eb=eb, h0=h0, tc=tc: v.tensor_copy(
                s["qh"][0:64, h0 * NTMAX + tc * 512: h0 * NTMAX + tc * 512 + 512],
                eb[0:64, :]))
            em.drain("v", inc=S["DV"])
            dv_cp = S["DV"].v
            em.w("y", S["DV"], dv_cp)
            em.dma("y", s["qh"][0:64, (h0 + 1) * NTMAX + tc * 512:
                                (h0 + 1) * NTMAX + tc * 512 + 512],
                   eb[64:128, :], ev_rot.sems[esl])
            ev_rot.release((ev_rot.sems[esl], ev_rot.sems[esl].v))
        esl = ev_rot.acquire(em, "v")
        eb = s["qev"][:, esl * 512:(esl + 1) * 512]
        em.op("v", lambda v, eb=eb, ps=slots[2]: v.tensor_scalar(
            eb, ps, RS, None, ALU.mult))
        em.drain("v")
        em.op("v", lambda v, eb=eb, tc=tc: v.tensor_tensor(
            eb, eb, s["rope"][:, q0 * 128 + tc * 512: q0 * 128 + tc * 512 + 512],
            ALU.mult))
        em.drain("v", inc=S["DV"])
        dv_e = S["DV"].v
        dv_prev_evac = dv_e
        em.w("y", S["DV"], dv_e)
        for qq in range(4):
            em.dma("y", s["qh"][64:96, qq * NTMAX + tc * 512: qq * NTMAX + tc * 512 + 512],
                   eb[qq * 32:(qq + 1) * 32, :], ev_rot.sems[esl])
        ev_rot.release((ev_rot.sems[esl], ev_rot.sems[esl].v))
    ev_rot.finish(em, "v")
    cq_rot.finish(em, "y")

    dv_rope = S["DV"].v

    bsem = em.dsem()
    em.w("s", S["DV"], dv_rope)
    for hl in range(4):
        h = hq * 4 + hl
        for tc in range(nch):
            em.w("s", S["DV"], S["DV"].v)
            em.w("s", bsem, bsem.v)
            em.op("s", lambda sc, hl=hl, tc=tc: sc.activation(
                s["q2s"][0:96, 0:512],
                s["qh"][0:96, hl * NTMAX + tc * 512: hl * NTMAX + tc * 512 + 512],
                AF.Square), inc=S["AC"])
            em.drain("s", inc=S["AC"])
            em.w("t", S["AC"], S["AC"].v)
            em.op("t", lambda t: t.matmul(
                pbig[0:1, PM: PM + 512], s["onescol"][0:96, 0:1], s["q2s"][0:96, 0:512],
                start=True, stop=True), inc=S["PE"])
            em.w("s", S["PE"], S["PE"].v)
            em.op("s", lambda sc: sc.activation(
                s["qn"][0:1, 0:512], pbig[0:1, PM: PM + 512], AF.Sqrt))
            em.drain("s", inc=S["AC"])
            em.w("v", S["AC"], S["AC"].v)
            em.op("v", lambda v, h=h: v.tensor_scalar(
                s["brow"][0:1, 0:512], s["qn"][0:1, 0:512],
                s["mk"][0:1, h:h + 1], -1.0, ALU.mult, ALU.mult))
            em.drain("v")
            em.op("v", lambda v: v.tensor_copy(s["browb"][0:1, 0:512],
                                               s["brow"][0:1, 0:512]))
            em.drain("v", inc=S["DV"])
            dv_b = S["DV"].v
            em.w("y", S["DV"], dv_b)
            em.dma("y", s["qh"][96:97, hl * NTMAX + tc * 512: hl * NTMAX + tc * 512 + 512],
                   s["browb"][0:1, 0:512], bsem)
    em.w("v", bsem, bsem.v)
    em.w("t", bsem, bsem.v)


def vhat_quarter(em, d, s, p, nkt, hq):
    S = em.sem
    pbig = p["big"]
    em.op("v", lambda v: v.memset(s["vhat"][:, :], 1.0))
    em.drain("v", inc=S["DV"])
    em.w("t", S["DV"], S["DV"].v)
    pb_rot = Rot(2)
    for kt in range(nkt):
        ysl = pb_rot.acquire(em, "t")
        py = pbig[:, (PA0, PA1)[ysl]: (PA0, PA1)[ysl] + 256]
        for cc in range(2):
            em.op("t", lambda t, py=py, cc=cc, kt=kt, hq=hq: t.matmul(
                py, s["ckv"][:, cc * T + kt * 128: cc * T + (kt + 1) * 128],
                s["wuvb"][:, cc * C + hq * 256: cc * C + hq * 256 + 256],
                start=(cc == 0), stop=(cc == 1)), inc=(S["PE"] if cc == 1 else None))
        pe_v = S["PE"].v
        em.w("v", S["PE"], pe_v)
        for hl in range(4):
            em.op("v", lambda v, py=py, hl=hl, kt=kt: v.tensor_copy(
                s["vhat"][:, kt * 260 + hl * 65: kt * 260 + hl * 65 + 64],
                py[:, hl * 64:(hl + 1) * 64]))
        em.drain("v", inc=S["DV"])
        pb_rot.release((S["DV"], S["DV"].v))


def attn_quarter(em, d, s, p, nkt, ntt, q0, hq):
    S = em.sem
    pbig = p["big"]
    nk = nkt * 128

    secs = []
    t0 = 0
    while t0 < ntt:
        secs.append((t0, min(t0 + 8, ntt)))
        t0 += 8

    kh_rot = Rot(2, em)
    pa_rot = Rot(3)
    pt_rot = Rot(4)
    yev_rot = Rot(2, em)
    kc_rot = Rot(2)
    lsem = em.dsem()
    for hl in range(4):
        h = hq * 4 + hl
        ksl = kh_rot.acquire(em, "v")
        kh = s["khat"][:, ksl * T: ksl * T + T]
        em.op("v", lambda v, kh=kh: v.memset(kh[64:128, 0:nk], 0.0))
        em.drain("v", inc=S["DV"])
        dv_z = S["DV"].v
        em.w("y", S["DV"], dv_z)
        em.dma("y", kh[64:96, 0:nk], s["kr"][:, 0:nk], kh_rot.sems[ksl])
        em.dma("y", kh[96:97, 0:nk], d["onesr"][0:1, 0:nk], kh_rot.sems[ksl])
        la_kr = kh_rot.sems[ksl].v
        for tc in range(nk // 512):
            csl = kc_rot.acquire(em, "t")
            pk = pbig[0:64, PB1 + csl * 512: PB1 + csl * 512 + 512]
            for cc in range(2):
                em.op("t", lambda t, pk=pk, cc=cc, h=h, tc=tc: t.matmul(
                    pk, s["wukb"][:, cc * C + h * 64: cc * C + h * 64 + 64],
                    s["ckv"][:, cc * T + tc * 512: cc * T + tc * 512 + 512],
                    start=(cc == 0), stop=(cc == 1), skip_group_check=True),
                    inc=(S["PE"] if cc == 1 else None))
            pe_kc = S["PE"].v
            em.w("v", S["PE"], pe_kc)
            em.op("v", lambda v, kh=kh, pk=pk, tc=tc: v.tensor_copy(
                kh[0:64, tc * 512: tc * 512 + 512], pk), inc=S["DV"])
            kc_rot.release((S["DV"], S["DV"].v))
        em.drain("v", inc=S["DV"])
        dv_kh = S["DV"].v
        em.w("t", kh_rot.sems[ksl], la_kr)
        em.w("t", S["DV"], dv_kh)

        for (st_, en) in secs:
            seclen = (en - st_) * 128
            pav = pbig[0:65, PB0: PB0 + seclen]
            kmax = q0 + en
            first_kt = True
            for kt in range(kmax):
                qlo = max(st_, kt - q0)
                cols0 = qlo * 128
                ncols = (en - qlo) * 128
                diag = (kt - q0 == qlo and kt >= q0 + st_)
                for ch in range((ncols + 511) // 512):
                    cw = min(512, ncols - ch * 512)
                    co = cols0 + ch * 512
                    psl = pa_rot.acquire(em, "t")
                    ps = pbig[:, (PA0, PA1, PM)[psl]: (PA0, PA1, PM)[psl] + cw]
                    em.op("t", lambda t, ps=ps, kh=kh, kt=kt, hl=hl, co=co, cw=cw: t.matmul(
                        ps, kh[:, kt * 128:(kt + 1) * 128],
                        s["qh"][:, hl * NTMAX + co: hl * NTMAX + co + cw],
                        start=True, stop=True), inc=S["PE"])
                    pe_s = S["PE"].v
                    tsl = pt_rot.acquire(em, "s")
                    ptb = s["pt"][:, tsl * 512: tsl * 512 + cw]
                    em.w("s", S["PE"], pe_s)
                    em.op("s", lambda sc, ptb=ptb, ps=ps: sc.activation(
                        ptb, ps, AF.Exp), inc=S["AC"])
                    em.drain("s", inc=S["AC"])
                    ac_e = S["AC"].v
                    pa_rot.release((S["AC"], ac_e))
                    dv_m = None
                    if diag and ch == 0:
                        em.w("v", S["AC"], ac_e)
                        em.op("v", lambda v, ptb=ptb: v.tensor_mul(
                            ptb[:, 0:128], ptb[:, 0:128], s["dmask"][:, :]))
                        em.drain("v", inc=S["DV"])
                        dv_m = S["DV"].v
                    em.w("t", S["AC"], ac_e)
                    if dv_m is not None:
                        em.w("t", S["DV"], dv_m)
                    off0 = co - st_ * 128
                    pieces = []
                    o_, rel_ = off0, 0
                    while rel_ < cw:
                        w_ = min(cw - rel_, 512 - (o_ % 512))
                        pieces.append((o_, rel_, w_))
                        o_ += w_
                        rel_ += w_
                    for pi, (po, pr, pw) in enumerate(pieces):
                        em.op("t", lambda t, pav=pav, ptb=ptb, kt=kt, hl=hl,
                              po=po, pr=pr, pw=pw, fk=first_kt: t.matmul(
                            pav[:, po: po + pw],
                            s["vhat"][:, kt * 260 + hl * 65: kt * 260 + (hl + 1) * 65],
                            ptb[:, pr: pr + pw],
                            start=fk, stop=False, skip_group_check=True),
                            inc=(S["PE"] if pi == len(pieces) - 1 else None))
                    pt_rot.release((S["PE"], S["PE"].v))
                first_kt = False
            em.drain("t", inc=S["PE"])
            pe_av = S["PE"].v
            em.w("v", S["PE"], pe_av)
            em.op("v", lambda v, pav=pav, seclen=seclen: v.tensor_copy(
                s["lr"][64:65, 0:seclen], pav[64:65, 0:seclen]))
            em.drain("v", inc=S["DV"])
            dv_l = S["DV"].v
            em.w("y", S["DV"], dv_l)
            em.w("y", lsem, lsem.v)
            em.dma("y", s["lr"][0:1, 0:seclen], s["lr"][64:65, 0:seclen], lsem)
            la_l = lsem.v
            em.w("v", lsem, la_l)
            em.op("v", lambda v, seclen=seclen: v.reciprocal(
                s["lr"][0:1, 0:seclen], s["lr"][0:1, 0:seclen]))
            em.drain("v", inc=S["DV"])
            for ch in range((seclen + 511) // 512):
                cw = min(512, seclen - ch * 512)
                esl = yev_rot.acquire(em, "v")
                yb = s["yev"][0:64, esl * 512: esl * 512 + cw]
                em.op("v", lambda v, yb=yb, pav=pav, ch=ch, cw=cw: v.tensor_copy(
                    yb, pav[0:64, ch * 512: ch * 512 + cw]))
                em.drain("v", inc=S["DV"])
                em.w("t", S["DV"], S["DV"].v)
                prl = pbig[0:64, PM: PM + cw]
                em.op("t", lambda t, prl=prl, ch=ch, cw=cw: t.matmul(
                    prl, s["ones1f"][0:1, 0:64], s["lr"][0:1, ch * 512: ch * 512 + cw],
                    start=True, stop=True), inc=S["PE"])
                em.w("v", S["PE"], S["PE"].v)
                em.op("v", lambda v, yb=yb, prl=prl: v.tensor_tensor(
                    yb, yb, prl, ALU.mult))
                em.drain("v", inc=S["DV"])
                dv_y = S["DV"].v
                em.w("y", S["DV"], dv_y)
                em.dma("y", d["d_yn"][h, :, q0 * 128 + st_ * 128 + ch * 512:
                                      q0 * 128 + st_ * 128 + ch * 512 + cw], yb,
                       yev_rot.sems[esl])
                yev_rot.release((yev_rot.sems[esl], yev_rot.sems[esl].v))
            em.w("t", S["DV"], S["DV"].v)
        kh_rot.release((S["PE"], S["PE"].v))
    yev_rot.finish(em, "y")
    kh_rot.finish(em, "y")
    em.w("y", lsem, lsem.v)


def wo_phase(em, d, s, p, ntt, q0):
    S = em.sem
    pbig = p["big"]
    wsem = em.dsem("g")
    for dc in range(8):
        em.dma("g", s["wob"][:, dc * C:(dc + 1) * C],
               d["wo"][dc * 128:(dc + 1) * 128, :], wsem)
    em.w("t", wsem, wsem.v)

    yw_rot = Rot(2, em)
    xt_rot = Rot(4, em)
    y_rot = Rot(2)
    o_rot = Rot(2, em)

    def ld_x(j):
        xsl_ = xt_rot.acquire(em, "y")
        xres_ = s["xt"][:, xsl_ * C:(xsl_ + 1) * C]
        em.dma("y", xres_, d["d_x1"][q0 + j, :, :], xt_rot.sems[xsl_])
        return xsl_, xt_rot.sems[xsl_].v, xres_

    def ld_yw(g):
        j0_ = g * 4
        j1_ = min(j0_ + 4, ntt)
        gw_ = (j1_ - j0_) * 128
        ysl_ = yw_rot.acquire(em, "y")
        for dc in range(8):
            em.dma("y", s["yw"][:, ysl_ * 4096 + dc * 512: ysl_ * 4096 + dc * 512 + gw_],
                   d["d_yn"][2 * dc: 2 * dc + 2, :,
                             q0 * 128 + j0_ * 128: q0 * 128 + j0_ * 128 + gw_]
                   .rearrange("a p c -> (a p) c"),
                   yw_rot.sems[ysl_])
        return ysl_, yw_rot.sems[ysl_].v

    ngr = (ntt + 3) // 4
    pend = ld_x(0)
    pendy = ld_yw(0)
    for g in range(ngr):
        j0 = g * 4
        j1 = min(j0 + 4, ntt)
        ysl, la_y = pendy
        if g + 1 < ngr:
            pendy = ld_yw(g + 1)
        pe_g = 0
        for j in range(j0, j1):
            xsl, la_x, xres = pend
            if j + 1 < ntt:
                pend = ld_x(j + 1)
            psl = y_rot.acquire(em, "t")
            py = pbig[:, PB0 + psl * 1024: PB0 + psl * 1024 + 1024]
            em.w("t", yw_rot.sems[ysl], la_y)
            for half in range(2):
                ph = py[:, half * 512: half * 512 + 512]
                for dc in range(8):
                    em.op("t", lambda t, ph=ph, ysl=ysl, dc=dc, j=j, j0=j0, half=half: t.matmul(
                        ph,
                        s["yw"][:, ysl * 4096 + dc * 512 + (j - j0) * 128:
                                ysl * 4096 + dc * 512 + (j - j0) * 128 + 128],
                        s["wob"][:, dc * C + half * 512: dc * C + half * 512 + 512],
                        start=(dc == 0), stop=(dc == 7)),
                        inc=(S["PE"] if (dc == 7 and half == 1) else None))
            pe_y = S["PE"].v
            pe_g = pe_y
            osl = o_rot.acquire(em, "v")
            xo = s["x1t"][:, osl * C:(osl + 1) * C]
            em.w("v", S["PE"], pe_y)
            em.w("v", xt_rot.sems[xsl], la_x)
            em.op("v", lambda v, xo=xo, py=py, xres=xres: v.tensor_add(
                xo, xres, py), inc=S["DV"])
            dv_x = S["DV"].v
            y_rot.release((S["DV"], dv_x))
            xt_rot.release_slot(xsl, (S["DV"], dv_x))
            em.w("y", S["DV"], dv_x + 1)
            em.dma("y", d["d_x2"][j, :, :], xo, o_rot.sems[osl])
            o_rot.release((o_rot.sems[osl], o_rot.sems[osl].v))
        yw_rot.release_slot(ysl, (S["PE"], pe_g))
    em.drain("v", inc=S["DV"])
    o_rot.finish(em, "y")
    yw_rot.finish(em, "y")
    xt_rot.finish(em, "y")


def build_nc():
    nc = bass.Bass("TRN2", target_bir_lowering=False, debug=False)
    d = {}

    def di(name, shape, dt, kind=None):
        if kind:
            d[name] = nc.dram_tensor(name, shape, dt, kind=kind)
        else:
            d[name] = nc.dram_tensor(name, shape, dt)

    di("x4", [T, C], F32, "ExternalInput")
    di("wdq", [C, 4096], F32, "ExternalInput")
    di("wuq", [4096, C], F32, "ExternalInput")
    di("wdkv", [C, 256], F32, "ExternalInput")
    di("wuk", [256, C], F32, "ExternalInput")
    di("wuv", [256, C], F32, "ExternalInput")
    di("wqr", [4096, 512], F32, "ExternalInput")
    di("wkr", [C, DHR], F32, "ExternalInput")
    di("wo", [C, C], F32, "ExternalInput")
    di("wg", [C, E], F32, "ExternalInput")
    di("bg", [1, E], F32, "ExternalInput")
    di("w1", [E, C, C], F32, "ExternalInput")
    di("b1c", [128, 64], F32, "ExternalInput")
    di("w2", [E, C, C], F32, "ExternalInput")
    di("b2", [E, C], BF16, "ExternalInput")
    di("cidf", [128, 128], F32, "ExternalInput")
    di("cidb", [128, 128], BF16, "ExternalInput")
    di("rope", [128, T], F32, "ExternalInput")
    di("dmask", [128, 128], BF16, "ExternalInput")
    di("cunit", [8, 8 * 128], BF16, "ExternalInput")
    di("onesr", [1, T], BF16, "ExternalInput")
    di("out", [NTMAX, C], F32, "ExternalOutput")
    if DBG_PHASE is not None:
        di("odbg", [128, 8192], F32, "ExternalOutput")

    di("d_h", [E, 128, 8, T], BF16)
    di("d_x1", [32, 128, C], F32)
    di("d_x2", [20, 128, C], F32)
    di("d_xnA", [8, 128, T], BF16)
    di("d_xn2", [8, 128, T], BF16)
    di("d_cq", [32, 128, NTMAX], BF16)
    di("d_yn", [H, HS, T], BF16)
    di("d_yp", [32, 128, C], F32)

    ctx = ExitStack()
    semsA, semsB = {}, {}
    for name in ("PE", "DV", "AC", "GP", "SY"):
        hdl = ctx.enter_context(nc.semaphore(name))
        semsA[name] = Sem(hdl)
        semsB[name] = Sem(hdl)
    dpool = [ctx.enter_context(nc.semaphore(f"D{i}")) for i in range(N_DSEM)]
    gpool = [ctx.enter_context(nc.semaphore(f"G{i}")) for i in range(N_GSEM)]
    emA, emB = Em(semsA, dpool, gpool), Em(semsB, dpool, gpool)
    EMS = ((emA, ROLE[0]), (emB, ROLE[1]))

    s = {}
    p = {}
    p["big"] = ctx.enter_context(nc.psum_tensor("pbig", [128, 3584], F32))
    p["bf"] = ctx.enter_context(nc.psum_tensor("pbf", [128, 1024], BF16))

    _ser = [0]

    def sb(scope, name, shape, dt):
        _ser[0] += 1
        s[name] = scope.enter_context(
            nc.sbuf_tensor(f"sb{_ser[0]}_{name}", shape, dt))

    sb(ctx, "ones1f", [1, 128], F32)
    sb(ctx, "ones512f", [1, 512], F32)
    sb(ctx, "onescol", [128, 1], BF16)
    sb(ctx, "epsc", [128, 1], F32)
    sb(ctx, "cidf", [128, 128], F32)
    sb(ctx, "cidb", [128, 128], BF16)
    sb(ctx, "dmask", [128, 128], BF16)
    sb(ctx, "bg", [1, E], F32)
    sb(ctx, "b1c", [128, 64], F32)
    sb(ctx, "b2b", [E, C], BF16)
    sb(ctx, "cunit", [8, 8 * 128], BF16)
    sb(ctx, "wgc", [128, 8 * E], BF16)
    sb(ctx, "wfm2", [8, T], BF16)
    sb(ctx, "xt", [128, 4 * C], F32)
    sb(ctx, "x1t", [128, 2 * C], F32)
    sb(ctx, "sqs", [128, 2 * C], BF16)
    sb(ctx, "q2s", [128, 512], BF16)
    sb(ctx, "nrm", [1, 512], F32)
    sb(ctx, "mk", [1, 2 * H], F32)
    sb(ctx, "qn", [1, 512], F32)
    sb(ctx, "brow", [1, 512], F32)
    sb(ctx, "browb", [1, 512], BF16)
    sb(ctx, "gsc", [128, E], F32)
    sb(ctx, "gmx", [128, E], F32)
    sb(ctx, "gge", [128, E], F32)
    sb(ctx, "gwv", [128, E], F32)
    sb(ctx, "stat", [128, 6 * 32], F32)

    for em, r in EMS:
        preload(em, d, s)
    if DBG_PHASE == "pre":
        for em, r in EMS:
            em.stop_with_dump(d, [(s["cidf"][:, :], 128, 128),
                                  (s["wgc"][:, 0:64], 128, 64),
                                  (s["b1c"][:, :], 128, 64)])
        return _finish(nc, ctx, EMS)

    def run_ln(ntiles_key, src_key, dst):
        with ExitStack() as sp:
            sb(sp, "xnt", [128, 2 * C], BF16)
            sb(sp, "fmst", [128, 2 * 8 * 512], BF16)
            sl_ = dict(s)
            for em, r in EMS:
                if src_key == "x4":
                    src = lambda i: d["x4"][i * 128:(i + 1) * 128, :]
                elif src_key == "x1":
                    src = lambda i: d["d_x1"][i, :, :]
                else:
                    src = lambda i: d["d_x2"][i, :, :]
                ln_phase(em, d, sl_, p, r[ntiles_key], src, dst)
                em.barrier()

    def run_gate(ntiles_key):
        with ExitStack() as sp:
            sb(sp, "gld", [128, 2 * 1024], BF16)
            sl_ = dict(s)
            for em, r in EMS:
                gate_phase(em, d, sl_, p, r[ntiles_key], d["d_xnA"])
                em.barrier()

    def run_ffn(nkt_key, res_key, dst_key):
        with ExitStack() as sp:
            sb(sp, "w1b", [128, 2 * 8 * C], BF16)
            sb(sp, "w2b", [128, 32 * C], BF16)
            sb(sp, "hrelu", [128, 4 * 512], BF16)
            sb(sp, "hst", [128, 4 * 512], BF16)
            sb(sp, "hld", [128, 2 * 4096], BF16)
            sb(sp, "yt", [128, 2 * C], F32)
            sb(sp, "ypld", [128, 2 * C], F32)
            sb(sp, "xnld", [128, 2 * 8 * 512], BF16)
            sl_ = dict(s)
            for em, r in EMS:
                if res_key == "x4":
                    res = lambda j: d["x4"][j * 128:(j + 1) * 128, :]
                else:
                    res = lambda j: d["d_x2"][j, :, :]
                if dst_key == "x1":
                    dst = lambda j: d["d_x1"][j, :, :]
                else:
                    dst = lambda j: d["out"][j * 128:(j + 1) * 128, :]
                ffn_phase(em, d, sl_, p, r[nkt_key], d["d_xnA"], res, dst)
                em.barrier()

    def dbg_stop(taps_fn=None):
        for em, r in EMS:
            em.stop_with_dump(d, taps_fn() if taps_fn else [])

    run_ln("NKT", "x4", d["d_xnA"])
    if DBG_PHASE == "ln1":
        dbg_stop(lambda: [(d["d_xnA"][0, :, 0:2048], 128, 2048),
                          (d["d_xnA"][1, :, 0:2048], 128, 2048)])
        return _finish(nc, ctx, EMS)
    run_gate("NKT")
    if DBG_PHASE == "gate1":
        dbg_stop(lambda: [(s["wfm2"][0:8, 0:2560], 8, 2560)])
        return _finish(nc, ctx, EMS)
    run_ffn("NKT", "x4", "x1")
    if DBG_PHASE == "ffn1":
        dbg_stop(lambda: [(d["d_x1"][0, :, :], 128, 1024),
                          (d["d_x1"][11, :, :], 128, 1024)])
        return _finish(nc, ctx, EMS)
    run_ln("NKT", "x1", d["d_xn2"])
    if DBG_PHASE == "ln2":
        dbg_stop(lambda: [(d["d_xn2"][2, :, 0:2048], 128, 2048),
                          (d["d_xn2"][2, :, 2048:4096], 128, 2048),
                          (d["d_xn2"][5, :, 0:2048], 128, 2048),
                          (d["d_xn2"][5, :, 2048:4096], 128, 2048)])
        return _finish(nc, ctx, EMS)

    with ExitStack() as sp:
        sb(sp, "rope", [128, T], F32)
        sb(sp, "ckv", [128, 2 * T], BF16)
        sb(sp, "kr", [DHR, T], BF16)
        sb(sp, "wdkvb", [128, 8 * 256], BF16)
        sb(sp, "wukb", [128, 2 * C], BF16)
        sb(sp, "wuvb", [128, 2 * C], BF16)
        sb(sp, "wkrb", [128, 8 * DHR], BF16)
        with ExitStack() as spk:
            sb(spk, "xnld", [128, 2 * 8 * 512], BF16)
            sb(spk, "fro", [1, C], F32)
            sk_ = dict(s)
            for em, r in EMS:
                kv_phase(em, d, sk_, p, r["NKT"])
                em.barrier()
        if DBG_PHASE == "kv":
            dbg_stop(lambda: [(s["ckv"][:, 0:2048], 128, 2048),
                              (s["kr"][0:DHR, 0:1024], DHR, 1024),
                              (s["mk"][0:1, 0:2 * H], 1, 2 * H),
                              (d["d_xn2"][0, :, 0:2048], 128, 2048),
                              (d["d_xn2"][3, :, 0:2048], 128, 2048)])
            sp.close()
            return _finish(nc, ctx, EMS)

        with ExitStack() as spq:
            sb(spq, "wdqh", [128, 8 * 2048], BF16)
            sb(spq, "xnld", [128, 2 * 8 * 512], BF16)
            sb(spq, "cqst", [128, 2 * 512], BF16)
            sq_ = dict(s)
            for em, r in EMS:
                q1_phase(em, d, sq_, p, r["NTT"], r["Q0"])
                em.barrier()
        if DBG_PHASE == "q1":
            dbg_stop(lambda: [(d["d_cq"][0, :, 0:2048], 128, 2048),
                              (d["d_cq"][31, :, 0:2048], 128, 2048)])
            sp.close()
            return _finish(nc, ctx, EMS)

        with ExitStack() as spq:
            sb(spq, "wuqh", [128, 32 * 256], BF16)
            sb(spq, "wqrh", [128, 32 * 128], BF16)
            sb(spq, "cqld", [128, 4 * 512], BF16)
            sb(spq, "qev", [128, 2 * 512], BF16)
            sb(spq, "qh", [128, 4 * NTMAX], BF16)
            sb(spq, "vhat", [128, 32 * 4 * 65], BF16)
            sb(spq, "khat", [128, 2 * T], BF16)
            sb(spq, "pt", [128, 4 * 512], BF16)
            sb(spq, "lr", [128, 1024], F32)
            sb(spq, "yev", [64, 2 * 512], BF16)
            sa_ = dict(s)
            for em, r in EMS:
                nq = 1 if DBG_PHASE in ("q2", "vhat", "attn") else 4
                for hq in range(nq):
                    q2_quarter(em, d, sa_, p, r["NTT"], r["Q0"], hq)
                    em.barrier()
                    if DBG_PHASE == "q2":
                        em.stop_with_dump(d, [(s["qh"][:, 0:2048], 128, 2048),
                                              (s["qh"][:, NTMAX:NTMAX + 2048], 128, 2048)])
                        break
                    vhat_quarter(em, d, sa_, p, r["NKT"], hq)
                    em.barrier()
                    if DBG_PHASE == "vhat":
                        em.stop_with_dump(d, [(s["vhat"][:, 0:2080], 128, 2080)])
                        break
                    attn_quarter(em, d, sa_, p, r["NKT"], r["NTT"], r["Q0"], hq)
                    em.barrier()
                    if DBG_PHASE == "attn":
                        nt_, o_ = r["NTT"] * 128, r["Q0"] * 128
                        em.stop_with_dump(
                            d, [(d["d_yn"][0, :, o_:o_ + nt_], HS, nt_),
                                (d["d_yn"][3, :, o_:o_ + nt_], HS, nt_)])
                        break
        if DBG_PHASE in ("q2", "vhat", "attn"):
            sp.close()
            return _finish(nc, ctx, EMS)

    with ExitStack() as sp:
        sb(sp, "wob", [128, 8 * C], BF16)
        sb(sp, "yw", [128, 2 * 4096], BF16)
        sw_ = dict(s)
        for em, r in EMS:
            wo_phase(em, d, sw_, p, r["NTT"], r["Q0"])
            em.barrier()
    if DBG_PHASE == "wo":
        dbg_stop(lambda: [(d["d_x2"][0, :, :], 128, 1024),
                          (d["d_x2"][11, :, :], 128, 1024)])
        return _finish(nc, ctx, EMS)
    run_ln("NTT", "x2", d["d_xnA"])
    run_gate("NTT")
    run_ffn("NTT", "x2", "out")
    for em, r in EMS:
        for sm in em.dsems:
            em.w("y", sm, sm.v)

    return _finish(nc, ctx, EMS)


def _finish(nc, ctx, EMS):
    emA, emB = EMS[0][0], EMS[1][0]
    blk = ctx.enter_context(nc.Block())

    def body(eng, key):
        pid = eng.partition_id()
        with eng.register(f"role_{key}") as r:
            eng.reg_mov(r, pid)
            eng.reg_mod(r, r, 2)
            with eng.If_eq(r, 0):
                for fn in emA.q[key]:
                    fn(eng)
            with eng.Else():
                for fn in emB.q[key]:
                    fn(eng)

    @blk.tensor
    def _(t):
        body(t, "t")

    @blk.vector
    def _(v):
        body(v, "v")

    @blk.scalar
    def _(sc):
        body(sc, "s")

    @blk.gpsimd
    def _(g):
        body(g, "g")

    @blk.sync
    def _(y):
        body(y, "y")

    ctx.close()
    return nc


def make_consts():
    t = np.arange(T, dtype=np.float32)
    inv = 1.0 / (10000.0 ** (np.arange(0, DHR, 2, dtype=np.float32) / DHR))
    freqs = t[:, None] * inv[None, :]
    emb = np.concatenate([freqs, freqs], axis=-1)
    rope = np.ascontiguousarray(np.tile(
        (np.cos(emb) + np.sin(emb)).astype(np.float32).T, (4, 1)))
    ki, qi = np.meshgrid(np.arange(128), np.arange(128), indexing="ij")
    dmask = (qi >= ki).astype(ml_dtypes.bfloat16)
    identf = np.eye(128, dtype=np.float32)
    identb = np.eye(128).astype(ml_dtypes.bfloat16)
    cunit = np.zeros((8, 8 * 128), dtype=ml_dtypes.bfloat16)
    for e in range(8):
        cunit[e, e * 128:(e + 1) * 128] = 1.0
    onesr = np.ones((1, T), dtype=ml_dtypes.bfloat16)
    return rope, dmask, identf, identb, cunit, onesr


_NC_CACHE = {}


def kernel(L0_attn_x, L0_attn_W_DQ, L0_attn_W_UQ, L0_attn_W_DKV, L0_attn_W_UK,
           L0_attn_W_UV, L0_attn_W_QR, L0_attn_W_KR, L0_attn_W_O,
           L0_ffn_W_gate, L0_ffn_b_gate, L0_ffn_W1, L0_ffn_b1, L0_ffn_W2,
           L0_ffn_b2):
    key = ("nc", DBG_PHASE)
    if key not in _NC_CACHE:
        _NC_CACHE[key] = build_nc()
    nc = _NC_CACHE[key]
    rope, dmask, identf, identb, cunit, onesr = make_consts()

    x = np.ascontiguousarray(np.asarray(L0_attn_x, dtype=np.float32))
    b1 = np.asarray(L0_ffn_b1, np.float32).reshape(E * 8, 128).T.copy()
    common = dict(
        wdq=np.asarray(L0_attn_W_DQ, np.float32),
        wuq=np.asarray(L0_attn_W_UQ, np.float32),
        wdkv=np.asarray(L0_attn_W_DKV, np.float32),
        wuk=np.asarray(L0_attn_W_UK, np.float32),
        wuv=np.asarray(L0_attn_W_UV, np.float32),
        wqr=np.asarray(L0_attn_W_QR, np.float32),
        wkr=np.asarray(L0_attn_W_KR, np.float32),
        wo=np.asarray(L0_attn_W_O, np.float32),
        wg=np.asarray(L0_ffn_W_gate, np.float32),
        bg=np.asarray(L0_ffn_b_gate, np.float32).reshape(1, E),
        w1=np.asarray(L0_ffn_W1, np.float32),
        b1c=b1,
        w2=np.asarray(L0_ffn_W2, np.float32),
        b2=np.asarray(L0_ffn_b2, np.float32).astype(ml_dtypes.bfloat16),
        cidf=identf, cidb=identb, rope=rope, dmask=dmask, cunit=cunit,
        onesr=onesr,
    )
    in_maps = []
    for c in range(8):
        m = dict(common)
        m["x4"] = x[c // 2]
        in_maps.append(m)

    res = run_bass_kernel_spmd(nc, in_maps, core_ids=list(range(8)))
    out = np.empty((NB, T, C), dtype=np.float32)
    for b in range(NB):
        out[b, 0:SPLIT, :] = res.results[2 * b]["out"]
        out[b, SPLIT:T, :] = res.results[2 * b + 1]["out"][0: T - SPLIT]
    if DBG_PHASE is not None:
        kernel.dbg = [np.asarray(r["odbg"]) for r in res.results if "odbg" in r]
    return out


# version 05 (final)
# speedup vs baseline: 1.0353x; 1.0202x over previous
import math
import numpy as np
import ml_dtypes
import concourse.bass as bass
import concourse.mybir as mybir
from concourse.bass_utils import run_bass_kernel_spmd
from contextlib import ExitStack

F32 = mybir.dt.float32
BF16 = mybir.dt.bfloat16
AF = mybir.ActivationFunctionType
ALU = mybir.AluOpType
AX = mybir.AxisListType

NB, T, C = 4, 4096, 1024
H, HS, DHR = 16, 64, 32
E = 8
SPLIT = 2560
NTMAX = SPLIT
RS = 1.0 / math.sqrt(HS + DHR)

DBG_PHASE = None
N_DSEM = 40
N_GSEM = 8


class Sem:
    def __init__(self, handle):
        self.h = handle
        self.v = 0


class Em:
    ENGS = ("t", "v", "s", "g", "y")

    def __init__(self, sems, dpool, gpool):
        self.q = {k: [] for k in self.ENGS}
        self.sem = sems
        self.dsems = [Sem(h) for h in dpool]
        self.gsems = [Sem(h) for h in gpool]
        self.dnext = 0
        self.gnext = 0
        self.stopped = False

    def dsem(self, qk="y"):
        if qk == "g":
            sm = self.gsems[self.gnext % len(self.gsems)]
            self.gnext += 1
            return sm
        sm = self.dsems[self.dnext % len(self.dsems)]
        self.dnext += 1
        return sm

    def w(self, e, sem, val=None):
        if self.stopped:
            return
        v = sem.v if val is None else val
        if v <= 0:
            return
        self.q[e].append(lambda eng, s=sem.h, v=v: eng.wait_ge(s, v))

    def op(self, e, fn, inc=None, n=1):
        if self.stopped:
            return None
        if inc is None:
            self.q[e].append(lambda eng, fn=fn: fn(eng))
            return None
        inc.v += n
        self.q[e].append(lambda eng, fn=fn, s=inc.h, n=n: fn(eng).then_inc(s, n))
        return inc.v

    def drain(self, e, inc=None, n=1):
        if self.stopped:
            return None
        if inc is None:
            self.q[e].append(lambda eng: eng.drain())
            return None
        inc.v += n
        self.q[e].append(lambda eng, s=inc.h, n=n: eng.drain().then_inc(s, n))
        return inc.v

    def dma(self, e, dst, src, sem):
        if self.stopped:
            return None
        sem.v += 16
        self.q[e].append(
            lambda eng, dst=dst, src=src, s=sem.h: eng.dma_start(dst, src).then_inc(s, 16))
        return sem.v

    def barrier(self):
        if self.stopped:
            return
        S = self.sem
        self.drain("t", inc=S["PE"])
        self.drain("v", inc=S["DV"])
        self.drain("s", inc=S["AC"])
        self.drain("g", inc=S["GP"])
        self.drain("y", inc=S["SY"])
        toks = [(S[k], S[k].v) for k in ("PE", "DV", "AC", "GP", "SY")]
        for e in self.ENGS:
            for sem, v in toks:
                self.w(e, sem, v)

    def stop_with_dump(self, d, taps):
        self.barrier()
        sm = self.dsem("g")
        col = 0
        for ap, rows, ncols in taps:
            self.dma("g", d["odbg"][0:rows, col:col + ncols], ap, sm)
            col += ncols
        self.w("g", sm, sm.v)
        self.stopped = True


class Rot:

    def __init__(self, n, em=None, qk="y"):
        self.n = n
        self.rel = [[] for _ in range(n)]
        self.i = 0
        self.sems = [em.dsem(qk) for _ in range(n)] if em is not None else None

    def acquire(self, em, engine):
        sl = self.i % self.n
        for sem, v in self.rel[sl]:
            em.w(engine, sem, v)
        self.rel[sl] = []
        self.i += 1
        return sl

    def release(self, *tokens):
        sl = (self.i - 1) % self.n
        self.rel[sl].extend(tokens)

    def release_slot(self, sl, *tokens):
        self.rel[sl].extend(tokens)

    def finish(self, em, engine):
        if self.sems:
            for sm in self.sems:
                em.w(engine, sm, sm.v)


PA0, PA1 = 0, 512
PB0, PB1 = 1024, 2048
PM = 3072

ROLE = [dict(NKT=20, NTT=20, Q0=0), dict(NKT=32, NTT=12, Q0=20)]


def preload(em, d, s):
    S = em.sem
    em.op("v", lambda v: v.memset(s["ones1f"][:, :], 1.0))
    em.op("v", lambda v: v.memset(s["ones512f"][:, :], 1.0))
    em.op("v", lambda v: v.memset(s["onescol"][:, :], 1.0))
    em.op("v", lambda v: v.memset(s["epsc"][:, :], 1e-5))
    em.drain("v", inc=S["DV"])
    sm = em.dsem("g")
    em.dma("g", s["cidf"][:, :], d["cidf"][:, :], sm)
    em.dma("g", s["cidb"][:, :], d["cidb"][:, :], sm)
    em.dma("g", s["dmask"][:, :], d["dmask"][:, :], sm)
    em.dma("g", s["bg"][:, :], d["bg"][:, :], sm)
    em.dma("g", s["b1c"][:, :], d["b1c"][:, :], sm)
    em.dma("g", s["b2b"][:, :], d["b2"][:, :], sm)
    em.dma("g", s["cunit"][:, :], d["cunit"][:, :], sm)
    for xc in range(8):
        em.dma("g", s["wgc"][:, xc * 8:(xc + 1) * 8],
               d["wg"][xc * 128:(xc + 1) * 128, :], sm)
    em.w("g", sm, sm.v)
    em.barrier()


def ln_phase(em, d, s, p, ntiles, src_fn, dst_dram):
    S = em.sem
    st = s["stat"]
    pbf = p["bf"]

    em.op("v", lambda v: v.memset(st[:, :], 0.0))
    em.drain("v", inc=S["DV"])
    dv_zero = S["DV"].v

    xt_rot = Rot(4, em)
    sq_rot = Rot(2)
    for i in range(ntiles):
        sl = xt_rot.acquire(em, "y")
        xs = s["xt"][:, sl * C:(sl + 1) * C]
        em.dma("y", xs, src_fn(i), xt_rot.sems[sl])
        lv = xt_rot.sems[sl].v
        em.w("s", xt_rot.sems[sl], lv)
        em.w("s", S["DV"], dv_zero)
        qsl = sq_rot.acquire(em, "s")
        em.op("s", lambda sc, xs=xs, i=i, qsl=qsl: sc.activation(
            s["sqs"][:, qsl * C:(qsl + 1) * C], xs, AF.Square,
            accum_out=st[:, 32 + i:33 + i]), inc=S["AC"])
        ac = S["AC"].v
        sq_rot.release((S["AC"], ac))
        em.w("v", xt_rot.sems[sl], lv)
        em.op("v", lambda v, xs=xs, i=i: v.reduce_sum(st[:, i:i + 1], xs, axis=AX.X),
              inc=S["DV"])
        xt_rot.release((S["AC"], ac), (S["DV"], S["DV"].v))
    em.drain("s", inc=S["AC"])
    em.drain("v", inc=S["DV"])
    ac_stats = S["AC"].v

    n = ntiles
    em.w("v", S["AC"], ac_stats)
    em.op("v", lambda v: v.tensor_scalar(st[:, 64:64 + n], st[:, 0:n], 1.0 / C, None, ALU.mult))
    em.drain("v")
    em.op("v", lambda v: v.tensor_scalar(st[:, 96:96 + n], st[:, 32:32 + n], 1.0 / C, None, ALU.mult))
    em.drain("v")
    em.op("v", lambda v: v.tensor_mul(st[:, 160:160 + n], st[:, 64:64 + n], st[:, 64:64 + n]))
    em.drain("v")
    em.op("v", lambda v: v.tensor_sub(st[:, 96:96 + n], st[:, 96:96 + n], st[:, 160:160 + n]))
    em.drain("v", inc=S["DV"])
    em.w("s", S["DV"], S["DV"].v)
    em.op("s", lambda sc: sc.activation(st[:, 160:160 + n], st[:, 96:96 + n], AF.Ln,
                                        bias=s["epsc"][:, :]))
    em.drain("s")
    em.op("s", lambda sc: sc.activation(st[:, 128:128 + n], st[:, 160:160 + n], AF.Exp,
                                        scale=-0.5))
    em.drain("s", inc=S["AC"])
    ac_rstd = S["AC"].v
    em.w("v", S["AC"], ac_rstd)

    xnt_rot = Rot(2)
    pbf_rot = Rot(1)
    fm_rot = Rot(2, em)
    fsl = 0
    for i in range(ntiles):
        sl = xt_rot.acquire(em, "y")
        xs = s["xt"][:, sl * C:(sl + 1) * C]
        em.dma("y", xs, src_fn(i), xt_rot.sems[sl])
        lv = xt_rot.sems[sl].v
        nsl = xnt_rot.acquire(em, "v")
        xnt = s["xnt"][:, nsl * C:(nsl + 1) * C]
        em.w("v", xt_rot.sems[sl], lv)
        em.op("v", lambda v, xs=xs, xnt=xnt, i=i: v.tensor_scalar(
            xnt, xs, st[:, 64 + i:65 + i], st[:, 128 + i:129 + i], ALU.subtract, ALU.mult))
        em.drain("v", inc=S["DV"])
        dv_xnt = S["DV"].v
        xt_rot.release((S["DV"], dv_xnt))

        pbf_rot.acquire(em, "t")
        em.w("t", S["DV"], dv_xnt)
        for xc in range(8):
            em.op("t", lambda t, xnt=xnt, xc=xc: t.transpose(
                pbf[:, xc * 128:(xc + 1) * 128], xnt[:, xc * 128:(xc + 1) * 128],
                s["cidb"][:, :]))
        em.drain("t", inc=S["PE"])
        pe_tr = S["PE"].v
        xnt_rot.release((S["PE"], pe_tr))
        em.w("v", S["PE"], pe_tr)
        if i % 4 == 0:
            fsl = fm_rot.acquire(em, "v")
        for xc in range(8):
            em.op("v", lambda v, xc=xc, i=i, fsl=fsl: v.tensor_copy(
                s["fmst"][:, fsl * 4096 + xc * 512 + (i % 4) * 128:
                          fsl * 4096 + xc * 512 + (i % 4) * 128 + 128],
                pbf[:, xc * 128:(xc + 1) * 128]))
        em.drain("v", inc=S["DV"])
        pbf_rot.release((S["DV"], S["DV"].v))
        if i % 4 == 3 or i == ntiles - 1:
            g0 = (i // 4) * 4
            ncb = (i - g0 + 1) * 128
            em.w("y", S["DV"], S["DV"].v)
            for xc in range(8):
                em.dma("y", dst_dram[xc, :, g0 * 128: g0 * 128 + ncb],
                       s["fmst"][:, fsl * 4096 + xc * 512: fsl * 4096 + xc * 512 + ncb],
                       fm_rot.sems[fsl])
            fm_rot.release((fm_rot.sems[fsl], fm_rot.sems[fsl].v))
    xt_rot.finish(em, "y")
    fm_rot.finish(em, "y")


def gate_phase(em, d, s, p, ntiles, xn_dram):
    S = em.sem
    pbig = p["big"]
    gld_rot = Rot(2, em)
    sc_rot = Rot(2)
    tr_rot = Rot(2)
    for i in range(ntiles):
        lsl = gld_rot.acquire(em, "y")
        gl = s["gld"][:, lsl * 1024:(lsl + 1) * 1024]
        for xc in range(8):
            em.dma("y", gl[:, xc * 128:(xc + 1) * 128],
                   xn_dram[xc, :, i * 128:(i + 1) * 128], gld_rot.sems[lsl])
        lv = gld_rot.sems[lsl].v
        ssl = sc_rot.acquire(em, "t")
        ps = pbig[:, (PA0, PA1)[ssl]: (PA0, PA1)[ssl] + 8]
        em.w("t", gld_rot.sems[lsl], lv)
        for xc in range(8):
            em.op("t", lambda t, ps=ps, gl=gl, xc=xc: t.matmul(
                ps, gl[:, xc * 128:(xc + 1) * 128],
                s["wgc"][:, xc * 8:(xc + 1) * 8], start=(xc == 0), stop=False))
        em.op("t", lambda t, ps=ps: t.matmul(
            ps, s["ones1f"][:, :], s["bg"][:, :], start=False, stop=True), inc=S["PE"])
        pe_sc = S["PE"].v
        gld_rot.release((S["PE"], pe_sc))
        em.w("v", S["PE"], pe_sc)
        em.op("v", lambda v, ps=ps: v.tensor_copy(s["gsc"][:, :], ps))
        em.drain("v")
        em.op("v", lambda v: v.max(out=s["gmx"][:, :], in_=s["gsc"][:, :]))
        em.drain("v")
        em.op("v", lambda v: v.tensor_scalar(
            s["gge"][:, :], s["gsc"][:, :], s["gmx"][:, 1:2], None, ALU.is_ge))
        em.drain("v")
        em.op("v", lambda v: v.tensor_mul(s["gwv"][:, :], s["gsc"][:, :], s["gge"][:, :]))
        em.drain("v", inc=S["DV"])
        dv_w = S["DV"].v
        sc_rot.release((S["DV"], dv_w))

        tsl = tr_rot.acquire(em, "t")
        em.w("t", S["DV"], dv_w)
        p8 = pbig[0:8, (PB0, PB1)[tsl]: (PB0, PB1)[tsl] + 128]
        em.op("t", lambda t, p8=p8: t.matmul(
            p8, s["gwv"][:, :], s["cidf"][:, :], start=True, stop=True), inc=S["PE"])
        pe_tr = S["PE"].v
        em.w("v", S["PE"], pe_tr)
        em.op("v", lambda v, p8=p8, i=i: v.tensor_copy(
            s["wfm2"][0:8, i * 128:(i + 1) * 128], p8))
        em.drain("v", inc=S["DV"])
        tr_rot.release((S["DV"], S["DV"].v))
    gld_rot.finish(em, "y")


def ffn_phase(em, d, s, p, nkt, xn_dram, res_fn, dst_fn):
    S = em.sem
    pbig = p["big"]
    nch = nkt * 128 // 512

    w1_rot = Rot(2, em, "g")
    xn_rot = Rot(2, em)
    wbc_rot = Rot(2)
    h_rot = Rot(2)
    relu_rot = Rot(4)
    hst_rot = Rot(4, em)
    w2sem = [em.dsem("g"), em.dsem("g")]
    pe0 = S["PE"].v
    w1info = []
    for e in range(E):
        wsl = w1_rot.acquire(em, "g")
        for xc in range(8):
            em.dma("g", s["w1b"][:, wsl * 8 * C + xc * C: wsl * 8 * C + (xc + 1) * C],
                   d["w1"][e, xc * 128:(xc + 1) * 128, :], w1_rot.sems[wsl])
        w1info.append((wsl, w1_rot.sems[wsl].v))
        w1_rot.release((S["PE"], pe0 + (e + 1) * nch * 9))
        if e < 4:
            for hc in range(8):
                em.dma("g", s["w2b"][:, (e * 8 + hc) * C:(e * 8 + hc + 1) * C],
                       d["w2"][e, hc * 128:(hc + 1) * 128, :], w2sem[0])

    def ld_xn(tc):
        xsl_ = xn_rot.acquire(em, "y")
        for xc in range(8):
            em.dma("y", s["xnld"][:, xsl_ * 4096 + xc * 512: xsl_ * 4096 + (xc + 1) * 512],
                   xn_dram[xc, :, tc * 512: tc * 512 + 512], xn_rot.sems[xsl_])
        return xsl_, xn_rot.sems[xsl_].v

    pend = ld_xn(0)
    for idx in range(E * nch):
        e, tc = divmod(idx, nch)
        xsl, la_x = pend
        if idx + 1 < E * nch:
            pend = ld_xn((idx + 1) % nch)
        wsl, lw_e = w1info[e]
        if tc == 0:
            em.w("t", w1_rot.sems[wsl], lw_e)
        if True:
            bsl = wbc_rot.acquire(em, "t")
            pw = pbig[:, PB0 + bsl * 512: PB0 + bsl * 512 + 512]
            em.op("t", lambda t, pw=pw, e=e, tc=tc: t.matmul(
                pw, s["cunit"][0:8, e * 128:(e + 1) * 128],
                s["wfm2"][0:8, tc * 512: tc * 512 + 512],
                start=True, stop=True), inc=S["PE"])
            pe_wbc = S["PE"].v
            em.w("t", xn_rot.sems[xsl], la_x)
            for ht in range(8):
                hsl = h_rot.acquire(em, "t")
                ph = pbig[:, PA0 + hsl * 512: PA0 + hsl * 512 + 512]
                for xc in range(8):
                    em.op("t", lambda t, ph=ph, wsl=wsl, xsl=xsl, xc=xc, ht=ht: t.matmul(
                        ph,
                        s["w1b"][:, wsl * 8 * C + xc * C + ht * 128:
                                 wsl * 8 * C + xc * C + ht * 128 + 128],
                        s["xnld"][:, xsl * 4096 + xc * 512: xsl * 4096 + (xc + 1) * 512],
                        start=(xc == 0), stop=(xc == 7)),
                        inc=(S["PE"] if xc == 7 else None))
                pe_h = S["PE"].v
                pe_last = pe_h
                rsl = relu_rot.acquire(em, "s")
                rb = s["hrelu"][:, rsl * 512:(rsl + 1) * 512]
                em.w("s", S["PE"], pe_h)
                em.op("s", lambda sc, rb=rb, ph=ph, e=e, ht=ht: sc.activation(
                    rb, ph, AF.Relu, bias=s["b1c"][:, e * 8 + ht: e * 8 + ht + 1]),
                    inc=S["AC"])
                ac_r = S["AC"].v
                h_rot.release((S["AC"], ac_r))
                osl = hst_rot.acquire(em, "v")
                ob = s["hst"][:, osl * 512:(osl + 1) * 512]
                em.w("v", S["AC"], ac_r + 1)
                em.w("v", S["PE"], pe_wbc)
                em.op("v", lambda v, ob=ob, rb=rb, pw=pw: v.tensor_tensor(
                    ob, rb, pw, ALU.mult), inc=S["DV"])
                dv_h = S["DV"].v
                relu_rot.release((S["DV"], dv_h))
                if ht == 7:
                    wbc_rot.release((S["DV"], dv_h))
                    xn_rot.release_slot(xsl, (S["PE"], pe_h))
                em.w("y", S["DV"], dv_h + 1)
                em.dma("y", d["d_h"][e, :, ht, tc * 512: tc * 512 + 512], ob,
                       hst_rot.sems[osl])
                hst_rot.release((hst_rot.sems[osl], hst_rot.sems[osl].v))
    assert S["PE"].v == pe0 + E * nch * 9
    em.drain("s", inc=S["AC"])
    em.drain("v", inc=S["DV"])
    dv_w1_done = S["DV"].v

    for pa in range(2):
        es = list(range(pa * 4, pa * 4 + 4))
        if pa == 1:
            em.w("g", S["PE"], S["PE"].v)
            for e in es:
                for hc in range(8):
                    em.dma("g", s["w2b"][:, ((e - 4) * 8 + hc) * C:
                                         ((e - 4) * 8 + hc + 1) * C],
                           d["w2"][e, hc * 128:(hc + 1) * 128, :], w2sem[1])
        em.w("t", w2sem[pa], w2sem[pa].v)
        hld_rot = Rot(2, em)
        xt_rot = Rot(2, em)
        yp_rot = Rot(2, em)
        y_rot = Rot(2)
        o_rot = Rot(2, em)
        hst_rot.finish(em, "y")

        def ld_j(j):
            lsl_ = hld_rot.acquire(em, "y")
            for ei, e in enumerate(es):
                em.dma("y",
                       s["hld"][:, lsl_ * 4096 + ei * 1024: lsl_ * 4096 + (ei + 1) * 1024]
                       .rearrange("p (a c) -> p a c", a=8),
                       d["d_h"][e, :, :, j * 128:(j + 1) * 128],
                       hld_rot.sems[lsl_])
            if pa == 1:
                xsl_ = xt_rot.acquire(em, "y")
                xres_ = s["xt"][:, xsl_ * C:(xsl_ + 1) * C]
                em.dma("y", xres_, res_fn(j), xt_rot.sems[xsl_])
                psl_ = yp_rot.acquire(em, "y")
                ypl_ = s["ypld"][:, psl_ * C:(psl_ + 1) * C]
                em.dma("y", ypl_, d["d_yp"][j, :, :], yp_rot.sems[psl_])
                return (lsl_, hld_rot.sems[lsl_].v, xsl_, xt_rot.sems[xsl_].v,
                        xres_, psl_, yp_rot.sems[psl_].v, ypl_)
            return (lsl_, hld_rot.sems[lsl_].v, 0, 0, None, 0, 0, None)

        pend = ld_j(0)
        for j in range(nkt):
            lsl, la_h, xsl, la_x, xres, psl, la_yp, ypl = pend
            if j + 1 < nkt:
                pend = ld_j(j + 1)
            ysl = y_rot.acquire(em, "t")
            py = pbig[:, PB0 + ysl * 1024: PB0 + ysl * 1024 + 1024]
            em.w("t", hld_rot.sems[lsl], la_h)
            em.w("t", S["DV"], dv_w1_done)
            for half in range(2):
                ph = py[:, half * 512: half * 512 + 512]
                k = 0
                for ei, e in enumerate(es):
                    for hc in range(8):
                        last = (ei == 3 and hc == 7)
                        em.op("t", lambda t, ph=ph, lsl=lsl, ei=ei, hc=hc,
                              half=half, k=k, last=last, pa=pa: t.matmul(
                            ph,
                            s["hld"][:, lsl * 4096 + ei * 1024 + hc * 128:
                                     lsl * 4096 + ei * 1024 + hc * 128 + 128],
                            s["w2b"][:, (ei * 8 + hc) * C + half * 512:
                                     (ei * 8 + hc) * C + half * 512 + 512],
                            start=(k == 0), stop=(last and pa == 0)),
                            inc=(S["PE"] if (last and pa == 0 and half == 1) else None))
                        k += 1
                if pa == 1:
                    em.op("t", lambda t, ph=ph, j=j, half=half: t.matmul(
                        ph, s["wfm2"][0:8, j * 128:(j + 1) * 128],
                        s["b2b"][0:8, half * 512: half * 512 + 512],
                        start=False, stop=True),
                        inc=(S["PE"] if half == 1 else None))
            pe_y = S["PE"].v
            hld_rot.release_slot(lsl, (S["PE"], pe_y))
            if pa == 0:
                osl = o_rot.acquire(em, "v")
                yt = s["yt"][:, osl * C:(osl + 1) * C]
                em.w("v", S["PE"], pe_y)
                em.op("v", lambda v, yt=yt, py=py: v.tensor_copy(yt, py), inc=S["DV"])
                dv_y = S["DV"].v
                y_rot.release((S["DV"], dv_y))
                em.w("y", S["DV"], dv_y + 1)
                em.dma("y", d["d_yp"][j, :, :], yt, o_rot.sems[osl])
                o_rot.release((o_rot.sems[osl], o_rot.sems[osl].v))
            else:
                osl = o_rot.acquire(em, "v")
                xo = s["x1t"][:, osl * C:(osl + 1) * C]
                em.w("v", S["PE"], pe_y)
                em.w("v", xt_rot.sems[xsl], la_x)
                em.w("v", yp_rot.sems[psl], la_yp)
                em.op("v", lambda v, py=py, ypl=ypl: v.tensor_add(ypl, ypl, py))
                em.drain("v")
                em.op("v", lambda v, xo=xo, ypl=ypl, xres=xres: v.scalar_tensor_tensor(
                    xo, ypl, 0.5, xres, ALU.mult, ALU.add), inc=S["DV"])
                dv_x = S["DV"].v
                y_rot.release((S["DV"], dv_x))
                xt_rot.release_slot(xsl, (S["DV"], dv_x))
                yp_rot.release_slot(psl, (S["DV"], dv_x))
                em.w("y", S["DV"], dv_x + 1)
                em.dma("y", dst_fn(j), xo, o_rot.sems[osl])
                o_rot.release((o_rot.sems[osl], o_rot.sems[osl].v))
        em.drain("v", inc=S["DV"])
        o_rot.finish(em, "y")
        hld_rot.finish(em, "y")
        if pa == 1:
            xt_rot.finish(em, "y")
            yp_rot.finish(em, "y")
    xn_rot.finish(em, "y")
    w1_rot.finish(em, "g")
    em.w("g", w2sem[0], w2sem[0].v)
    em.w("g", w2sem[1], w2sem[1].v)


def kv_phase(em, d, s, p, nkt):
    S = em.sem
    pbig = p["big"]
    nch = nkt * 128 // 512

    wsem = em.dsem("g")
    em.dma("g", s["rope"][:, :], d["rope"][:, :], wsem)
    for xc in range(8):
        em.dma("g", s["wdkvb"][:, xc * 256:(xc + 1) * 256],
               d["wdkv"][xc * 128:(xc + 1) * 128, :], wsem)
        em.dma("g", s["wkrb"][:, xc * DHR:(xc + 1) * DHR],
               d["wkr"][xc * 128:(xc + 1) * 128, :], wsem)
    for cc in range(2):
        em.dma("g", s["wukb"][:, cc * C:(cc + 1) * C],
               d["wuk"][cc * 128:(cc + 1) * 128, :], wsem)
        em.dma("g", s["wuvb"][:, cc * C:(cc + 1) * C],
               d["wuv"][cc * 128:(cc + 1) * 128, :], wsem)
    em.w("t", wsem, wsem.v)
    em.w("v", wsem, wsem.v)
    em.w("s", wsem, wsem.v)

    xn_rot = Rot(2, em)
    pa_rot = Rot(2)
    pm_rot = Rot(1)
    for tc in range(nch):
        xsl = xn_rot.acquire(em, "y")
        for xc in range(8):
            em.dma("y", s["xnld"][:, xsl * 4096 + xc * 512: xsl * 4096 + (xc + 1) * 512],
                   d["d_xn2"][xc, :, tc * 512: tc * 512 + 512], xn_rot.sems[xsl])
        la_x = xn_rot.sems[xsl].v
        em.w("t", xn_rot.sems[xsl], la_x)
        for cc in range(2):
            psl = pa_rot.acquire(em, "t")
            ps = pbig[:, (PA0, PA1)[psl]: (PA0, PA1)[psl] + 512]
            for xc in range(8):
                em.op("t", lambda t, ps=ps, xsl=xsl, xc=xc, cc=cc: t.matmul(
                    ps, s["wdkvb"][:, xc * 256 + cc * 128: xc * 256 + cc * 128 + 128],
                    s["xnld"][:, xsl * 4096 + xc * 512: xsl * 4096 + (xc + 1) * 512],
                    start=(xc == 0), stop=(xc == 7)), inc=(S["PE"] if xc == 7 else None))
            pe_c = S["PE"].v
            em.w("v", S["PE"], pe_c)
            em.op("v", lambda v, ps=ps, cc=cc, tc=tc: v.tensor_copy(
                s["ckv"][:, cc * T + tc * 512: cc * T + tc * 512 + 512], ps), inc=S["DV"])
            pa_rot.release((S["DV"], S["DV"].v))
        pm_rot.acquire(em, "t")
        pk = pbig[0:DHR, PM: PM + 512]
        for xc in range(8):
            em.op("t", lambda t, pk=pk, xsl=xsl, xc=xc: t.matmul(
                pk, s["wkrb"][:, xc * DHR:(xc + 1) * DHR],
                s["xnld"][:, xsl * 4096 + xc * 512: xsl * 4096 + (xc + 1) * 512],
                start=(xc == 0), stop=(xc == 7)), inc=(S["PE"] if xc == 7 else None))
        pe_k = S["PE"].v
        em.w("v", S["PE"], pe_k)
        em.op("v", lambda v, pk=pk, tc=tc: v.tensor_tensor(
            s["kr"][:, tc * 512: tc * 512 + 512], pk,
            s["rope"][0:DHR, tc * 512: tc * 512 + 512], ALU.mult), inc=S["DV"])
        dv_kr = S["DV"].v
        pm_rot.release((S["DV"], dv_kr))
        xn_rot.release((S["DV"], dv_kr))
    em.drain("v", inc=S["DV"])
    xn_rot.finish(em, "y")

    em.op("v", lambda v: v.memset(s["mk"][:, :], 0.0))
    em.drain("v", inc=S["DV"])

    for tc in range(nch):
        for cc in range(2):
            em.w("s", S["DV"], S["DV"].v)
            em.w("s", S["PE"], S["PE"].v)
            em.op("s", lambda sc, cc=cc, tc=tc: sc.activation(
                s["q2s"][:, 0:512], s["ckv"][:, cc * T + tc * 512: cc * T + tc * 512 + 512],
                AF.Square), inc=S["AC"])
            em.drain("s", inc=S["AC"])
            em.w("t", S["AC"], S["AC"].v)
            em.op("t", lambda t, cc=cc: t.matmul(
                pbig[0:1, PM: PM + 512], s["onescol"][0:128, 0:1], s["q2s"][:, 0:512],
                start=(cc == 0), stop=(cc == 1)), inc=S["PE"])
            if cc == 1:
                em.w("v", S["PE"], S["PE"].v)
                em.op("v", lambda v: v.reduce_max(s["nrm"][0:1, 0:1],
                                                  pbig[0:1, PM: PM + 512], axis=AX.X))
                em.drain("v")
                em.op("v", lambda v: v.tensor_max(
                    s["mk"][0:1, H:H + 1], s["mk"][0:1, H:H + 1], s["nrm"][0:1, 0:1]))
                em.drain("v", inc=S["DV"])
    for tc in range(nch):
        em.w("s", S["DV"], S["DV"].v)
        em.op("s", lambda sc, tc=tc: sc.activation(
            s["q2s"][0:DHR, 0:512], s["kr"][:, tc * 512: tc * 512 + 512], AF.Square),
            inc=S["AC"])
        em.drain("s", inc=S["AC"])
        em.w("t", S["AC"], S["AC"].v)
        em.op("t", lambda t: t.matmul(
            pbig[0:1, PM: PM + 512], s["onescol"][0:DHR, 0:1], s["q2s"][0:DHR, 0:512],
            start=True, stop=True), inc=S["PE"])
        em.w("v", S["PE"], S["PE"].v)
        em.op("v", lambda v: v.reduce_max(s["nrm"][0:1, 0:1],
                                          pbig[0:1, PM: PM + 512], axis=AX.X))
        em.drain("v")
        em.op("v", lambda v: v.tensor_max(
            s["mk"][0:1, H + 1:H + 2], s["mk"][0:1, H + 1:H + 2], s["nrm"][0:1, 0:1]))
        em.drain("v", inc=S["DV"])

    for colh in range(2):
        for cc in range(2):
            em.w("s", S["DV"], S["DV"].v)
            em.w("s", S["PE"], S["PE"].v)
            em.op("s", lambda sc, cc=cc, colh=colh: sc.activation(
                s["q2s"][:, 0:512],
                s["wukb"][:, cc * C + colh * 512: cc * C + colh * 512 + 512],
                AF.Square), inc=S["AC"])
            em.drain("s", inc=S["AC"])
            em.w("t", S["AC"], S["AC"].v)
            em.op("t", lambda t, cc=cc: t.matmul(
                pbig[0:1, PM: PM + 512], s["onescol"][0:128, 0:1], s["q2s"][:, 0:512],
                start=(cc == 0), stop=(cc == 1)), inc=S["PE"])
        em.w("v", S["PE"], S["PE"].v)
        em.op("v", lambda v, colh=colh: v.tensor_copy(
            s["fro"][0:1, colh * 512: colh * 512 + 512], pbig[0:1, PM: PM + 512]),
            inc=S["DV"])
    em.drain("v", inc=S["DV"])
    for h in range(H):
        em.op("v", lambda v, h=h: v.reduce_sum(
            s["mk"][0:1, h:h + 1], s["fro"][0:1, h * 64:(h + 1) * 64], axis=AX.X))
        em.drain("v")
    em.op("v", lambda v: v.tensor_scalar(
        s["mk"][0:1, 0:H], s["mk"][0:1, 0:H], s["mk"][0:1, H:H + 1],
        s["mk"][0:1, H + 1:H + 2], ALU.mult, ALU.add))
    em.drain("v", inc=S["DV"])
    em.w("s", S["DV"], S["DV"].v)
    em.op("s", lambda sc: sc.activation(s["mk"][0:1, 0:H], s["mk"][0:1, 0:H], AF.Sqrt))
    em.drain("s", inc=S["AC"])
    em.w("v", S["AC"], S["AC"].v)
    em.w("t", S["AC"], S["AC"].v)


def q1_phase(em, d, s, p, ntt, q0):
    S = em.sem
    pbig = p["big"]
    nch = ntt * 128 // 512
    wsem = em.dsem("g")
    xn_rot = Rot(2, em)
    st_rot = Rot(2, em)
    for wh in range(2):
        if wh == 1:
            em.w("g", S["PE"], S["PE"].v)
        for xc in range(8):
            em.dma("g", s["wdqh"][:, xc * 2048:(xc + 1) * 2048],
                   d["wdq"][xc * 128:(xc + 1) * 128, wh * 2048:(wh + 1) * 2048], wsem)
        em.w("t", wsem, wsem.v)
        pa_rot = Rot(2)

        def ld_xn(tc):
            xsl_ = xn_rot.acquire(em, "y")
            for xc in range(8):
                em.dma("y", s["xnld"][:, xsl_ * 4096 + xc * 512: xsl_ * 4096 + (xc + 1) * 512],
                       d["d_xn2"][xc, :, q0 * 128 + tc * 512: q0 * 128 + tc * 512 + 512],
                       xn_rot.sems[xsl_])
            return xsl_, xn_rot.sems[xsl_].v

        pend = ld_xn(0)
        for tc in range(nch):
            xsl, la_x = pend
            if tc + 1 < nch:
                pend = ld_xn(tc + 1)
            em.w("t", xn_rot.sems[xsl], la_x)
            dv_last = 0
            for cq in range(16):
                cqc = wh * 16 + cq
                psl = pa_rot.acquire(em, "t")
                ps = pbig[:, (PA0, PA1)[psl]: (PA0, PA1)[psl] + 512]
                for xc in range(8):
                    em.op("t", lambda t, ps=ps, xsl=xsl, xc=xc, cq=cq: t.matmul(
                        ps, s["wdqh"][:, xc * 2048 + cq * 128: xc * 2048 + cq * 128 + 128],
                        s["xnld"][:, xsl * 4096 + xc * 512: xsl * 4096 + (xc + 1) * 512],
                        start=(xc == 0), stop=(xc == 7)), inc=(S["PE"] if xc == 7 else None))
                pe_c = S["PE"].v
                ssl = st_rot.acquire(em, "v")
                cb = s["cqst"][:, ssl * 512:(ssl + 1) * 512]
                em.w("v", S["PE"], pe_c)
                em.op("v", lambda v, cb=cb, ps=ps: v.tensor_copy(cb, ps), inc=S["DV"])
                dv_c = S["DV"].v
                dv_last = dv_c
                pa_rot.release((S["DV"], dv_c))
                em.w("y", S["DV"], dv_c + 1)
                em.dma("y", d["d_cq"][cqc, :, tc * 512: tc * 512 + 512], cb,
                       st_rot.sems[ssl])
                st_rot.release((st_rot.sems[ssl], st_rot.sems[ssl].v))
            xn_rot.release_slot(xsl, (S["DV"], dv_last))
        em.drain("v", inc=S["DV"])
    st_rot.finish(em, "y")
    xn_rot.finish(em, "y")


def q2_quarter(em, d, s, p, ntt, q0, hq):
    S = em.sem
    pbig = p["big"]
    nch = ntt * 128 // 512
    nt = ntt * 128

    wsem = em.dsem("g")
    em.w("g", S["PE"], S["PE"].v)
    for cqc in range(32):
        em.dma("g", s["wuqh"][:, cqc * 256: cqc * 256 + 256],
               d["wuq"][cqc * 128:(cqc + 1) * 128, hq * 256:(hq + 1) * 256], wsem)
        em.dma("g", s["wqrh"][:, cqc * 128: cqc * 128 + 128],
               d["wqr"][cqc * 128:(cqc + 1) * 128, hq * 128:(hq + 1) * 128], wsem)
    em.w("t", wsem, wsem.v)

    for hl in range(4):
        em.op("v", lambda v, hl=hl: v.memset(
            s["qh"][64:128, hl * NTMAX: hl * NTMAX + nt], 0.0))
    em.drain("v", inc=S["DV"])
    em.w("t", S["DV"], S["DV"].v)

    cq_rot = Rot(4, em)
    ev_rot = Rot(2, em)
    dv_prev_evac = 0
    for tc in range(nch):
        slots = [pbig[:, off: off + 512] for off in (PA0, PA1, PB0)]
        em.w("t", S["DV"], dv_prev_evac)
        pe_acc = 0
        for cqc in range(32):
            csl = cq_rot.acquire(em, "y")
            cl = s["cqld"][:, csl * 512:(csl + 1) * 512]
            em.dma("y", cl, d["d_cq"][cqc, :, tc * 512: tc * 512 + 512],
                   cq_rot.sems[csl])
            la_c = cq_rot.sems[csl].v
            em.w("t", cq_rot.sems[csl], la_c)
            for dt in range(2):
                em.op("t", lambda t, ps=slots[dt], cqc=cqc, dt=dt, cl=cl: t.matmul(
                    ps, s["wuqh"][:, cqc * 256 + dt * 128: cqc * 256 + dt * 128 + 128],
                    cl, start=(cqc == 0), stop=(cqc == 31), skip_group_check=True))
            em.op("t", lambda t, ps=slots[2], cqc=cqc, cl=cl: t.matmul(
                ps, s["wqrh"][:, cqc * 128: cqc * 128 + 128],
                cl, start=(cqc == 0), stop=(cqc == 31), skip_group_check=True),
                inc=S["PE"])
            pe_acc = S["PE"].v
            cq_rot.release((S["PE"], pe_acc))
        em.w("v", S["PE"], pe_acc)
        for dt in range(2):
            esl = ev_rot.acquire(em, "v")
            eb = s["qev"][:, esl * 512:(esl + 1) * 512]
            em.op("v", lambda v, eb=eb, ps=slots[dt]: v.tensor_scalar(
                eb, ps, RS, None, ALU.mult))
            em.drain("v", inc=S["DV"])
            h0 = dt * 2
            em.op("v", lambda v, eb=eb, h0=h0, tc=tc: v.tensor_copy(
                s["qh"][0:64, h0 * NTMAX + tc * 512: h0 * NTMAX + tc * 512 + 512],
                eb[0:64, :]))
            em.drain("v", inc=S["DV"])
            dv_cp = S["DV"].v
            em.w("y", S["DV"], dv_cp)
            em.dma("y", s["qh"][0:64, (h0 + 1) * NTMAX + tc * 512:
                                (h0 + 1) * NTMAX + tc * 512 + 512],
                   eb[64:128, :], ev_rot.sems[esl])
            ev_rot.release((ev_rot.sems[esl], ev_rot.sems[esl].v))
        esl = ev_rot.acquire(em, "v")
        eb = s["qev"][:, esl * 512:(esl + 1) * 512]
        em.op("v", lambda v, eb=eb, ps=slots[2]: v.tensor_scalar(
            eb, ps, RS, None, ALU.mult))
        em.drain("v")
        em.op("v", lambda v, eb=eb, tc=tc: v.tensor_tensor(
            eb, eb, s["rope"][:, q0 * 128 + tc * 512: q0 * 128 + tc * 512 + 512],
            ALU.mult))
        em.drain("v", inc=S["DV"])
        dv_e = S["DV"].v
        dv_prev_evac = dv_e
        em.w("y", S["DV"], dv_e)
        for qq in range(4):
            em.dma("y", s["qh"][64:96, qq * NTMAX + tc * 512: qq * NTMAX + tc * 512 + 512],
                   eb[qq * 32:(qq + 1) * 32, :], ev_rot.sems[esl])
        ev_rot.release((ev_rot.sems[esl], ev_rot.sems[esl].v))
    ev_rot.finish(em, "v")
    cq_rot.finish(em, "y")

    dv_rope = S["DV"].v

    bsem = em.dsem()
    em.w("s", S["DV"], dv_rope)
    for hl in range(4):
        h = hq * 4 + hl
        for tc in range(nch):
            em.w("s", S["DV"], S["DV"].v)
            em.w("s", bsem, bsem.v)
            em.op("s", lambda sc, hl=hl, tc=tc: sc.activation(
                s["q2s"][0:96, 0:512],
                s["qh"][0:96, hl * NTMAX + tc * 512: hl * NTMAX + tc * 512 + 512],
                AF.Square), inc=S["AC"])
            em.drain("s", inc=S["AC"])
            em.w("t", S["AC"], S["AC"].v)
            em.op("t", lambda t: t.matmul(
                pbig[0:1, PM: PM + 512], s["onescol"][0:96, 0:1], s["q2s"][0:96, 0:512],
                start=True, stop=True), inc=S["PE"])
            em.w("s", S["PE"], S["PE"].v)
            em.op("s", lambda sc: sc.activation(
                s["qn"][0:1, 0:512], pbig[0:1, PM: PM + 512], AF.Sqrt))
            em.drain("s", inc=S["AC"])
            em.w("v", S["AC"], S["AC"].v)
            em.op("v", lambda v, h=h: v.tensor_scalar(
                s["brow"][0:1, 0:512], s["qn"][0:1, 0:512],
                s["mk"][0:1, h:h + 1], -1.0, ALU.mult, ALU.mult))
            em.drain("v")
            em.op("v", lambda v: v.tensor_copy(s["browb"][0:1, 0:512],
                                               s["brow"][0:1, 0:512]))
            em.drain("v", inc=S["DV"])
            dv_b = S["DV"].v
            em.w("y", S["DV"], dv_b)
            em.dma("y", s["qh"][96:97, hl * NTMAX + tc * 512: hl * NTMAX + tc * 512 + 512],
                   s["browb"][0:1, 0:512], bsem)
    em.w("v", bsem, bsem.v)
    em.w("t", bsem, bsem.v)


def vhat_quarter(em, d, s, p, nkt, hq):
    S = em.sem
    pbig = p["big"]
    em.op("v", lambda v: v.memset(s["vhat"][:, :], 1.0))
    em.drain("v", inc=S["DV"])
    em.w("t", S["DV"], S["DV"].v)
    pb_rot = Rot(2)
    for kt in range(nkt):
        ysl = pb_rot.acquire(em, "t")
        py = pbig[:, (PA0, PA1)[ysl]: (PA0, PA1)[ysl] + 256]
        for cc in range(2):
            em.op("t", lambda t, py=py, cc=cc, kt=kt, hq=hq: t.matmul(
                py, s["ckv"][:, cc * T + kt * 128: cc * T + (kt + 1) * 128],
                s["wuvb"][:, cc * C + hq * 256: cc * C + hq * 256 + 256],
                start=(cc == 0), stop=(cc == 1)), inc=(S["PE"] if cc == 1 else None))
        pe_v = S["PE"].v
        em.w("v", S["PE"], pe_v)
        for hl in range(4):
            em.op("v", lambda v, py=py, hl=hl, kt=kt: v.tensor_copy(
                s["vhat"][:, kt * 260 + hl * 65: kt * 260 + hl * 65 + 64],
                py[:, hl * 64:(hl + 1) * 64]))
        em.drain("v", inc=S["DV"])
        pb_rot.release((S["DV"], S["DV"].v))


def attn_quarter(em, d, s, p, nkt, ntt, q0, hq):
    S = em.sem
    pbig = p["big"]
    nk = nkt * 128

    secs = []
    t0 = 0
    while t0 < ntt:
        secs.append((t0, min(t0 + 8, ntt)))
        t0 += 8

    kh_rot = Rot(2, em)
    pa_rot = Rot(3)
    pt_rot = Rot(4)
    yev_rot = Rot(2, em)
    kc_rot = Rot(2)
    lsem = em.dsem()
    for hl in range(4):
        h = hq * 4 + hl
        ksl = kh_rot.acquire(em, "v")
        kh = s["khat"][:, ksl * T: ksl * T + T]
        em.op("v", lambda v, kh=kh: v.memset(kh[64:128, 0:nk], 0.0))
        em.drain("v", inc=S["DV"])
        dv_z = S["DV"].v
        em.w("y", S["DV"], dv_z)
        em.dma("y", kh[64:96, 0:nk], s["kr"][:, 0:nk], kh_rot.sems[ksl])
        em.dma("y", kh[96:97, 0:nk], d["onesr"][0:1, 0:nk], kh_rot.sems[ksl])
        la_kr = kh_rot.sems[ksl].v
        for tc in range(nk // 512):
            csl = kc_rot.acquire(em, "t")
            pk = pbig[0:64, PB1 + csl * 512: PB1 + csl * 512 + 512]
            for cc in range(2):
                em.op("t", lambda t, pk=pk, cc=cc, h=h, tc=tc: t.matmul(
                    pk, s["wukb"][:, cc * C + h * 64: cc * C + h * 64 + 64],
                    s["ckv"][:, cc * T + tc * 512: cc * T + tc * 512 + 512],
                    start=(cc == 0), stop=(cc == 1), skip_group_check=True),
                    inc=(S["PE"] if cc == 1 else None))
            pe_kc = S["PE"].v
            em.w("v", S["PE"], pe_kc)
            em.op("v", lambda v, kh=kh, pk=pk, tc=tc: v.tensor_copy(
                kh[0:64, tc * 512: tc * 512 + 512], pk), inc=S["DV"])
            kc_rot.release((S["DV"], S["DV"].v))
        em.drain("v", inc=S["DV"])
        dv_kh = S["DV"].v
        em.w("t", kh_rot.sems[ksl], la_kr)
        em.w("t", S["DV"], dv_kh)

        for (st_, en) in secs:
            seclen = (en - st_) * 128
            pav = pbig[0:65, PB0: PB0 + seclen]
            kmax = q0 + en
            first_kt = True
            for kt in range(kmax):
                qlo = max(st_, kt - q0)
                cols0 = qlo * 128
                ncols = (en - qlo) * 128
                diag = (kt - q0 == qlo and kt >= q0 + st_)
                chinfo = []
                for ch in range((ncols + 511) // 512):
                    cw = min(512, ncols - ch * 512)
                    co = cols0 + ch * 512
                    psl = pa_rot.acquire(em, "t")
                    ps = pbig[:, (PA0, PA1, PM)[psl]: (PA0, PA1, PM)[psl] + cw]
                    em.op("t", lambda t, ps=ps, kh=kh, kt=kt, hl=hl, co=co, cw=cw: t.matmul(
                        ps, kh[:, kt * 128:(kt + 1) * 128],
                        s["qh"][:, hl * NTMAX + co: hl * NTMAX + co + cw],
                        start=True, stop=True), inc=S["PE"])
                    pe_s = S["PE"].v
                    tsl = pt_rot.acquire(em, "s")
                    ptb = s["pt"][:, tsl * 512: tsl * 512 + cw]
                    em.w("s", S["PE"], pe_s)
                    em.op("s", lambda sc, ptb=ptb, ps=ps: sc.activation(
                        ptb, ps, AF.Exp), inc=S["AC"])
                    chinfo.append((psl, tsl, ptb, co, cw))
                em.drain("s", inc=S["AC"])
                ac_grp = S["AC"].v
                for psl, _, _, _, _ in chinfo:
                    pa_rot.release_slot(psl, (S["AC"], ac_grp))
                dv_m = None
                if diag:
                    em.w("v", S["AC"], ac_grp)
                    em.op("v", lambda v, ptb=chinfo[0][2]: v.tensor_mul(
                        ptb[:, 0:128], ptb[:, 0:128], s["dmask"][:, :]))
                    em.drain("v", inc=S["DV"])
                    dv_m = S["DV"].v
                em.w("t", S["AC"], ac_grp)
                if dv_m is not None:
                    em.w("t", S["DV"], dv_m)
                for psl, tsl, ptb, co, cw in chinfo:
                    off0 = co - st_ * 128
                    pieces = []
                    o_, rel_ = off0, 0
                    while rel_ < cw:
                        w_ = min(cw - rel_, 512 - (o_ % 512))
                        pieces.append((o_, rel_, w_))
                        o_ += w_
                        rel_ += w_
                    for pi, (po, pr, pw) in enumerate(pieces):
                        em.op("t", lambda t, pav=pav, ptb=ptb, kt=kt, hl=hl,
                              po=po, pr=pr, pw=pw, fk=first_kt: t.matmul(
                            pav[:, po: po + pw],
                            s["vhat"][:, kt * 260 + hl * 65: kt * 260 + (hl + 1) * 65],
                            ptb[:, pr: pr + pw],
                            start=fk, stop=False, skip_group_check=True),
                            inc=(S["PE"] if pi == len(pieces) - 1 else None))
                    pt_rot.release_slot(tsl, (S["PE"], S["PE"].v))
                first_kt = False
            em.drain("t", inc=S["PE"])
            pe_av = S["PE"].v
            em.w("v", S["PE"], pe_av)
            em.op("v", lambda v, pav=pav, seclen=seclen: v.tensor_copy(
                s["lr"][64:65, 0:seclen], pav[64:65, 0:seclen]))
            em.drain("v", inc=S["DV"])
            dv_l = S["DV"].v
            em.w("y", S["DV"], dv_l)
            em.w("y", lsem, lsem.v)
            em.dma("y", s["lr"][0:1, 0:seclen], s["lr"][64:65, 0:seclen], lsem)
            la_l = lsem.v
            em.w("v", lsem, la_l)
            em.op("v", lambda v, seclen=seclen: v.reciprocal(
                s["lr"][0:1, 0:seclen], s["lr"][0:1, 0:seclen]))
            em.drain("v", inc=S["DV"])
            for ch in range((seclen + 511) // 512):
                cw = min(512, seclen - ch * 512)
                esl = yev_rot.acquire(em, "v")
                yb = s["yev"][0:64, esl * 512: esl * 512 + cw]
                em.op("v", lambda v, yb=yb, pav=pav, ch=ch, cw=cw: v.tensor_copy(
                    yb, pav[0:64, ch * 512: ch * 512 + cw]))
                em.drain("v", inc=S["DV"])
                em.w("t", S["DV"], S["DV"].v)
                prl = pbig[0:64, PM: PM + cw]
                em.op("t", lambda t, prl=prl, ch=ch, cw=cw: t.matmul(
                    prl, s["ones1f"][0:1, 0:64], s["lr"][0:1, ch * 512: ch * 512 + cw],
                    start=True, stop=True), inc=S["PE"])
                em.w("v", S["PE"], S["PE"].v)
                em.op("v", lambda v, yb=yb, prl=prl: v.tensor_tensor(
                    yb, yb, prl, ALU.mult))
                em.drain("v", inc=S["DV"])
                dv_y = S["DV"].v
                em.w("y", S["DV"], dv_y)
                em.dma("y", d["d_yn"][h, :, q0 * 128 + st_ * 128 + ch * 512:
                                      q0 * 128 + st_ * 128 + ch * 512 + cw], yb,
                       yev_rot.sems[esl])
                yev_rot.release((yev_rot.sems[esl], yev_rot.sems[esl].v))
            em.w("t", S["DV"], S["DV"].v)
        kh_rot.release((S["PE"], S["PE"].v))
    yev_rot.finish(em, "y")
    kh_rot.finish(em, "y")
    em.w("y", lsem, lsem.v)


def wo_phase(em, d, s, p, ntt, q0):
    S = em.sem
    pbig = p["big"]
    wsem = em.dsem("g")
    for dc in range(8):
        em.dma("g", s["wob"][:, dc * C:(dc + 1) * C],
               d["wo"][dc * 128:(dc + 1) * 128, :], wsem)
    em.w("t", wsem, wsem.v)

    yw_rot = Rot(2, em)
    xt_rot = Rot(4, em)
    y_rot = Rot(2)
    o_rot = Rot(2, em)

    def ld_x(j):
        xsl_ = xt_rot.acquire(em, "y")
        xres_ = s["xt"][:, xsl_ * C:(xsl_ + 1) * C]
        em.dma("y", xres_, d["d_x1"][q0 + j, :, :], xt_rot.sems[xsl_])
        return xsl_, xt_rot.sems[xsl_].v, xres_

    def ld_yw(g):
        j0_ = g * 4
        j1_ = min(j0_ + 4, ntt)
        gw_ = (j1_ - j0_) * 128
        ysl_ = yw_rot.acquire(em, "y")
        for dc in range(8):
            em.dma("y", s["yw"][:, ysl_ * 4096 + dc * 512: ysl_ * 4096 + dc * 512 + gw_],
                   d["d_yn"][2 * dc: 2 * dc + 2, :,
                             q0 * 128 + j0_ * 128: q0 * 128 + j0_ * 128 + gw_]
                   .rearrange("a p c -> (a p) c"),
                   yw_rot.sems[ysl_])
        return ysl_, yw_rot.sems[ysl_].v

    ngr = (ntt + 3) // 4
    pend = ld_x(0)
    pendy = ld_yw(0)
    for g in range(ngr):
        j0 = g * 4
        j1 = min(j0 + 4, ntt)
        ysl, la_y = pendy
        if g + 1 < ngr:
            pendy = ld_yw(g + 1)
        pe_g = 0
        for j in range(j0, j1):
            xsl, la_x, xres = pend
            if j + 1 < ntt:
                pend = ld_x(j + 1)
            psl = y_rot.acquire(em, "t")
            py = pbig[:, PB0 + psl * 1024: PB0 + psl * 1024 + 1024]
            em.w("t", yw_rot.sems[ysl], la_y)
            for half in range(2):
                ph = py[:, half * 512: half * 512 + 512]
                for dc in range(8):
                    em.op("t", lambda t, ph=ph, ysl=ysl, dc=dc, j=j, j0=j0, half=half: t.matmul(
                        ph,
                        s["yw"][:, ysl * 4096 + dc * 512 + (j - j0) * 128:
                                ysl * 4096 + dc * 512 + (j - j0) * 128 + 128],
                        s["wob"][:, dc * C + half * 512: dc * C + half * 512 + 512],
                        start=(dc == 0), stop=(dc == 7)),
                        inc=(S["PE"] if (dc == 7 and half == 1) else None))
            pe_y = S["PE"].v
            pe_g = pe_y
            osl = o_rot.acquire(em, "v")
            xo = s["x1t"][:, osl * C:(osl + 1) * C]
            em.w("v", S["PE"], pe_y)
            em.w("v", xt_rot.sems[xsl], la_x)
            em.op("v", lambda v, xo=xo, py=py, xres=xres: v.tensor_add(
                xo, xres, py), inc=S["DV"])
            dv_x = S["DV"].v
            y_rot.release((S["DV"], dv_x))
            xt_rot.release_slot(xsl, (S["DV"], dv_x))
            em.w("y", S["DV"], dv_x + 1)
            em.dma("y", d["d_x2"][j, :, :], xo, o_rot.sems[osl])
            o_rot.release((o_rot.sems[osl], o_rot.sems[osl].v))
        yw_rot.release_slot(ysl, (S["PE"], pe_g))
    em.drain("v", inc=S["DV"])
    o_rot.finish(em, "y")
    yw_rot.finish(em, "y")
    xt_rot.finish(em, "y")


def build_nc():
    nc = bass.Bass("TRN2", target_bir_lowering=False, debug=False)
    d = {}

    def di(name, shape, dt, kind=None):
        if kind:
            d[name] = nc.dram_tensor(name, shape, dt, kind=kind)
        else:
            d[name] = nc.dram_tensor(name, shape, dt)

    di("x4", [T, C], F32, "ExternalInput")
    di("wdq", [C, 4096], F32, "ExternalInput")
    di("wuq", [4096, C], F32, "ExternalInput")
    di("wdkv", [C, 256], F32, "ExternalInput")
    di("wuk", [256, C], F32, "ExternalInput")
    di("wuv", [256, C], F32, "ExternalInput")
    di("wqr", [4096, 512], F32, "ExternalInput")
    di("wkr", [C, DHR], F32, "ExternalInput")
    di("wo", [C, C], F32, "ExternalInput")
    di("wg", [C, E], F32, "ExternalInput")
    di("bg", [1, E], F32, "ExternalInput")
    di("w1", [E, C, C], F32, "ExternalInput")
    di("b1c", [128, 64], F32, "ExternalInput")
    di("w2", [E, C, C], F32, "ExternalInput")
    di("b2", [E, C], BF16, "ExternalInput")
    di("cidf", [128, 128], F32, "ExternalInput")
    di("cidb", [128, 128], BF16, "ExternalInput")
    di("rope", [128, T], F32, "ExternalInput")
    di("dmask", [128, 128], BF16, "ExternalInput")
    di("cunit", [8, 8 * 128], BF16, "ExternalInput")
    di("onesr", [1, T], BF16, "ExternalInput")
    di("out", [NTMAX, C], F32, "ExternalOutput")
    if DBG_PHASE is not None:
        di("odbg", [128, 8192], F32, "ExternalOutput")

    di("d_h", [E, 128, 8, T], BF16)
    di("d_x1", [32, 128, C], F32)
    di("d_x2", [20, 128, C], F32)
    di("d_xnA", [8, 128, T], BF16)
    di("d_xn2", [8, 128, T], BF16)
    di("d_cq", [32, 128, NTMAX], BF16)
    di("d_yn", [H, HS, T], BF16)
    di("d_yp", [32, 128, C], F32)

    ctx = ExitStack()
    semsA, semsB = {}, {}
    for name in ("PE", "DV", "AC", "GP", "SY"):
        hdl = ctx.enter_context(nc.semaphore(name))
        semsA[name] = Sem(hdl)
        semsB[name] = Sem(hdl)
    dpool = [ctx.enter_context(nc.semaphore(f"D{i}")) for i in range(N_DSEM)]
    gpool = [ctx.enter_context(nc.semaphore(f"G{i}")) for i in range(N_GSEM)]
    emA, emB = Em(semsA, dpool, gpool), Em(semsB, dpool, gpool)
    EMS = ((emA, ROLE[0]), (emB, ROLE[1]))

    s = {}
    p = {}
    p["big"] = ctx.enter_context(nc.psum_tensor("pbig", [128, 3584], F32))
    p["bf"] = ctx.enter_context(nc.psum_tensor("pbf", [128, 1024], BF16))

    _ser = [0]

    def sb(scope, name, shape, dt):
        _ser[0] += 1
        s[name] = scope.enter_context(
            nc.sbuf_tensor(f"sb{_ser[0]}_{name}", shape, dt))

    sb(ctx, "ones1f", [1, 128], F32)
    sb(ctx, "ones512f", [1, 512], F32)
    sb(ctx, "onescol", [128, 1], BF16)
    sb(ctx, "epsc", [128, 1], F32)
    sb(ctx, "cidf", [128, 128], F32)
    sb(ctx, "cidb", [128, 128], BF16)
    sb(ctx, "dmask", [128, 128], BF16)
    sb(ctx, "bg", [1, E], F32)
    sb(ctx, "b1c", [128, 64], F32)
    sb(ctx, "b2b", [E, C], BF16)
    sb(ctx, "cunit", [8, 8 * 128], BF16)
    sb(ctx, "wgc", [128, 8 * E], BF16)
    sb(ctx, "wfm2", [8, T], BF16)
    sb(ctx, "xt", [128, 4 * C], F32)
    sb(ctx, "x1t", [128, 2 * C], F32)
    sb(ctx, "sqs", [128, 2 * C], BF16)
    sb(ctx, "q2s", [128, 512], BF16)
    sb(ctx, "nrm", [1, 512], F32)
    sb(ctx, "mk", [1, 2 * H], F32)
    sb(ctx, "qn", [1, 512], F32)
    sb(ctx, "brow", [1, 512], F32)
    sb(ctx, "browb", [1, 512], BF16)
    sb(ctx, "gsc", [128, E], F32)
    sb(ctx, "gmx", [128, E], F32)
    sb(ctx, "gge", [128, E], F32)
    sb(ctx, "gwv", [128, E], F32)
    sb(ctx, "stat", [128, 6 * 32], F32)

    for em, r in EMS:
        preload(em, d, s)
    if DBG_PHASE == "pre":
        for em, r in EMS:
            em.stop_with_dump(d, [(s["cidf"][:, :], 128, 128),
                                  (s["wgc"][:, 0:64], 128, 64),
                                  (s["b1c"][:, :], 128, 64)])
        return _finish(nc, ctx, EMS)

    def run_ln(ntiles_key, src_key, dst):
        with ExitStack() as sp:
            sb(sp, "xnt", [128, 2 * C], BF16)
            sb(sp, "fmst", [128, 2 * 8 * 512], BF16)
            sl_ = dict(s)
            for em, r in EMS:
                if src_key == "x4":
                    src = lambda i: d["x4"][i * 128:(i + 1) * 128, :]
                elif src_key == "x1":
                    src = lambda i: d["d_x1"][i, :, :]
                else:
                    src = lambda i: d["d_x2"][i, :, :]
                ln_phase(em, d, sl_, p, r[ntiles_key], src, dst)
                em.barrier()

    def run_gate(ntiles_key):
        with ExitStack() as sp:
            sb(sp, "gld", [128, 2 * 1024], BF16)
            sl_ = dict(s)
            for em, r in EMS:
                gate_phase(em, d, sl_, p, r[ntiles_key], d["d_xnA"])
                em.barrier()

    def run_ffn(nkt_key, res_key, dst_key):
        with ExitStack() as sp:
            sb(sp, "w1b", [128, 2 * 8 * C], BF16)
            sb(sp, "w2b", [128, 32 * C], BF16)
            sb(sp, "hrelu", [128, 4 * 512], BF16)
            sb(sp, "hst", [128, 4 * 512], BF16)
            sb(sp, "hld", [128, 2 * 4096], BF16)
            sb(sp, "yt", [128, 2 * C], F32)
            sb(sp, "ypld", [128, 2 * C], F32)
            sb(sp, "xnld", [128, 2 * 8 * 512], BF16)
            sl_ = dict(s)
            for em, r in EMS:
                if res_key == "x4":
                    res = lambda j: d["x4"][j * 128:(j + 1) * 128, :]
                else:
                    res = lambda j: d["d_x2"][j, :, :]
                if dst_key == "x1":
                    dst = lambda j: d["d_x1"][j, :, :]
                else:
                    dst = lambda j: d["out"][j * 128:(j + 1) * 128, :]
                ffn_phase(em, d, sl_, p, r[nkt_key], d["d_xnA"], res, dst)
                em.barrier()

    def dbg_stop(taps_fn=None):
        for em, r in EMS:
            em.stop_with_dump(d, taps_fn() if taps_fn else [])

    run_ln("NKT", "x4", d["d_xnA"])
    if DBG_PHASE == "ln1":
        dbg_stop(lambda: [(d["d_xnA"][0, :, 0:2048], 128, 2048),
                          (d["d_xnA"][1, :, 0:2048], 128, 2048)])
        return _finish(nc, ctx, EMS)
    run_gate("NKT")
    if DBG_PHASE == "gate1":
        dbg_stop(lambda: [(s["wfm2"][0:8, 0:2560], 8, 2560)])
        return _finish(nc, ctx, EMS)
    run_ffn("NKT", "x4", "x1")
    if DBG_PHASE == "ffn1":
        dbg_stop(lambda: [(d["d_x1"][0, :, :], 128, 1024),
                          (d["d_x1"][11, :, :], 128, 1024)])
        return _finish(nc, ctx, EMS)
    run_ln("NKT", "x1", d["d_xn2"])
    if DBG_PHASE == "ln2":
        dbg_stop(lambda: [(d["d_xn2"][2, :, 0:2048], 128, 2048),
                          (d["d_xn2"][2, :, 2048:4096], 128, 2048),
                          (d["d_xn2"][5, :, 0:2048], 128, 2048),
                          (d["d_xn2"][5, :, 2048:4096], 128, 2048)])
        return _finish(nc, ctx, EMS)

    with ExitStack() as sp:
        sb(sp, "rope", [128, T], F32)
        sb(sp, "ckv", [128, 2 * T], BF16)
        sb(sp, "kr", [DHR, T], BF16)
        sb(sp, "wdkvb", [128, 8 * 256], BF16)
        sb(sp, "wukb", [128, 2 * C], BF16)
        sb(sp, "wuvb", [128, 2 * C], BF16)
        sb(sp, "wkrb", [128, 8 * DHR], BF16)
        with ExitStack() as spk:
            sb(spk, "xnld", [128, 2 * 8 * 512], BF16)
            sb(spk, "fro", [1, C], F32)
            sk_ = dict(s)
            for em, r in EMS:
                kv_phase(em, d, sk_, p, r["NKT"])
                em.barrier()
        if DBG_PHASE == "kv":
            dbg_stop(lambda: [(s["ckv"][:, 0:2048], 128, 2048),
                              (s["kr"][0:DHR, 0:1024], DHR, 1024),
                              (s["mk"][0:1, 0:2 * H], 1, 2 * H),
                              (d["d_xn2"][0, :, 0:2048], 128, 2048),
                              (d["d_xn2"][3, :, 0:2048], 128, 2048)])
            sp.close()
            return _finish(nc, ctx, EMS)

        with ExitStack() as spq:
            sb(spq, "wdqh", [128, 8 * 2048], BF16)
            sb(spq, "xnld", [128, 2 * 8 * 512], BF16)
            sb(spq, "cqst", [128, 2 * 512], BF16)
            sq_ = dict(s)
            for em, r in EMS:
                q1_phase(em, d, sq_, p, r["NTT"], r["Q0"])
                em.barrier()
        if DBG_PHASE == "q1":
            dbg_stop(lambda: [(d["d_cq"][0, :, 0:2048], 128, 2048),
                              (d["d_cq"][31, :, 0:2048], 128, 2048)])
            sp.close()
            return _finish(nc, ctx, EMS)

        with ExitStack() as spq:
            sb(spq, "wuqh", [128, 32 * 256], BF16)
            sb(spq, "wqrh", [128, 32 * 128], BF16)
            sb(spq, "cqld", [128, 4 * 512], BF16)
            sb(spq, "qev", [128, 2 * 512], BF16)
            sb(spq, "qh", [128, 4 * NTMAX], BF16)
            sb(spq, "vhat", [128, 32 * 4 * 65], BF16)
            sb(spq, "khat", [128, 2 * T], BF16)
            sb(spq, "pt", [128, 4 * 512], BF16)
            sb(spq, "lr", [128, 1024], F32)
            sb(spq, "yev", [64, 2 * 512], BF16)
            sa_ = dict(s)
            for em, r in EMS:
                nq = 1 if DBG_PHASE in ("q2", "vhat", "attn") else 4
                for hq in range(nq):
                    q2_quarter(em, d, sa_, p, r["NTT"], r["Q0"], hq)
                    em.barrier()
                    if DBG_PHASE == "q2":
                        em.stop_with_dump(d, [(s["qh"][:, 0:2048], 128, 2048),
                                              (s["qh"][:, NTMAX:NTMAX + 2048], 128, 2048)])
                        break
                    vhat_quarter(em, d, sa_, p, r["NKT"], hq)
                    em.barrier()
                    if DBG_PHASE == "vhat":
                        em.stop_with_dump(d, [(s["vhat"][:, 0:2080], 128, 2080)])
                        break
                    attn_quarter(em, d, sa_, p, r["NKT"], r["NTT"], r["Q0"], hq)
                    em.barrier()
                    if DBG_PHASE == "attn":
                        nt_, o_ = r["NTT"] * 128, r["Q0"] * 128
                        em.stop_with_dump(
                            d, [(d["d_yn"][0, :, o_:o_ + nt_], HS, nt_),
                                (d["d_yn"][3, :, o_:o_ + nt_], HS, nt_)])
                        break
        if DBG_PHASE in ("q2", "vhat", "attn"):
            sp.close()
            return _finish(nc, ctx, EMS)

    with ExitStack() as sp:
        sb(sp, "wob", [128, 8 * C], BF16)
        sb(sp, "yw", [128, 2 * 4096], BF16)
        sw_ = dict(s)
        for em, r in EMS:
            wo_phase(em, d, sw_, p, r["NTT"], r["Q0"])
            em.barrier()
    if DBG_PHASE == "wo":
        dbg_stop(lambda: [(d["d_x2"][0, :, :], 128, 1024),
                          (d["d_x2"][11, :, :], 128, 1024)])
        return _finish(nc, ctx, EMS)
    run_ln("NTT", "x2", d["d_xnA"])
    run_gate("NTT")
    run_ffn("NTT", "x2", "out")
    for em, r in EMS:
        for sm in em.dsems:
            em.w("y", sm, sm.v)

    return _finish(nc, ctx, EMS)


def _finish(nc, ctx, EMS):
    emA, emB = EMS[0][0], EMS[1][0]
    blk = ctx.enter_context(nc.Block())

    def body(eng, key):
        pid = eng.partition_id()
        with eng.register(f"role_{key}") as r:
            eng.reg_mov(r, pid)
            eng.reg_mod(r, r, 2)
            with eng.If_eq(r, 0):
                for fn in emA.q[key]:
                    fn(eng)
            with eng.Else():
                for fn in emB.q[key]:
                    fn(eng)

    @blk.tensor
    def _(t):
        body(t, "t")

    @blk.vector
    def _(v):
        body(v, "v")

    @blk.scalar
    def _(sc):
        body(sc, "s")

    @blk.gpsimd
    def _(g):
        body(g, "g")

    @blk.sync
    def _(y):
        body(y, "y")

    ctx.close()
    return nc


def make_consts():
    t = np.arange(T, dtype=np.float32)
    inv = 1.0 / (10000.0 ** (np.arange(0, DHR, 2, dtype=np.float32) / DHR))
    freqs = t[:, None] * inv[None, :]
    emb = np.concatenate([freqs, freqs], axis=-1)
    rope = np.ascontiguousarray(np.tile(
        (np.cos(emb) + np.sin(emb)).astype(np.float32).T, (4, 1)))
    ki, qi = np.meshgrid(np.arange(128), np.arange(128), indexing="ij")
    dmask = (qi >= ki).astype(ml_dtypes.bfloat16)
    identf = np.eye(128, dtype=np.float32)
    identb = np.eye(128).astype(ml_dtypes.bfloat16)
    cunit = np.zeros((8, 8 * 128), dtype=ml_dtypes.bfloat16)
    for e in range(8):
        cunit[e, e * 128:(e + 1) * 128] = 1.0
    onesr = np.ones((1, T), dtype=ml_dtypes.bfloat16)
    return rope, dmask, identf, identb, cunit, onesr


_NC_CACHE = {}


def kernel(L0_attn_x, L0_attn_W_DQ, L0_attn_W_UQ, L0_attn_W_DKV, L0_attn_W_UK,
           L0_attn_W_UV, L0_attn_W_QR, L0_attn_W_KR, L0_attn_W_O,
           L0_ffn_W_gate, L0_ffn_b_gate, L0_ffn_W1, L0_ffn_b1, L0_ffn_W2,
           L0_ffn_b2):
    key = ("nc", DBG_PHASE)
    if key not in _NC_CACHE:
        _NC_CACHE[key] = build_nc()
    nc = _NC_CACHE[key]
    rope, dmask, identf, identb, cunit, onesr = make_consts()

    x = np.ascontiguousarray(np.asarray(L0_attn_x, dtype=np.float32))
    b1 = np.asarray(L0_ffn_b1, np.float32).reshape(E * 8, 128).T.copy()
    common = dict(
        wdq=np.asarray(L0_attn_W_DQ, np.float32),
        wuq=np.asarray(L0_attn_W_UQ, np.float32),
        wdkv=np.asarray(L0_attn_W_DKV, np.float32),
        wuk=np.asarray(L0_attn_W_UK, np.float32),
        wuv=np.asarray(L0_attn_W_UV, np.float32),
        wqr=np.asarray(L0_attn_W_QR, np.float32),
        wkr=np.asarray(L0_attn_W_KR, np.float32),
        wo=np.asarray(L0_attn_W_O, np.float32),
        wg=np.asarray(L0_ffn_W_gate, np.float32),
        bg=np.asarray(L0_ffn_b_gate, np.float32).reshape(1, E),
        w1=np.asarray(L0_ffn_W1, np.float32),
        b1c=b1,
        w2=np.asarray(L0_ffn_W2, np.float32),
        b2=np.asarray(L0_ffn_b2, np.float32).astype(ml_dtypes.bfloat16),
        cidf=identf, cidb=identb, rope=rope, dmask=dmask, cunit=cunit,
        onesr=onesr,
    )
    in_maps = []
    for c in range(8):
        m = dict(common)
        m["x4"] = x[c // 2]
        in_maps.append(m)

    res = run_bass_kernel_spmd(nc, in_maps, core_ids=list(range(8)))
    out = np.empty((NB, T, C), dtype=np.float32)
    for b in range(NB):
        out[b, 0:SPLIT, :] = res.results[2 * b]["out"]
        out[b, SPLIT:T, :] = res.results[2 * b + 1]["out"][0: T - SPLIT]
    if DBG_PHASE is not None:
        kernel.dbg = [np.asarray(r["odbg"]) for r in res.results if "odbg" in r]
    return out


# version 06
# speedup vs baseline: 1.0386x; 1.0032x over previous
import math
import numpy as np
import ml_dtypes
import concourse.bass as bass
import concourse.mybir as mybir
from concourse.bass_utils import run_bass_kernel_spmd
from contextlib import ExitStack

F32 = mybir.dt.float32
BF16 = mybir.dt.bfloat16
AF = mybir.ActivationFunctionType
ALU = mybir.AluOpType
AX = mybir.AxisListType

NB, T, C = 4, 4096, 1024
H, HS, DHR = 16, 64, 32
E = 8
SPLIT = 2560
NTMAX = SPLIT
RS = 1.0 / math.sqrt(HS + DHR)

DBG_PHASE = None
N_DSEM = 40
N_GSEM = 8


class Sem:
    def __init__(self, handle):
        self.h = handle
        self.v = 0


class Em:
    ENGS = ("t", "v", "s", "g", "y")

    def __init__(self, sems, dpool, gpool):
        self.q = {k: [] for k in self.ENGS}
        self.sem = sems
        self.dsems = [Sem(h) for h in dpool]
        self.gsems = [Sem(h) for h in gpool]
        self.dnext = 0
        self.gnext = 0
        self.stopped = False

    def dsem(self, qk="y"):
        if qk == "g":
            sm = self.gsems[self.gnext % len(self.gsems)]
            self.gnext += 1
            return sm
        sm = self.dsems[self.dnext % len(self.dsems)]
        self.dnext += 1
        return sm

    def w(self, e, sem, val=None):
        if self.stopped:
            return
        v = sem.v if val is None else val
        if v <= 0:
            return
        self.q[e].append(lambda eng, s=sem.h, v=v: eng.wait_ge(s, v))

    def op(self, e, fn, inc=None, n=1):
        if self.stopped:
            return None
        if inc is None:
            self.q[e].append(lambda eng, fn=fn: fn(eng))
            return None
        inc.v += n
        self.q[e].append(lambda eng, fn=fn, s=inc.h, n=n: fn(eng).then_inc(s, n))
        return inc.v

    def drain(self, e, inc=None, n=1):
        if self.stopped:
            return None
        if inc is None:
            self.q[e].append(lambda eng: eng.drain())
            return None
        inc.v += n
        self.q[e].append(lambda eng, s=inc.h, n=n: eng.drain().then_inc(s, n))
        return inc.v

    def dma(self, e, dst, src, sem):
        if self.stopped:
            return None
        sem.v += 16
        self.q[e].append(
            lambda eng, dst=dst, src=src, s=sem.h: eng.dma_start(dst, src).then_inc(s, 16))
        return sem.v

    def barrier(self):
        if self.stopped:
            return
        S = self.sem
        self.drain("t", inc=S["PE"])
        self.drain("v", inc=S["DV"])
        self.drain("s", inc=S["AC"])
        self.drain("g", inc=S["GP"])
        self.drain("y", inc=S["SY"])
        toks = [(S[k], S[k].v) for k in ("PE", "DV", "AC", "GP", "SY")]
        for e in self.ENGS:
            for sem, v in toks:
                self.w(e, sem, v)

    def stop_with_dump(self, d, taps):
        self.barrier()
        sm = self.dsem("g")
        col = 0
        for ap, rows, ncols in taps:
            self.dma("g", d["odbg"][0:rows, col:col + ncols], ap, sm)
            col += ncols
        self.w("g", sm, sm.v)
        self.stopped = True


class Rot:

    def __init__(self, n, em=None, qk="y"):
        self.n = n
        self.rel = [[] for _ in range(n)]
        self.i = 0
        self.sems = [em.dsem(qk) for _ in range(n)] if em is not None else None

    def acquire(self, em, engine):
        sl = self.i % self.n
        for sem, v in self.rel[sl]:
            em.w(engine, sem, v)
        self.rel[sl] = []
        self.i += 1
        return sl

    def release(self, *tokens):
        sl = (self.i - 1) % self.n
        self.rel[sl].extend(tokens)

    def release_slot(self, sl, *tokens):
        self.rel[sl].extend(tokens)

    def finish(self, em, engine):
        if self.sems:
            for sm in self.sems:
                em.w(engine, sm, sm.v)


PA0, PA1 = 0, 512
PB0, PB1 = 1024, 2048
PM = 3072

ROLE = [dict(NKT=20, NTT=20, Q0=0), dict(NKT=32, NTT=12, Q0=20)]


def preload(em, d, s):
    S = em.sem
    em.op("v", lambda v: v.memset(s["ones1f"][:, :], 1.0))
    em.op("v", lambda v: v.memset(s["ones512f"][:, :], 1.0))
    em.op("v", lambda v: v.memset(s["onescol"][:, :], 1.0))
    em.op("v", lambda v: v.memset(s["epsc"][:, :], 1e-5))
    em.drain("v", inc=S["DV"])
    sm = em.dsem("g")
    em.dma("g", s["cidf"][:, :], d["cidf"][:, :], sm)
    em.dma("g", s["cidb"][:, :], d["cidb"][:, :], sm)
    em.dma("g", s["dmask"][:, :], d["dmask"][:, :], sm)
    em.dma("g", s["bg"][:, :], d["bg"][:, :], sm)
    em.dma("g", s["b1c"][:, :], d["b1c"][:, :], sm)
    em.dma("g", s["b2b"][:, :], d["b2"][:, :], sm)
    em.dma("g", s["cunit"][:, :], d["cunit"][:, :], sm)
    for xc in range(8):
        em.dma("g", s["wgc"][:, xc * 8:(xc + 1) * 8],
               d["wg"][xc * 128:(xc + 1) * 128, :], sm)
    em.w("g", sm, sm.v)
    em.barrier()


def ln_phase(em, d, s, p, ntiles, src_fn, dst_dram):
    S = em.sem
    st = s["stat"]
    pbf = p["bf"]

    em.op("v", lambda v: v.memset(st[:, :], 0.0))
    em.drain("v", inc=S["DV"])
    dv_zero = S["DV"].v

    xt_rot = Rot(4, em)
    sq_rot = Rot(2)
    for i in range(ntiles):
        sl = xt_rot.acquire(em, "y")
        xs = s["xt"][:, sl * C:(sl + 1) * C]
        em.dma("y", xs, src_fn(i), xt_rot.sems[sl])
        lv = xt_rot.sems[sl].v
        em.w("s", xt_rot.sems[sl], lv)
        em.w("s", S["DV"], dv_zero)
        qsl = sq_rot.acquire(em, "s")
        em.op("s", lambda sc, xs=xs, i=i, qsl=qsl: sc.activation(
            s["sqs"][:, qsl * C:(qsl + 1) * C], xs, AF.Square,
            accum_out=st[:, 32 + i:33 + i]), inc=S["AC"])
        ac = S["AC"].v
        sq_rot.release((S["AC"], ac))
        em.w("v", xt_rot.sems[sl], lv)
        em.op("v", lambda v, xs=xs, i=i: v.reduce_sum(st[:, i:i + 1], xs, axis=AX.X),
              inc=S["DV"])
        xt_rot.release((S["AC"], ac), (S["DV"], S["DV"].v))
    em.drain("s", inc=S["AC"])
    em.drain("v", inc=S["DV"])
    ac_stats = S["AC"].v

    n = ntiles
    em.w("v", S["AC"], ac_stats)
    em.op("v", lambda v: v.tensor_scalar(st[:, 64:64 + n], st[:, 0:n], 1.0 / C, None, ALU.mult))
    em.drain("v")
    em.op("v", lambda v: v.tensor_scalar(st[:, 96:96 + n], st[:, 32:32 + n], 1.0 / C, None, ALU.mult))
    em.drain("v")
    em.op("v", lambda v: v.tensor_mul(st[:, 160:160 + n], st[:, 64:64 + n], st[:, 64:64 + n]))
    em.drain("v")
    em.op("v", lambda v: v.tensor_sub(st[:, 96:96 + n], st[:, 96:96 + n], st[:, 160:160 + n]))
    em.drain("v", inc=S["DV"])
    em.w("s", S["DV"], S["DV"].v)
    em.op("s", lambda sc: sc.activation(st[:, 160:160 + n], st[:, 96:96 + n], AF.Ln,
                                        bias=s["epsc"][:, :]))
    em.drain("s")
    em.op("s", lambda sc: sc.activation(st[:, 128:128 + n], st[:, 160:160 + n], AF.Exp,
                                        scale=-0.5))
    em.drain("s", inc=S["AC"])
    ac_rstd = S["AC"].v
    em.w("v", S["AC"], ac_rstd)

    xnt_rot = Rot(2)
    pbf_rot = Rot(1)
    fm_rot = Rot(2, em)
    fsl = 0
    for i in range(ntiles):
        sl = xt_rot.acquire(em, "y")
        xs = s["xt"][:, sl * C:(sl + 1) * C]
        em.dma("y", xs, src_fn(i), xt_rot.sems[sl])
        lv = xt_rot.sems[sl].v
        nsl = xnt_rot.acquire(em, "v")
        xnt = s["xnt"][:, nsl * C:(nsl + 1) * C]
        em.w("v", xt_rot.sems[sl], lv)
        em.op("v", lambda v, xs=xs, xnt=xnt, i=i: v.tensor_scalar(
            xnt, xs, st[:, 64 + i:65 + i], st[:, 128 + i:129 + i], ALU.subtract, ALU.mult))
        em.drain("v", inc=S["DV"])
        dv_xnt = S["DV"].v
        xt_rot.release((S["DV"], dv_xnt))

        pbf_rot.acquire(em, "t")
        em.w("t", S["DV"], dv_xnt)
        for xc in range(8):
            em.op("t", lambda t, xnt=xnt, xc=xc: t.transpose(
                pbf[:, xc * 128:(xc + 1) * 128], xnt[:, xc * 128:(xc + 1) * 128],
                s["cidb"][:, :]))
        em.drain("t", inc=S["PE"])
        pe_tr = S["PE"].v
        xnt_rot.release((S["PE"], pe_tr))
        em.w("v", S["PE"], pe_tr)
        if i % 4 == 0:
            fsl = fm_rot.acquire(em, "v")
        for xc in range(8):
            em.op("v", lambda v, xc=xc, i=i, fsl=fsl: v.tensor_copy(
                s["fmst"][:, fsl * 4096 + xc * 512 + (i % 4) * 128:
                          fsl * 4096 + xc * 512 + (i % 4) * 128 + 128],
                pbf[:, xc * 128:(xc + 1) * 128]))
        em.drain("v", inc=S["DV"])
        pbf_rot.release((S["DV"], S["DV"].v))
        if i % 4 == 3 or i == ntiles - 1:
            g0 = (i // 4) * 4
            ncb = (i - g0 + 1) * 128
            em.w("y", S["DV"], S["DV"].v)
            for xc in range(8):
                em.dma("y", dst_dram[xc, :, g0 * 128: g0 * 128 + ncb],
                       s["fmst"][:, fsl * 4096 + xc * 512: fsl * 4096 + xc * 512 + ncb],
                       fm_rot.sems[fsl])
            fm_rot.release((fm_rot.sems[fsl], fm_rot.sems[fsl].v))
    xt_rot.finish(em, "y")
    fm_rot.finish(em, "y")


def gate_phase(em, d, s, p, ntiles, xn_dram):
    S = em.sem
    pbig = p["big"]
    gld_rot = Rot(2, em)
    sc_rot = Rot(2)
    tr_rot = Rot(2)
    for i in range(ntiles):
        lsl = gld_rot.acquire(em, "y")
        gl = s["gld"][:, lsl * 1024:(lsl + 1) * 1024]
        for xc in range(8):
            em.dma("y", gl[:, xc * 128:(xc + 1) * 128],
                   xn_dram[xc, :, i * 128:(i + 1) * 128], gld_rot.sems[lsl])
        lv = gld_rot.sems[lsl].v
        ssl = sc_rot.acquire(em, "t")
        ps = pbig[:, (PA0, PA1)[ssl]: (PA0, PA1)[ssl] + 8]
        em.w("t", gld_rot.sems[lsl], lv)
        for xc in range(8):
            em.op("t", lambda t, ps=ps, gl=gl, xc=xc: t.matmul(
                ps, gl[:, xc * 128:(xc + 1) * 128],
                s["wgc"][:, xc * 8:(xc + 1) * 8], start=(xc == 0), stop=False))
        em.op("t", lambda t, ps=ps: t.matmul(
            ps, s["ones1f"][:, :], s["bg"][:, :], start=False, stop=True), inc=S["PE"])
        pe_sc = S["PE"].v
        gld_rot.release((S["PE"], pe_sc))
        em.w("v", S["PE"], pe_sc)
        em.op("v", lambda v, ps=ps: v.tensor_copy(s["gsc"][:, :], ps))
        em.drain("v")
        em.op("v", lambda v: v.max(out=s["gmx"][:, :], in_=s["gsc"][:, :]))
        em.drain("v")
        em.op("v", lambda v: v.tensor_scalar(
            s["gge"][:, :], s["gsc"][:, :], s["gmx"][:, 1:2], None, ALU.is_ge))
        em.drain("v")
        em.op("v", lambda v: v.tensor_mul(s["gwv"][:, :], s["gsc"][:, :], s["gge"][:, :]))
        em.drain("v", inc=S["DV"])
        dv_w = S["DV"].v
        sc_rot.release((S["DV"], dv_w))

        tsl = tr_rot.acquire(em, "t")
        em.w("t", S["DV"], dv_w)
        p8 = pbig[0:8, (PB0, PB1)[tsl]: (PB0, PB1)[tsl] + 128]
        em.op("t", lambda t, p8=p8: t.matmul(
            p8, s["gwv"][:, :], s["cidf"][:, :], start=True, stop=True), inc=S["PE"])
        pe_tr = S["PE"].v
        em.w("v", S["PE"], pe_tr)
        em.op("v", lambda v, p8=p8, i=i: v.tensor_copy(
            s["wfm2"][0:8, i * 128:(i + 1) * 128], p8))
        em.drain("v", inc=S["DV"])
        tr_rot.release((S["DV"], S["DV"].v))
    gld_rot.finish(em, "y")


def ffn_phase(em, d, s, p, nkt, xn_dram, res_fn, dst_fn):
    S = em.sem
    pbig = p["big"]
    nch = nkt * 128 // 512

    w1_rot = Rot(2, em, "g")
    xn_rot = Rot(2, em)
    wbc_rot = Rot(2)
    h_rot = Rot(2)
    relu_rot = Rot(4)
    hst_rot = Rot(4, em)
    w2sem = [em.dsem("g"), em.dsem("g")]
    pe0 = S["PE"].v
    w1info = []
    for e in range(E):
        wsl = w1_rot.acquire(em, "g")
        for xc in range(8):
            em.dma("g", s["w1b"][:, wsl * 8 * C + xc * C: wsl * 8 * C + (xc + 1) * C],
                   d["w1"][e, xc * 128:(xc + 1) * 128, :], w1_rot.sems[wsl])
        w1info.append((wsl, w1_rot.sems[wsl].v))
        w1_rot.release((S["PE"], pe0 + (e + 1) * nch * 9))
        if e < 4:
            for hc in range(8):
                em.dma("g", s["w2b"][:, (e * 8 + hc) * C:(e * 8 + hc + 1) * C],
                       d["w2"][e, hc * 128:(hc + 1) * 128, :], w2sem[0])

    def ld_xn(tc):
        xsl_ = xn_rot.acquire(em, "y")
        for xc in range(8):
            em.dma("y", s["xnld"][:, xsl_ * 4096 + xc * 512: xsl_ * 4096 + (xc + 1) * 512],
                   xn_dram[xc, :, tc * 512: tc * 512 + 512], xn_rot.sems[xsl_])
        return xsl_, xn_rot.sems[xsl_].v

    pend = ld_xn(0)
    for idx in range(E * nch):
        e, tc = divmod(idx, nch)
        xsl, la_x = pend
        if idx + 1 < E * nch:
            pend = ld_xn((idx + 1) % nch)
        wsl, lw_e = w1info[e]
        if tc == 0:
            em.w("t", w1_rot.sems[wsl], lw_e)
        if True:
            bsl = wbc_rot.acquire(em, "t")
            pw = pbig[:, PB0 + bsl * 512: PB0 + bsl * 512 + 512]
            em.op("t", lambda t, pw=pw, e=e, tc=tc: t.matmul(
                pw, s["cunit"][0:8, e * 128:(e + 1) * 128],
                s["wfm2"][0:8, tc * 512: tc * 512 + 512],
                start=True, stop=True), inc=S["PE"])
            pe_wbc = S["PE"].v
            em.w("t", xn_rot.sems[xsl], la_x)
            for ht in range(8):
                hsl = h_rot.acquire(em, "t")
                ph = pbig[:, PA0 + hsl * 512: PA0 + hsl * 512 + 512]
                for xc in range(8):
                    em.op("t", lambda t, ph=ph, wsl=wsl, xsl=xsl, xc=xc, ht=ht: t.matmul(
                        ph,
                        s["w1b"][:, wsl * 8 * C + xc * C + ht * 128:
                                 wsl * 8 * C + xc * C + ht * 128 + 128],
                        s["xnld"][:, xsl * 4096 + xc * 512: xsl * 4096 + (xc + 1) * 512],
                        start=(xc == 0), stop=(xc == 7)),
                        inc=(S["PE"] if xc == 7 else None))
                pe_h = S["PE"].v
                pe_last = pe_h
                rsl = relu_rot.acquire(em, "s")
                rb = s["hrelu"][:, rsl * 512:(rsl + 1) * 512]
                em.w("s", S["PE"], pe_h)
                em.op("s", lambda sc, rb=rb, ph=ph, e=e, ht=ht: sc.activation(
                    rb, ph, AF.Relu, bias=s["b1c"][:, e * 8 + ht: e * 8 + ht + 1]),
                    inc=S["AC"])
                ac_r = S["AC"].v
                h_rot.release((S["AC"], ac_r))
                osl = hst_rot.acquire(em, "v")
                ob = s["hst"][:, osl * 512:(osl + 1) * 512]
                em.w("v", S["AC"], ac_r + 1)
                em.w("v", S["PE"], pe_wbc)
                em.op("v", lambda v, ob=ob, rb=rb, pw=pw: v.tensor_tensor(
                    ob, rb, pw, ALU.mult), inc=S["DV"])
                dv_h = S["DV"].v
                relu_rot.release((S["DV"], dv_h))
                if ht == 7:
                    wbc_rot.release((S["DV"], dv_h))
                    xn_rot.release_slot(xsl, (S["PE"], pe_h))
                em.w("y", S["DV"], dv_h + 1)
                em.dma("y", d["d_h"][e, :, ht, tc * 512: tc * 512 + 512], ob,
                       hst_rot.sems[osl])
                hst_rot.release((hst_rot.sems[osl], hst_rot.sems[osl].v))
    assert S["PE"].v == pe0 + E * nch * 9
    em.drain("s", inc=S["AC"])
    em.drain("v", inc=S["DV"])
    dv_w1_done = S["DV"].v

    for pa in range(2):
        es = list(range(pa * 4, pa * 4 + 4))
        if pa == 1:
            em.w("g", S["PE"], S["PE"].v)
            for e in es:
                for hc in range(8):
                    em.dma("g", s["w2b"][:, ((e - 4) * 8 + hc) * C:
                                         ((e - 4) * 8 + hc + 1) * C],
                           d["w2"][e, hc * 128:(hc + 1) * 128, :], w2sem[1])
        em.w("t", w2sem[pa], w2sem[pa].v)
        hld_rot = Rot(2, em)
        xt_rot = Rot(2, em)
        yp_rot = Rot(2, em)
        y_rot = Rot(2)
        o_rot = Rot(2, em)
        hst_rot.finish(em, "y")

        def ld_j(j):
            lsl_ = hld_rot.acquire(em, "y")
            for ei, e in enumerate(es):
                em.dma("y",
                       s["hld"][:, lsl_ * 4096 + ei * 1024: lsl_ * 4096 + (ei + 1) * 1024]
                       .rearrange("p (a c) -> p a c", a=8),
                       d["d_h"][e, :, :, j * 128:(j + 1) * 128],
                       hld_rot.sems[lsl_])
            if pa == 1:
                xsl_ = xt_rot.acquire(em, "y")
                xres_ = s["xt"][:, xsl_ * C:(xsl_ + 1) * C]
                em.dma("y", xres_, res_fn(j), xt_rot.sems[xsl_])
                psl_ = yp_rot.acquire(em, "y")
                ypl_ = s["ypld"][:, psl_ * C:(psl_ + 1) * C]
                em.dma("y", ypl_, d["d_yp"][j, :, :], yp_rot.sems[psl_])
                return (lsl_, hld_rot.sems[lsl_].v, xsl_, xt_rot.sems[xsl_].v,
                        xres_, psl_, yp_rot.sems[psl_].v, ypl_)
            return (lsl_, hld_rot.sems[lsl_].v, 0, 0, None, 0, 0, None)

        pend = ld_j(0)
        for j in range(nkt):
            lsl, la_h, xsl, la_x, xres, psl, la_yp, ypl = pend
            if j + 1 < nkt:
                pend = ld_j(j + 1)
            ysl = y_rot.acquire(em, "t")
            py = pbig[:, PB0 + ysl * 1024: PB0 + ysl * 1024 + 1024]
            em.w("t", hld_rot.sems[lsl], la_h)
            em.w("t", S["DV"], dv_w1_done)
            for half in range(2):
                ph = py[:, half * 512: half * 512 + 512]
                k = 0
                for ei, e in enumerate(es):
                    for hc in range(8):
                        last = (ei == 3 and hc == 7)
                        em.op("t", lambda t, ph=ph, lsl=lsl, ei=ei, hc=hc,
                              half=half, k=k, last=last, pa=pa: t.matmul(
                            ph,
                            s["hld"][:, lsl * 4096 + ei * 1024 + hc * 128:
                                     lsl * 4096 + ei * 1024 + hc * 128 + 128],
                            s["w2b"][:, (ei * 8 + hc) * C + half * 512:
                                     (ei * 8 + hc) * C + half * 512 + 512],
                            start=(k == 0), stop=(last and pa == 0)),
                            inc=(S["PE"] if (last and pa == 0 and half == 1) else None))
                        k += 1
                if pa == 1:
                    em.op("t", lambda t, ph=ph, j=j, half=half: t.matmul(
                        ph, s["wfm2"][0:8, j * 128:(j + 1) * 128],
                        s["b2b"][0:8, half * 512: half * 512 + 512],
                        start=False, stop=True),
                        inc=(S["PE"] if half == 1 else None))
            pe_y = S["PE"].v
            hld_rot.release_slot(lsl, (S["PE"], pe_y))
            if pa == 0:
                osl = o_rot.acquire(em, "v")
                yt = s["yt"][:, osl * C:(osl + 1) * C]
                em.w("v", S["PE"], pe_y)
                em.op("v", lambda v, yt=yt, py=py: v.tensor_copy(yt, py), inc=S["DV"])
                dv_y = S["DV"].v
                y_rot.release((S["DV"], dv_y))
                em.w("y", S["DV"], dv_y + 1)
                em.dma("y", d["d_yp"][j, :, :], yt, o_rot.sems[osl])
                o_rot.release((o_rot.sems[osl], o_rot.sems[osl].v))
            else:
                osl = o_rot.acquire(em, "v")
                xo = s["x1t"][:, osl * C:(osl + 1) * C]
                em.w("v", S["PE"], pe_y)
                em.w("v", xt_rot.sems[xsl], la_x)
                em.w("v", yp_rot.sems[psl], la_yp)
                em.op("v", lambda v, py=py, ypl=ypl: v.tensor_add(ypl, ypl, py))
                em.drain("v")
                em.op("v", lambda v, xo=xo, ypl=ypl, xres=xres: v.scalar_tensor_tensor(
                    xo, ypl, 0.5, xres, ALU.mult, ALU.add), inc=S["DV"])
                dv_x = S["DV"].v
                y_rot.release((S["DV"], dv_x))
                xt_rot.release_slot(xsl, (S["DV"], dv_x))
                yp_rot.release_slot(psl, (S["DV"], dv_x))
                em.w("y", S["DV"], dv_x + 1)
                em.dma("y", dst_fn(j), xo, o_rot.sems[osl])
                o_rot.release((o_rot.sems[osl], o_rot.sems[osl].v))
        em.drain("v", inc=S["DV"])
        o_rot.finish(em, "y")
        hld_rot.finish(em, "y")
        if pa == 1:
            xt_rot.finish(em, "y")
            yp_rot.finish(em, "y")
    xn_rot.finish(em, "y")
    w1_rot.finish(em, "g")
    em.w("g", w2sem[0], w2sem[0].v)
    em.w("g", w2sem[1], w2sem[1].v)


def kv_phase(em, d, s, p, nkt):
    S = em.sem
    pbig = p["big"]
    nch = nkt * 128 // 512

    wsem = em.dsem("g")
    em.dma("g", s["rope"][:, :], d["rope"][:, :], wsem)
    for xc in range(8):
        em.dma("g", s["wdkvb"][:, xc * 256:(xc + 1) * 256],
               d["wdkv"][xc * 128:(xc + 1) * 128, :], wsem)
        em.dma("g", s["wkrb"][:, xc * DHR:(xc + 1) * DHR],
               d["wkr"][xc * 128:(xc + 1) * 128, :], wsem)
    for cc in range(2):
        em.dma("g", s["wukb"][:, cc * C:(cc + 1) * C],
               d["wuk"][cc * 128:(cc + 1) * 128, :], wsem)
        em.dma("g", s["wuvb"][:, cc * C:(cc + 1) * C],
               d["wuv"][cc * 128:(cc + 1) * 128, :], wsem)
    em.w("t", wsem, wsem.v)
    em.w("v", wsem, wsem.v)
    em.w("s", wsem, wsem.v)

    xn_rot = Rot(2, em)
    pa_rot = Rot(2)
    pm_rot = Rot(1)
    for tc in range(nch):
        xsl = xn_rot.acquire(em, "y")
        for xc in range(8):
            em.dma("y", s["xnld"][:, xsl * 4096 + xc * 512: xsl * 4096 + (xc + 1) * 512],
                   d["d_xn2"][xc, :, tc * 512: tc * 512 + 512], xn_rot.sems[xsl])
        la_x = xn_rot.sems[xsl].v
        em.w("t", xn_rot.sems[xsl], la_x)
        for cc in range(2):
            psl = pa_rot.acquire(em, "t")
            ps = pbig[:, (PA0, PA1)[psl]: (PA0, PA1)[psl] + 512]
            for xc in range(8):
                em.op("t", lambda t, ps=ps, xsl=xsl, xc=xc, cc=cc: t.matmul(
                    ps, s["wdkvb"][:, xc * 256 + cc * 128: xc * 256 + cc * 128 + 128],
                    s["xnld"][:, xsl * 4096 + xc * 512: xsl * 4096 + (xc + 1) * 512],
                    start=(xc == 0), stop=(xc == 7)), inc=(S["PE"] if xc == 7 else None))
            pe_c = S["PE"].v
            em.w("v", S["PE"], pe_c)
            em.op("v", lambda v, ps=ps, cc=cc, tc=tc: v.tensor_copy(
                s["ckv"][:, cc * T + tc * 512: cc * T + tc * 512 + 512], ps), inc=S["DV"])
            pa_rot.release((S["DV"], S["DV"].v))
        pm_rot.acquire(em, "t")
        pk = pbig[0:DHR, PM: PM + 512]
        for xc in range(8):
            em.op("t", lambda t, pk=pk, xsl=xsl, xc=xc: t.matmul(
                pk, s["wkrb"][:, xc * DHR:(xc + 1) * DHR],
                s["xnld"][:, xsl * 4096 + xc * 512: xsl * 4096 + (xc + 1) * 512],
                start=(xc == 0), stop=(xc == 7)), inc=(S["PE"] if xc == 7 else None))
        pe_k = S["PE"].v
        em.w("v", S["PE"], pe_k)
        em.op("v", lambda v, pk=pk, tc=tc: v.tensor_tensor(
            s["kr"][:, tc * 512: tc * 512 + 512], pk,
            s["rope"][0:DHR, tc * 512: tc * 512 + 512], ALU.mult), inc=S["DV"])
        dv_kr = S["DV"].v
        pm_rot.release((S["DV"], dv_kr))
        xn_rot.release((S["DV"], dv_kr))
    em.drain("v", inc=S["DV"])
    xn_rot.finish(em, "y")

    em.op("v", lambda v: v.memset(s["mk"][:, :], 0.0))
    em.drain("v", inc=S["DV"])

    for tc in range(nch):
        for cc in range(2):
            em.w("s", S["DV"], S["DV"].v)
            em.w("s", S["PE"], S["PE"].v)
            em.op("s", lambda sc, cc=cc, tc=tc: sc.activation(
                s["q2s"][:, 0:512], s["ckv"][:, cc * T + tc * 512: cc * T + tc * 512 + 512],
                AF.Square), inc=S["AC"])
            em.drain("s", inc=S["AC"])
            em.w("t", S["AC"], S["AC"].v)
            em.op("t", lambda t, cc=cc: t.matmul(
                pbig[0:1, PM: PM + 512], s["onescol"][0:128, 0:1], s["q2s"][:, 0:512],
                start=(cc == 0), stop=(cc == 1)), inc=S["PE"])
            if cc == 1:
                em.w("v", S["PE"], S["PE"].v)
                em.op("v", lambda v: v.reduce_max(s["nrm"][0:1, 0:1],
                                                  pbig[0:1, PM: PM + 512], axis=AX.X))
                em.drain("v")
                em.op("v", lambda v: v.tensor_max(
                    s["mk"][0:1, H:H + 1], s["mk"][0:1, H:H + 1], s["nrm"][0:1, 0:1]))
                em.drain("v", inc=S["DV"])
    for tc in range(nch):
        em.w("s", S["DV"], S["DV"].v)
        em.op("s", lambda sc, tc=tc: sc.activation(
            s["q2s"][0:DHR, 0:512], s["kr"][:, tc * 512: tc * 512 + 512], AF.Square),
            inc=S["AC"])
        em.drain("s", inc=S["AC"])
        em.w("t", S["AC"], S["AC"].v)
        em.op("t", lambda t: t.matmul(
            pbig[0:1, PM: PM + 512], s["onescol"][0:DHR, 0:1], s["q2s"][0:DHR, 0:512],
            start=True, stop=True), inc=S["PE"])
        em.w("v", S["PE"], S["PE"].v)
        em.op("v", lambda v: v.reduce_max(s["nrm"][0:1, 0:1],
                                          pbig[0:1, PM: PM + 512], axis=AX.X))
        em.drain("v")
        em.op("v", lambda v: v.tensor_max(
            s["mk"][0:1, H + 1:H + 2], s["mk"][0:1, H + 1:H + 2], s["nrm"][0:1, 0:1]))
        em.drain("v", inc=S["DV"])

    for colh in range(2):
        for cc in range(2):
            em.w("s", S["DV"], S["DV"].v)
            em.w("s", S["PE"], S["PE"].v)
            em.op("s", lambda sc, cc=cc, colh=colh: sc.activation(
                s["q2s"][:, 0:512],
                s["wukb"][:, cc * C + colh * 512: cc * C + colh * 512 + 512],
                AF.Square), inc=S["AC"])
            em.drain("s", inc=S["AC"])
            em.w("t", S["AC"], S["AC"].v)
            em.op("t", lambda t, cc=cc: t.matmul(
                pbig[0:1, PM: PM + 512], s["onescol"][0:128, 0:1], s["q2s"][:, 0:512],
                start=(cc == 0), stop=(cc == 1)), inc=S["PE"])
        em.w("v", S["PE"], S["PE"].v)
        em.op("v", lambda v, colh=colh: v.tensor_copy(
            s["fro"][0:1, colh * 512: colh * 512 + 512], pbig[0:1, PM: PM + 512]),
            inc=S["DV"])
    em.drain("v", inc=S["DV"])
    for h in range(H):
        em.op("v", lambda v, h=h: v.reduce_sum(
            s["mk"][0:1, h:h + 1], s["fro"][0:1, h * 64:(h + 1) * 64], axis=AX.X))
        em.drain("v")
    em.op("v", lambda v: v.tensor_scalar(
        s["mk"][0:1, 0:H], s["mk"][0:1, 0:H], s["mk"][0:1, H:H + 1],
        s["mk"][0:1, H + 1:H + 2], ALU.mult, ALU.add))
    em.drain("v", inc=S["DV"])
    em.w("s", S["DV"], S["DV"].v)
    em.op("s", lambda sc: sc.activation(s["mk"][0:1, 0:H], s["mk"][0:1, 0:H], AF.Sqrt))
    em.drain("s", inc=S["AC"])
    em.w("v", S["AC"], S["AC"].v)
    em.w("t", S["AC"], S["AC"].v)


def q1_phase(em, d, s, p, ntt, q0):
    S = em.sem
    pbig = p["big"]
    nch = ntt * 128 // 512
    wsem = em.dsem("g")
    xn_rot = Rot(2, em)
    st_rot = Rot(2, em)
    for wh in range(2):
        if wh == 1:
            em.w("g", S["PE"], S["PE"].v)
        for xc in range(8):
            em.dma("g", s["wdqh"][:, xc * 2048:(xc + 1) * 2048],
                   d["wdq"][xc * 128:(xc + 1) * 128, wh * 2048:(wh + 1) * 2048], wsem)
        em.w("t", wsem, wsem.v)
        pa_rot = Rot(2)

        def ld_xn(tc):
            xsl_ = xn_rot.acquire(em, "y")
            for xc in range(8):
                em.dma("y", s["xnld"][:, xsl_ * 4096 + xc * 512: xsl_ * 4096 + (xc + 1) * 512],
                       d["d_xn2"][xc, :, q0 * 128 + tc * 512: q0 * 128 + tc * 512 + 512],
                       xn_rot.sems[xsl_])
            return xsl_, xn_rot.sems[xsl_].v

        pend = ld_xn(0)
        for tc in range(nch):
            xsl, la_x = pend
            if tc + 1 < nch:
                pend = ld_xn(tc + 1)
            em.w("t", xn_rot.sems[xsl], la_x)
            dv_last = 0
            for cq in range(16):
                cqc = wh * 16 + cq
                psl = pa_rot.acquire(em, "t")
                ps = pbig[:, (PA0, PA1)[psl]: (PA0, PA1)[psl] + 512]
                for xc in range(8):
                    em.op("t", lambda t, ps=ps, xsl=xsl, xc=xc, cq=cq: t.matmul(
                        ps, s["wdqh"][:, xc * 2048 + cq * 128: xc * 2048 + cq * 128 + 128],
                        s["xnld"][:, xsl * 4096 + xc * 512: xsl * 4096 + (xc + 1) * 512],
                        start=(xc == 0), stop=(xc == 7)), inc=(S["PE"] if xc == 7 else None))
                pe_c = S["PE"].v
                ssl = st_rot.acquire(em, "v")
                cb = s["cqst"][:, ssl * 512:(ssl + 1) * 512]
                em.w("v", S["PE"], pe_c)
                em.op("v", lambda v, cb=cb, ps=ps: v.tensor_copy(cb, ps), inc=S["DV"])
                dv_c = S["DV"].v
                dv_last = dv_c
                pa_rot.release((S["DV"], dv_c))
                em.w("y", S["DV"], dv_c + 1)
                em.dma("y", d["d_cq"][cqc, :, tc * 512: tc * 512 + 512], cb,
                       st_rot.sems[ssl])
                st_rot.release((st_rot.sems[ssl], st_rot.sems[ssl].v))
            xn_rot.release_slot(xsl, (S["DV"], dv_last))
        em.drain("v", inc=S["DV"])
    st_rot.finish(em, "y")
    xn_rot.finish(em, "y")


def q2_quarter(em, d, s, p, ntt, q0, hq):
    S = em.sem
    pbig = p["big"]
    nch = ntt * 128 // 512
    nt = ntt * 128

    wsem = em.dsem("g")
    em.w("g", S["PE"], S["PE"].v)
    for cqc in range(32):
        em.dma("g", s["wuqh"][:, cqc * 256: cqc * 256 + 256],
               d["wuq"][cqc * 128:(cqc + 1) * 128, hq * 256:(hq + 1) * 256], wsem)
        em.dma("g", s["wqrh"][:, cqc * 128: cqc * 128 + 128],
               d["wqr"][cqc * 128:(cqc + 1) * 128, hq * 128:(hq + 1) * 128], wsem)
    em.w("t", wsem, wsem.v)

    for hl in range(4):
        em.op("v", lambda v, hl=hl: v.memset(
            s["qh"][64:128, hl * NTMAX: hl * NTMAX + nt], 0.0))
    em.drain("v", inc=S["DV"])
    em.w("t", S["DV"], S["DV"].v)

    cq_rot = Rot(4, em)
    ev_rot = Rot(2, em)
    dv_prev_evac = 0
    for tc in range(nch):
        slots = [pbig[:, off: off + 512] for off in (PA0, PA1, PB0)]
        em.w("t", S["DV"], dv_prev_evac)
        pe_acc = 0
        for cqc in range(32):
            csl = cq_rot.acquire(em, "y")
            cl = s["cqld"][:, csl * 512:(csl + 1) * 512]
            em.dma("y", cl, d["d_cq"][cqc, :, tc * 512: tc * 512 + 512],
                   cq_rot.sems[csl])
            la_c = cq_rot.sems[csl].v
            em.w("t", cq_rot.sems[csl], la_c)
            for dt in range(2):
                em.op("t", lambda t, ps=slots[dt], cqc=cqc, dt=dt, cl=cl: t.matmul(
                    ps, s["wuqh"][:, cqc * 256 + dt * 128: cqc * 256 + dt * 128 + 128],
                    cl, start=(cqc == 0), stop=(cqc == 31), skip_group_check=True))
            em.op("t", lambda t, ps=slots[2], cqc=cqc, cl=cl: t.matmul(
                ps, s["wqrh"][:, cqc * 128: cqc * 128 + 128],
                cl, start=(cqc == 0), stop=(cqc == 31), skip_group_check=True),
                inc=S["PE"])
            pe_acc = S["PE"].v
            cq_rot.release((S["PE"], pe_acc))
        em.w("v", S["PE"], pe_acc)
        for dt in range(2):
            esl = ev_rot.acquire(em, "v")
            eb = s["qev"][:, esl * 512:(esl + 1) * 512]
            em.op("v", lambda v, eb=eb, ps=slots[dt]: v.tensor_scalar(
                eb, ps, RS, None, ALU.mult))
            em.drain("v", inc=S["DV"])
            h0 = dt * 2
            em.op("v", lambda v, eb=eb, h0=h0, tc=tc: v.tensor_copy(
                s["qh"][0:64, h0 * NTMAX + tc * 512: h0 * NTMAX + tc * 512 + 512],
                eb[0:64, :]))
            em.drain("v", inc=S["DV"])
            dv_cp = S["DV"].v
            em.w("y", S["DV"], dv_cp)
            em.dma("y", s["qh"][0:64, (h0 + 1) * NTMAX + tc * 512:
                                (h0 + 1) * NTMAX + tc * 512 + 512],
                   eb[64:128, :], ev_rot.sems[esl])
            ev_rot.release((ev_rot.sems[esl], ev_rot.sems[esl].v))
        esl = ev_rot.acquire(em, "v")
        eb = s["qev"][:, esl * 512:(esl + 1) * 512]
        em.op("v", lambda v, eb=eb, ps=slots[2]: v.tensor_scalar(
            eb, ps, RS, None, ALU.mult))
        em.drain("v")
        em.op("v", lambda v, eb=eb, tc=tc: v.tensor_tensor(
            eb, eb, s["rope"][:, q0 * 128 + tc * 512: q0 * 128 + tc * 512 + 512],
            ALU.mult))
        em.drain("v", inc=S["DV"])
        dv_e = S["DV"].v
        dv_prev_evac = dv_e
        em.w("y", S["DV"], dv_e)
        for qq in range(4):
            em.dma("y", s["qh"][64:96, qq * NTMAX + tc * 512: qq * NTMAX + tc * 512 + 512],
                   eb[qq * 32:(qq + 1) * 32, :], ev_rot.sems[esl])
        ev_rot.release((ev_rot.sems[esl], ev_rot.sems[esl].v))
    ev_rot.finish(em, "v")
    cq_rot.finish(em, "y")

    dv_rope = S["DV"].v

    bsem = em.dsem()
    em.w("s", S["DV"], dv_rope)
    for hl in range(4):
        h = hq * 4 + hl
        for tc in range(nch):
            em.w("s", S["DV"], S["DV"].v)
            em.w("s", bsem, bsem.v)
            em.op("s", lambda sc, hl=hl, tc=tc: sc.activation(
                s["q2s"][0:96, 0:512],
                s["qh"][0:96, hl * NTMAX + tc * 512: hl * NTMAX + tc * 512 + 512],
                AF.Square), inc=S["AC"])
            em.drain("s", inc=S["AC"])
            em.w("t", S["AC"], S["AC"].v)
            em.op("t", lambda t: t.matmul(
                pbig[0:1, PM: PM + 512], s["onescol"][0:96, 0:1], s["q2s"][0:96, 0:512],
                start=True, stop=True), inc=S["PE"])
            em.w("s", S["PE"], S["PE"].v)
            em.op("s", lambda sc: sc.activation(
                s["qn"][0:1, 0:512], pbig[0:1, PM: PM + 512], AF.Sqrt))
            em.drain("s", inc=S["AC"])
            em.w("v", S["AC"], S["AC"].v)
            em.op("v", lambda v, h=h: v.tensor_scalar(
                s["browb"][0:1, 0:512], s["qn"][0:1, 0:512],
                s["mk"][0:1, h:h + 1], -1.0, ALU.mult, ALU.mult))
            em.drain("v", inc=S["DV"])
            dv_b = S["DV"].v
            em.w("y", S["DV"], dv_b)
            em.dma("y", s["qh"][96:97, hl * NTMAX + tc * 512: hl * NTMAX + tc * 512 + 512],
                   s["browb"][0:1, 0:512], bsem)
    em.w("v", bsem, bsem.v)
    em.w("t", bsem, bsem.v)


def vhat_quarter(em, d, s, p, nkt, hq):
    S = em.sem
    pbig = p["big"]
    em.op("v", lambda v: v.memset(s["vhat"][:, :], 1.0))
    em.drain("v", inc=S["DV"])
    em.w("t", S["DV"], S["DV"].v)
    pb_rot = Rot(2)
    for kt in range(nkt):
        ysl = pb_rot.acquire(em, "t")
        py = pbig[:, (PA0, PA1)[ysl]: (PA0, PA1)[ysl] + 256]
        for cc in range(2):
            em.op("t", lambda t, py=py, cc=cc, kt=kt, hq=hq: t.matmul(
                py, s["ckv"][:, cc * T + kt * 128: cc * T + (kt + 1) * 128],
                s["wuvb"][:, cc * C + hq * 256: cc * C + hq * 256 + 256],
                start=(cc == 0), stop=(cc == 1)), inc=(S["PE"] if cc == 1 else None))
        pe_v = S["PE"].v
        em.w("v", S["PE"], pe_v)
        for hl in range(4):
            em.op("v", lambda v, py=py, hl=hl, kt=kt: v.tensor_copy(
                s["vhat"][:, kt * 260 + hl * 65: kt * 260 + hl * 65 + 64],
                py[:, hl * 64:(hl + 1) * 64]))
        em.drain("v", inc=S["DV"])
        pb_rot.release((S["DV"], S["DV"].v))


def attn_quarter(em, d, s, p, nkt, ntt, q0, hq):
    S = em.sem
    pbig = p["big"]
    nk = nkt * 128

    secs = []
    t0 = 0
    while t0 < ntt:
        secs.append((t0, min(t0 + 8, ntt)))
        t0 += 8

    kh_rot = Rot(2, em)
    pa_rot = Rot(3)
    pt_rot = Rot(4)
    yev_rot = Rot(2, em)
    kc_rot = Rot(2)
    lsem = em.dsem()
    for hl in range(4):
        h = hq * 4 + hl
        ksl = kh_rot.acquire(em, "v")
        kh = s["khat"][:, ksl * T: ksl * T + T]
        em.op("v", lambda v, kh=kh: v.memset(kh[64:128, 0:nk], 0.0))
        em.drain("v", inc=S["DV"])
        dv_z = S["DV"].v
        em.w("y", S["DV"], dv_z)
        em.dma("y", kh[64:96, 0:nk], s["kr"][:, 0:nk], kh_rot.sems[ksl])
        em.dma("y", kh[96:97, 0:nk], d["onesr"][0:1, 0:nk], kh_rot.sems[ksl])
        la_kr = kh_rot.sems[ksl].v
        for tc in range(nk // 512):
            csl = kc_rot.acquire(em, "t")
            pk = pbig[0:64, PB1 + csl * 512: PB1 + csl * 512 + 512]
            for cc in range(2):
                em.op("t", lambda t, pk=pk, cc=cc, h=h, tc=tc: t.matmul(
                    pk, s["wukb"][:, cc * C + h * 64: cc * C + h * 64 + 64],
                    s["ckv"][:, cc * T + tc * 512: cc * T + tc * 512 + 512],
                    start=(cc == 0), stop=(cc == 1), skip_group_check=True),
                    inc=(S["PE"] if cc == 1 else None))
            pe_kc = S["PE"].v
            em.w("v", S["PE"], pe_kc)
            em.op("v", lambda v, kh=kh, pk=pk, tc=tc: v.tensor_copy(
                kh[0:64, tc * 512: tc * 512 + 512], pk), inc=S["DV"])
            kc_rot.release((S["DV"], S["DV"].v))
        em.drain("v", inc=S["DV"])
        dv_kh = S["DV"].v
        em.w("t", kh_rot.sems[ksl], la_kr)
        em.w("t", S["DV"], dv_kh)

        for (st_, en) in secs:
            seclen = (en - st_) * 128
            pav = pbig[0:65, PB0: PB0 + seclen]
            kmax = q0 + en
            first_kt = True
            for kt in range(kmax):
                qlo = max(st_, kt - q0)
                cols0 = qlo * 128
                ncols = (en - qlo) * 128
                diag = (kt - q0 == qlo and kt >= q0 + st_)
                chinfo = []
                for ch in range((ncols + 511) // 512):
                    cw = min(512, ncols - ch * 512)
                    co = cols0 + ch * 512
                    psl = pa_rot.acquire(em, "t")
                    ps = pbig[:, (PA0, PA1, PM)[psl]: (PA0, PA1, PM)[psl] + cw]
                    em.op("t", lambda t, ps=ps, kh=kh, kt=kt, hl=hl, co=co, cw=cw: t.matmul(
                        ps, kh[:, kt * 128:(kt + 1) * 128],
                        s["qh"][:, hl * NTMAX + co: hl * NTMAX + co + cw],
                        start=True, stop=True), inc=S["PE"])
                    pe_s = S["PE"].v
                    tsl = pt_rot.acquire(em, "s")
                    ptb = s["pt"][:, tsl * 512: tsl * 512 + cw]
                    em.w("s", S["PE"], pe_s)
                    em.op("s", lambda sc, ptb=ptb, ps=ps: sc.activation(
                        ptb, ps, AF.Exp), inc=S["AC"])
                    chinfo.append((psl, tsl, ptb, co, cw))
                em.drain("s", inc=S["AC"])
                ac_grp = S["AC"].v
                for psl, _, _, _, _ in chinfo:
                    pa_rot.release_slot(psl, (S["AC"], ac_grp))
                dv_m = None
                if diag:
                    em.w("v", S["AC"], ac_grp)
                    em.op("v", lambda v, ptb=chinfo[0][2]: v.tensor_mul(
                        ptb[:, 0:128], ptb[:, 0:128], s["dmask"][:, :]))
                    em.drain("v", inc=S["DV"])
                    dv_m = S["DV"].v
                em.w("t", S["AC"], ac_grp)
                if dv_m is not None:
                    em.w("t", S["DV"], dv_m)
                for psl, tsl, ptb, co, cw in chinfo:
                    off0 = co - st_ * 128
                    pieces = []
                    o_, rel_ = off0, 0
                    while rel_ < cw:
                        w_ = min(cw - rel_, 512 - (o_ % 512))
                        pieces.append((o_, rel_, w_))
                        o_ += w_
                        rel_ += w_
                    for pi, (po, pr, pw) in enumerate(pieces):
                        em.op("t", lambda t, pav=pav, ptb=ptb, kt=kt, hl=hl,
                              po=po, pr=pr, pw=pw, fk=first_kt: t.matmul(
                            pav[:, po: po + pw],
                            s["vhat"][:, kt * 260 + hl * 65: kt * 260 + (hl + 1) * 65],
                            ptb[:, pr: pr + pw],
                            start=fk, stop=False, skip_group_check=True),
                            inc=(S["PE"] if pi == len(pieces) - 1 else None))
                    pt_rot.release_slot(tsl, (S["PE"], S["PE"].v))
                first_kt = False
            em.drain("t", inc=S["PE"])
            pe_av = S["PE"].v
            em.w("v", S["PE"], pe_av)
            em.op("v", lambda v, pav=pav, seclen=seclen: v.tensor_copy(
                s["lr"][64:65, 0:seclen], pav[64:65, 0:seclen]))
            em.drain("v", inc=S["DV"])
            dv_l = S["DV"].v
            em.w("y", S["DV"], dv_l)
            em.w("y", lsem, lsem.v)
            em.dma("y", s["lr"][0:1, 0:seclen], s["lr"][64:65, 0:seclen], lsem)
            la_l = lsem.v
            em.w("v", lsem, la_l)
            em.op("v", lambda v, seclen=seclen: v.reciprocal(
                s["lr"][0:1, 0:seclen], s["lr"][0:1, 0:seclen]))
            em.drain("v", inc=S["DV"])
            for ch in range((seclen + 511) // 512):
                cw = min(512, seclen - ch * 512)
                esl = yev_rot.acquire(em, "v")
                yb = s["yev"][0:64, esl * 512: esl * 512 + cw]
                em.op("v", lambda v, yb=yb, pav=pav, ch=ch, cw=cw: v.tensor_copy(
                    yb, pav[0:64, ch * 512: ch * 512 + cw]))
                em.drain("v", inc=S["DV"])
                em.w("t", S["DV"], S["DV"].v)
                prl = pbig[0:64, PM: PM + cw]
                em.op("t", lambda t, prl=prl, ch=ch, cw=cw: t.matmul(
                    prl, s["ones1f"][0:1, 0:64], s["lr"][0:1, ch * 512: ch * 512 + cw],
                    start=True, stop=True), inc=S["PE"])
                em.w("v", S["PE"], S["PE"].v)
                em.op("v", lambda v, yb=yb, prl=prl: v.tensor_tensor(
                    yb, yb, prl, ALU.mult))
                em.drain("v", inc=S["DV"])
                dv_y = S["DV"].v
                em.w("y", S["DV"], dv_y)
                em.dma("y", d["d_yn"][h, :, q0 * 128 + st_ * 128 + ch * 512:
                                      q0 * 128 + st_ * 128 + ch * 512 + cw], yb,
                       yev_rot.sems[esl])
                yev_rot.release((yev_rot.sems[esl], yev_rot.sems[esl].v))
            em.w("t", S["DV"], S["DV"].v)
        kh_rot.release((S["PE"], S["PE"].v))
    yev_rot.finish(em, "y")
    kh_rot.finish(em, "y")
    em.w("y", lsem, lsem.v)


def wo_phase(em, d, s, p, ntt, q0):
    S = em.sem
    pbig = p["big"]
    wsem = em.dsem("g")
    for dc in range(8):
        em.dma("g", s["wob"][:, dc * C:(dc + 1) * C],
               d["wo"][dc * 128:(dc + 1) * 128, :], wsem)
    em.w("t", wsem, wsem.v)

    yw_rot = Rot(2, em)
    xt_rot = Rot(4, em)
    y_rot = Rot(2)
    o_rot = Rot(2, em)

    def ld_x(j):
        xsl_ = xt_rot.acquire(em, "y")
        xres_ = s["xt"][:, xsl_ * C:(xsl_ + 1) * C]
        em.dma("y", xres_, d["d_x1"][q0 + j, :, :], xt_rot.sems[xsl_])
        return xsl_, xt_rot.sems[xsl_].v, xres_

    def ld_yw(g):
        j0_ = g * 4
        j1_ = min(j0_ + 4, ntt)
        gw_ = (j1_ - j0_) * 128
        ysl_ = yw_rot.acquire(em, "y")
        for dc in range(8):
            em.dma("y", s["yw"][:, ysl_ * 4096 + dc * 512: ysl_ * 4096 + dc * 512 + gw_],
                   d["d_yn"][2 * dc: 2 * dc + 2, :,
                             q0 * 128 + j0_ * 128: q0 * 128 + j0_ * 128 + gw_]
                   .rearrange("a p c -> (a p) c"),
                   yw_rot.sems[ysl_])
        return ysl_, yw_rot.sems[ysl_].v

    ngr = (ntt + 3) // 4
    pend = ld_x(0)
    pendy = ld_yw(0)
    for g in range(ngr):
        j0 = g * 4
        j1 = min(j0 + 4, ntt)
        ysl, la_y = pendy
        if g + 1 < ngr:
            pendy = ld_yw(g + 1)
        pe_g = 0
        for j in range(j0, j1):
            xsl, la_x, xres = pend
            if j + 1 < ntt:
                pend = ld_x(j + 1)
            psl = y_rot.acquire(em, "t")
            py = pbig[:, PB0 + psl * 1024: PB0 + psl * 1024 + 1024]
            em.w("t", yw_rot.sems[ysl], la_y)
            for half in range(2):
                ph = py[:, half * 512: half * 512 + 512]
                for dc in range(8):
                    em.op("t", lambda t, ph=ph, ysl=ysl, dc=dc, j=j, j0=j0, half=half: t.matmul(
                        ph,
                        s["yw"][:, ysl * 4096 + dc * 512 + (j - j0) * 128:
                                ysl * 4096 + dc * 512 + (j - j0) * 128 + 128],
                        s["wob"][:, dc * C + half * 512: dc * C + half * 512 + 512],
                        start=(dc == 0), stop=(dc == 7)),
                        inc=(S["PE"] if (dc == 7 and half == 1) else None))
            pe_y = S["PE"].v
            pe_g = pe_y
            osl = o_rot.acquire(em, "v")
            xo = s["x1t"][:, osl * C:(osl + 1) * C]
            em.w("v", S["PE"], pe_y)
            em.w("v", xt_rot.sems[xsl], la_x)
            em.op("v", lambda v, xo=xo, py=py, xres=xres: v.tensor_add(
                xo, xres, py), inc=S["DV"])
            dv_x = S["DV"].v
            y_rot.release((S["DV"], dv_x))
            xt_rot.release_slot(xsl, (S["DV"], dv_x))
            em.w("y", S["DV"], dv_x + 1)
            em.dma("y", d["d_x2"][j, :, :], xo, o_rot.sems[osl])
            o_rot.release((o_rot.sems[osl], o_rot.sems[osl].v))
        yw_rot.release_slot(ysl, (S["PE"], pe_g))
    em.drain("v", inc=S["DV"])
    o_rot.finish(em, "y")
    yw_rot.finish(em, "y")
    xt_rot.finish(em, "y")


def build_nc():
    nc = bass.Bass("TRN2", target_bir_lowering=False, debug=False)
    d = {}

    def di(name, shape, dt, kind=None):
        if kind:
            d[name] = nc.dram_tensor(name, shape, dt, kind=kind)
        else:
            d[name] = nc.dram_tensor(name, shape, dt)

    di("x4", [T, C], F32, "ExternalInput")
    di("wdq", [C, 4096], F32, "ExternalInput")
    di("wuq", [4096, C], F32, "ExternalInput")
    di("wdkv", [C, 256], F32, "ExternalInput")
    di("wuk", [256, C], F32, "ExternalInput")
    di("wuv", [256, C], F32, "ExternalInput")
    di("wqr", [4096, 512], F32, "ExternalInput")
    di("wkr", [C, DHR], F32, "ExternalInput")
    di("wo", [C, C], F32, "ExternalInput")
    di("wg", [C, E], F32, "ExternalInput")
    di("bg", [1, E], F32, "ExternalInput")
    di("w1", [E, C, C], F32, "ExternalInput")
    di("b1c", [128, 64], F32, "ExternalInput")
    di("w2", [E, C, C], F32, "ExternalInput")
    di("b2", [E, C], BF16, "ExternalInput")
    di("cidf", [128, 128], F32, "ExternalInput")
    di("cidb", [128, 128], BF16, "ExternalInput")
    di("rope", [128, T], F32, "ExternalInput")
    di("dmask", [128, 128], BF16, "ExternalInput")
    di("cunit", [8, 8 * 128], BF16, "ExternalInput")
    di("onesr", [1, T], BF16, "ExternalInput")
    di("out", [NTMAX, C], F32, "ExternalOutput")
    if DBG_PHASE is not None:
        di("odbg", [128, 8192], F32, "ExternalOutput")

    di("d_h", [E, 128, 8, T], BF16)
    di("d_x1", [32, 128, C], F32)
    di("d_x2", [20, 128, C], F32)
    di("d_xnA", [8, 128, T], BF16)
    di("d_xn2", [8, 128, T], BF16)
    di("d_cq", [32, 128, NTMAX], BF16)
    di("d_yn", [H, HS, T], BF16)
    di("d_yp", [32, 128, C], F32)

    ctx = ExitStack()
    semsA, semsB = {}, {}
    for name in ("PE", "DV", "AC", "GP", "SY"):
        hdl = ctx.enter_context(nc.semaphore(name))
        semsA[name] = Sem(hdl)
        semsB[name] = Sem(hdl)
    dpool = [ctx.enter_context(nc.semaphore(f"D{i}")) for i in range(N_DSEM)]
    gpool = [ctx.enter_context(nc.semaphore(f"G{i}")) for i in range(N_GSEM)]
    emA, emB = Em(semsA, dpool, gpool), Em(semsB, dpool, gpool)
    EMS = ((emA, ROLE[0]), (emB, ROLE[1]))

    s = {}
    p = {}
    p["big"] = ctx.enter_context(nc.psum_tensor("pbig", [128, 3584], F32))
    p["bf"] = ctx.enter_context(nc.psum_tensor("pbf", [128, 1024], BF16))

    _ser = [0]

    def sb(scope, name, shape, dt):
        _ser[0] += 1
        s[name] = scope.enter_context(
            nc.sbuf_tensor(f"sb{_ser[0]}_{name}", shape, dt))

    sb(ctx, "ones1f", [1, 128], F32)
    sb(ctx, "ones512f", [1, 512], F32)
    sb(ctx, "onescol", [128, 1], BF16)
    sb(ctx, "epsc", [128, 1], F32)
    sb(ctx, "cidf", [128, 128], F32)
    sb(ctx, "cidb", [128, 128], BF16)
    sb(ctx, "dmask", [128, 128], BF16)
    sb(ctx, "bg", [1, E], F32)
    sb(ctx, "b1c", [128, 64], F32)
    sb(ctx, "b2b", [E, C], BF16)
    sb(ctx, "cunit", [8, 8 * 128], BF16)
    sb(ctx, "wgc", [128, 8 * E], BF16)
    sb(ctx, "wfm2", [8, T], BF16)
    sb(ctx, "xt", [128, 4 * C], F32)
    sb(ctx, "x1t", [128, 2 * C], F32)
    sb(ctx, "sqs", [128, 2 * C], BF16)
    sb(ctx, "q2s", [128, 512], BF16)
    sb(ctx, "nrm", [1, 512], F32)
    sb(ctx, "mk", [1, 2 * H], F32)
    sb(ctx, "qn", [1, 512], F32)
    sb(ctx, "brow", [1, 512], F32)
    sb(ctx, "browb", [1, 512], BF16)
    sb(ctx, "gsc", [128, E], F32)
    sb(ctx, "gmx", [128, E], F32)
    sb(ctx, "gge", [128, E], F32)
    sb(ctx, "gwv", [128, E], F32)
    sb(ctx, "stat", [128, 6 * 32], F32)

    for em, r in EMS:
        preload(em, d, s)
    if DBG_PHASE == "pre":
        for em, r in EMS:
            em.stop_with_dump(d, [(s["cidf"][:, :], 128, 128),
                                  (s["wgc"][:, 0:64], 128, 64),
                                  (s["b1c"][:, :], 128, 64)])
        return _finish(nc, ctx, EMS)

    def run_ln(ntiles_key, src_key, dst):
        with ExitStack() as sp:
            sb(sp, "xnt", [128, 2 * C], BF16)
            sb(sp, "fmst", [128, 2 * 8 * 512], BF16)
            sl_ = dict(s)
            for em, r in EMS:
                if src_key == "x4":
                    src = lambda i: d["x4"][i * 128:(i + 1) * 128, :]
                elif src_key == "x1":
                    src = lambda i: d["d_x1"][i, :, :]
                else:
                    src = lambda i: d["d_x2"][i, :, :]
                ln_phase(em, d, sl_, p, r[ntiles_key], src, dst)
                em.barrier()

    def run_gate(ntiles_key):
        with ExitStack() as sp:
            sb(sp, "gld", [128, 2 * 1024], BF16)
            sl_ = dict(s)
            for em, r in EMS:
                gate_phase(em, d, sl_, p, r[ntiles_key], d["d_xnA"])
                em.barrier()

    def run_ffn(nkt_key, res_key, dst_key):
        with ExitStack() as sp:
            sb(sp, "w1b", [128, 2 * 8 * C], BF16)
            sb(sp, "w2b", [128, 32 * C], BF16)
            sb(sp, "hrelu", [128, 4 * 512], BF16)
            sb(sp, "hst", [128, 4 * 512], BF16)
            sb(sp, "hld", [128, 2 * 4096], BF16)
            sb(sp, "yt", [128, 2 * C], F32)
            sb(sp, "ypld", [128, 2 * C], F32)
            sb(sp, "xnld", [128, 2 * 8 * 512], BF16)
            sl_ = dict(s)
            for em, r in EMS:
                if res_key == "x4":
                    res = lambda j: d["x4"][j * 128:(j + 1) * 128, :]
                else:
                    res = lambda j: d["d_x2"][j, :, :]
                if dst_key == "x1":
                    dst = lambda j: d["d_x1"][j, :, :]
                else:
                    dst = lambda j: d["out"][j * 128:(j + 1) * 128, :]
                ffn_phase(em, d, sl_, p, r[nkt_key], d["d_xnA"], res, dst)
                em.barrier()

    def dbg_stop(taps_fn=None):
        for em, r in EMS:
            em.stop_with_dump(d, taps_fn() if taps_fn else [])

    run_ln("NKT", "x4", d["d_xnA"])
    if DBG_PHASE == "ln1":
        dbg_stop(lambda: [(d["d_xnA"][0, :, 0:2048], 128, 2048),
                          (d["d_xnA"][1, :, 0:2048], 128, 2048)])
        return _finish(nc, ctx, EMS)
    run_gate("NKT")
    if DBG_PHASE == "gate1":
        dbg_stop(lambda: [(s["wfm2"][0:8, 0:2560], 8, 2560)])
        return _finish(nc, ctx, EMS)
    run_ffn("NKT", "x4", "x1")
    if DBG_PHASE == "ffn1":
        dbg_stop(lambda: [(d["d_x1"][0, :, :], 128, 1024),
                          (d["d_x1"][11, :, :], 128, 1024)])
        return _finish(nc, ctx, EMS)
    run_ln("NKT", "x1", d["d_xn2"])
    if DBG_PHASE == "ln2":
        dbg_stop(lambda: [(d["d_xn2"][2, :, 0:2048], 128, 2048),
                          (d["d_xn2"][2, :, 2048:4096], 128, 2048),
                          (d["d_xn2"][5, :, 0:2048], 128, 2048),
                          (d["d_xn2"][5, :, 2048:4096], 128, 2048)])
        return _finish(nc, ctx, EMS)

    with ExitStack() as sp:
        sb(sp, "rope", [128, T], F32)
        sb(sp, "ckv", [128, 2 * T], BF16)
        sb(sp, "kr", [DHR, T], BF16)
        sb(sp, "wdkvb", [128, 8 * 256], BF16)
        sb(sp, "wukb", [128, 2 * C], BF16)
        sb(sp, "wuvb", [128, 2 * C], BF16)
        sb(sp, "wkrb", [128, 8 * DHR], BF16)
        with ExitStack() as spk:
            sb(spk, "xnld", [128, 2 * 8 * 512], BF16)
            sb(spk, "fro", [1, C], F32)
            sk_ = dict(s)
            for em, r in EMS:
                kv_phase(em, d, sk_, p, r["NKT"])
                em.barrier()
        if DBG_PHASE == "kv":
            dbg_stop(lambda: [(s["ckv"][:, 0:2048], 128, 2048),
                              (s["kr"][0:DHR, 0:1024], DHR, 1024),
                              (s["mk"][0:1, 0:2 * H], 1, 2 * H),
                              (d["d_xn2"][0, :, 0:2048], 128, 2048),
                              (d["d_xn2"][3, :, 0:2048], 128, 2048)])
            sp.close()
            return _finish(nc, ctx, EMS)

        with ExitStack() as spq:
            sb(spq, "wdqh", [128, 8 * 2048], BF16)
            sb(spq, "xnld", [128, 2 * 8 * 512], BF16)
            sb(spq, "cqst", [128, 2 * 512], BF16)
            sq_ = dict(s)
            for em, r in EMS:
                q1_phase(em, d, sq_, p, r["NTT"], r["Q0"])
                em.barrier()
        if DBG_PHASE == "q1":
            dbg_stop(lambda: [(d["d_cq"][0, :, 0:2048], 128, 2048),
                              (d["d_cq"][31, :, 0:2048], 128, 2048)])
            sp.close()
            return _finish(nc, ctx, EMS)

        with ExitStack() as spq:
            sb(spq, "wuqh", [128, 32 * 256], BF16)
            sb(spq, "wqrh", [128, 32 * 128], BF16)
            sb(spq, "cqld", [128, 4 * 512], BF16)
            sb(spq, "qev", [128, 2 * 512], BF16)
            sb(spq, "qh", [128, 4 * NTMAX], BF16)
            sb(spq, "vhat", [128, 32 * 4 * 65], BF16)
            sb(spq, "khat", [128, 2 * T], BF16)
            sb(spq, "pt", [128, 4 * 512], BF16)
            sb(spq, "lr", [128, 1024], F32)
            sb(spq, "yev", [64, 2 * 512], BF16)
            sa_ = dict(s)
            for em, r in EMS:
                nq = 1 if DBG_PHASE in ("q2", "vhat", "attn") else 4
                for hq in range(nq):
                    q2_quarter(em, d, sa_, p, r["NTT"], r["Q0"], hq)
                    em.barrier()
                    if DBG_PHASE == "q2":
                        em.stop_with_dump(d, [(s["qh"][:, 0:2048], 128, 2048),
                                              (s["qh"][:, NTMAX:NTMAX + 2048], 128, 2048)])
                        break
                    vhat_quarter(em, d, sa_, p, r["NKT"], hq)
                    em.barrier()
                    if DBG_PHASE == "vhat":
                        em.stop_with_dump(d, [(s["vhat"][:, 0:2080], 128, 2080)])
                        break
                    attn_quarter(em, d, sa_, p, r["NKT"], r["NTT"], r["Q0"], hq)
                    em.barrier()
                    if DBG_PHASE == "attn":
                        nt_, o_ = r["NTT"] * 128, r["Q0"] * 128
                        em.stop_with_dump(
                            d, [(d["d_yn"][0, :, o_:o_ + nt_], HS, nt_),
                                (d["d_yn"][3, :, o_:o_ + nt_], HS, nt_)])
                        break
        if DBG_PHASE in ("q2", "vhat", "attn"):
            sp.close()
            return _finish(nc, ctx, EMS)

    with ExitStack() as sp:
        sb(sp, "wob", [128, 8 * C], BF16)
        sb(sp, "yw", [128, 2 * 4096], BF16)
        sw_ = dict(s)
        for em, r in EMS:
            wo_phase(em, d, sw_, p, r["NTT"], r["Q0"])
            em.barrier()
    if DBG_PHASE == "wo":
        dbg_stop(lambda: [(d["d_x2"][0, :, :], 128, 1024),
                          (d["d_x2"][11, :, :], 128, 1024)])
        return _finish(nc, ctx, EMS)
    run_ln("NTT", "x2", d["d_xnA"])
    run_gate("NTT")
    run_ffn("NTT", "x2", "out")
    for em, r in EMS:
        for sm in em.dsems:
            em.w("y", sm, sm.v)

    return _finish(nc, ctx, EMS)


def _finish(nc, ctx, EMS):
    emA, emB = EMS[0][0], EMS[1][0]
    blk = ctx.enter_context(nc.Block())

    def body(eng, key):
        pid = eng.partition_id()
        with eng.register(f"role_{key}") as r:
            eng.reg_mov(r, pid)
            eng.reg_mod(r, r, 2)
            with eng.If_eq(r, 0):
                for fn in emA.q[key]:
                    fn(eng)
            with eng.Else():
                for fn in emB.q[key]:
                    fn(eng)

    @blk.tensor
    def _(t):
        body(t, "t")

    @blk.vector
    def _(v):
        body(v, "v")

    @blk.scalar
    def _(sc):
        body(sc, "s")

    @blk.gpsimd
    def _(g):
        body(g, "g")

    @blk.sync
    def _(y):
        body(y, "y")

    ctx.close()
    return nc


def make_consts():
    t = np.arange(T, dtype=np.float32)
    inv = 1.0 / (10000.0 ** (np.arange(0, DHR, 2, dtype=np.float32) / DHR))
    freqs = t[:, None] * inv[None, :]
    emb = np.concatenate([freqs, freqs], axis=-1)
    rope = np.ascontiguousarray(np.tile(
        (np.cos(emb) + np.sin(emb)).astype(np.float32).T, (4, 1)))
    ki, qi = np.meshgrid(np.arange(128), np.arange(128), indexing="ij")
    dmask = (qi >= ki).astype(ml_dtypes.bfloat16)
    identf = np.eye(128, dtype=np.float32)
    identb = np.eye(128).astype(ml_dtypes.bfloat16)
    cunit = np.zeros((8, 8 * 128), dtype=ml_dtypes.bfloat16)
    for e in range(8):
        cunit[e, e * 128:(e + 1) * 128] = 1.0
    onesr = np.ones((1, T), dtype=ml_dtypes.bfloat16)
    return rope, dmask, identf, identb, cunit, onesr


_NC_CACHE = {}


def kernel(L0_attn_x, L0_attn_W_DQ, L0_attn_W_UQ, L0_attn_W_DKV, L0_attn_W_UK,
           L0_attn_W_UV, L0_attn_W_QR, L0_attn_W_KR, L0_attn_W_O,
           L0_ffn_W_gate, L0_ffn_b_gate, L0_ffn_W1, L0_ffn_b1, L0_ffn_W2,
           L0_ffn_b2):
    key = ("nc", DBG_PHASE)
    if key not in _NC_CACHE:
        _NC_CACHE[key] = build_nc()
    nc = _NC_CACHE[key]
    rope, dmask, identf, identb, cunit, onesr = make_consts()

    x = np.ascontiguousarray(np.asarray(L0_attn_x, dtype=np.float32))
    b1 = np.asarray(L0_ffn_b1, np.float32).reshape(E * 8, 128).T.copy()
    common = dict(
        wdq=np.asarray(L0_attn_W_DQ, np.float32),
        wuq=np.asarray(L0_attn_W_UQ, np.float32),
        wdkv=np.asarray(L0_attn_W_DKV, np.float32),
        wuk=np.asarray(L0_attn_W_UK, np.float32),
        wuv=np.asarray(L0_attn_W_UV, np.float32),
        wqr=np.asarray(L0_attn_W_QR, np.float32),
        wkr=np.asarray(L0_attn_W_KR, np.float32),
        wo=np.asarray(L0_attn_W_O, np.float32),
        wg=np.asarray(L0_ffn_W_gate, np.float32),
        bg=np.asarray(L0_ffn_b_gate, np.float32).reshape(1, E),
        w1=np.asarray(L0_ffn_W1, np.float32),
        b1c=b1,
        w2=np.asarray(L0_ffn_W2, np.float32),
        b2=np.asarray(L0_ffn_b2, np.float32).astype(ml_dtypes.bfloat16),
        cidf=identf, cidb=identb, rope=rope, dmask=dmask, cunit=cunit,
        onesr=onesr,
    )
    in_maps = []
    for c in range(8):
        m = dict(common)
        m["x4"] = x[c // 2]
        in_maps.append(m)

    res = run_bass_kernel_spmd(nc, in_maps, core_ids=list(range(8)))
    out = np.empty((NB, T, C), dtype=np.float32)
    for b in range(NB):
        out[b, 0:SPLIT, :] = res.results[2 * b]["out"]
        out[b, SPLIT:T, :] = res.results[2 * b + 1]["out"][0: T - SPLIT]
    if DBG_PHASE is not None:
        kernel.dbg = [np.asarray(r["odbg"]) for r in res.results if "odbg" in r]
    return out
